# version 01
# baseline (speedup 1.0000x reference)
from contextlib import ExitStack

import numpy as np
import ml_dtypes
import concourse.bass as bass
import concourse.mybir as mybir
from concourse.bass_utils import run_bass_kernel_spmd
from concourse.aot_env import allow_ici_all_to_all

F32 = mybir.dt.float32
BF16 = mybir.dt.bfloat16
AF = mybir.ActivationFunctionType
ALU = mybir.AluOpType
AX = mybir.AxisListType

D = 4096
NC = 32
NH = 4
NRB = 8
NT = 8
EXP_SCALE = 1.0 / np.sqrt(128.0)
EXP_SHIFT = -12.0
EPS = 1e-5

ENGINES = ("tensor", "vector", "scalar", "sync", "gpsimd")


class Em:

    def __init__(self, nc):
        self.nc = nc
        self.ops = {e: [] for e in ENGINES}
        self.sems = {}

    def add_sem(self, name, handle):
        self.sems[name] = [handle, 0]

    def val(self, name):
        return self.sems[name][1]

    def op(self, eng, build, wait=(), inc=None, inc_k=1):
        for (w, v) in wait:
            assert v <= self.sems[w][1], (w, v, self.sems[w][1])
        targets = [(self.sems[w][0], v) for (w, v) in wait if v > 0]
        handle = None
        if inc is not None:
            self.sems[inc][1] += inc_k
            handle = self.sems[inc][0]

        def f(e, targets=targets, build=build, handle=handle, inc_k=inc_k):
            for (h, v) in targets:
                e.wait_ge(h, v)
            ins = build(e)
            if handle is not None:
                ins.then_inc(handle, inc_k)

        self.ops[eng].append(f)
        return self.sems[inc][1] if inc is not None else None

    def wait(self, eng, waits):
        for (w, v) in waits:
            assert v <= self.sems[w][1], (w, v, self.sems[w][1])
        targets = [(self.sems[w][0], v) for (w, v) in waits if v > 0]

        def f(e, targets=targets):
            for (h, v) in targets:
                e.wait_ge(h, v)

        self.ops[eng].append(f)

    def flush(self, block):
        for e in ENGINES:
            if self.ops[e]:
                fns = self.ops[e]
                getattr(block, e)(lambda engine, fns=fns: [f(engine) for f in fns])
        self.ops = {e: [] for e in ENGINES}

    def drain_and_barrier(self, block):
        self.wait("sync", [("s_sync", self.val("s_sync"))])
        self.wait("gpsimd", [("s_gp", self.val("s_gp"))])
        self.wait("scalar", [("s_actq", self.val("s_actq")), ("act", self.val("act"))])
        self.wait("vector", [("dve", self.val("dve"))])
        self.wait("tensor", [("pe", self.val("pe"))])
        bar = self.sems["bar"]
        bar[1] += 5
        target = bar[1]
        for e in ENGINES:
            def f(eng, h=bar[0], t=target):
                eng.nop().then_inc(h, 1)
                eng.wait_ge(h, t)
            self.ops[e].append(f)
        self.flush(block)


def build_core(debug_taps=()):
    nc = bass.Bass("TRN2", target_bir_lowering=False)

    x_in = nc.dram_tensor("x_own", [NT * 128, D], F32, kind="ExternalInput").ap()
    wq4 = nc.dram_tensor("wq4", [NH, 128, NC, 128], F32, kind="ExternalInput").ap()
    wk4 = nc.dram_tensor("wk4", [NH, 128, NC, 128], F32, kind="ExternalInput").ap()
    wv4 = nc.dram_tensor("wv4", [NH, 128, NC, 128], F32, kind="ExternalInput").ap()
    wo4 = nc.dram_tensor("wo4", [NC, 128, NC, 128], F32, kind="ExternalInput").ap()
    bqkv = nc.dram_tensor("bqkv", [128, 3 * NH], F32, kind="ExternalInput").ap()
    bo4 = nc.dram_tensor("bo4", [128, NC], F32, kind="ExternalInput").ap()
    a4 = nc.dram_tensor("a4", [128, NC * 64], F32, kind="ExternalInput").ap()
    bt_in = nc.dram_tensor("bt", [64, D], F32, kind="ExternalInput").ap()
    wr4 = nc.dram_tensor("wr4", [128, NC * 4], F32, kind="ExternalInput").ap()
    y_out = nc.dram_tensor("y", [NT * 128, D], F32, kind="ExternalOutput").ap()

    xn_pub = nc.dram_tensor("xn_pub", [D, 1024], BF16).ap()
    xn_gath = nc.dram_tensor("xn_gath", [8 * D, 1024], BF16).ap()
    q_scr = nc.dram_tensor("q_scr", [NH * 128, 8192], BF16).ap()
    k_scr = nc.dram_tensor("k_scr", [NH * 128, 8192], BF16).ap()
    v_scr = nc.dram_tensor("v_scr", [NH * 128, 8192], BF16).ap()
    att_pub = nc.dram_tensor("att_pub", [D, 1024], BF16).ap()
    att_own = nc.dram_tensor("att_own", [D, 1024], BF16).ap()
    x1_dram = nc.dram_tensor("x1_scr", [NT * 128, D], F32).ap()

    ident_bf = nc.inline_tensor(np.eye(128, dtype=ml_dtypes.bfloat16), "c_ident_bf")
    ident_f32 = nc.inline_tensor(np.eye(128, dtype=np.float32), "c_ident_f32")
    qi = np.arange(128)[:, None]
    ti = np.arange(128)[None, :]
    tri01 = nc.inline_tensor((ti <= qi).astype(ml_dtypes.bfloat16), "c_tri01")
    e4c = nc.inline_tensor(
        np.repeat(np.eye(4, dtype=ml_dtypes.bfloat16), 16, axis=1), "c_e4")
    shiftc = nc.inline_tensor(
        np.full((128, 1), EXP_SHIFT, dtype=np.float32), "c_shift")

    taps = {}
    for t in debug_taps:
        src = {"xn_gath": xn_gath, "q_scr": q_scr, "k_scr": k_scr,
               "v_scr": v_scr, "att_pub": att_pub, "att_own": att_own,
               "x1": x1_dram}[t]
        shp = list(src.tensor.shape)
        taps[t] = (nc.dram_tensor("tap_" + t, shp, src.tensor.dtype,
                                  kind="ExternalOutput").ap(), src)

    with ExitStack() as _st1:
        ident_sb = _st1.enter_context(nc.sbuf_tensor("ident_sb", [128, 128], BF16, allow_name_mangling=True))
        ident32_sb = _st1.enter_context(nc.sbuf_tensor("ident32_sb", [128, 128], F32, allow_name_mangling=True))
        tri_sb = _st1.enter_context(nc.sbuf_tensor("tri_sb", [128, 128], BF16, allow_name_mangling=True))
        e4_sb = _st1.enter_context(nc.sbuf_tensor("e4_sb", [4, 64], BF16, allow_name_mangling=True))
        bias_sb = _st1.enter_context(nc.sbuf_tensor("bias_sb", [128, 3 * NH + NC], F32, allow_name_mangling=True))
        wr_sb = _st1.enter_context(nc.sbuf_tensor("wr_sb", [128, NC * 4], BF16, allow_name_mangling=True))
        a_sb = _st1.enter_context(nc.sbuf_tensor("a_sb", [128, NC * 64], BF16, allow_name_mangling=True))
        bmat_sb = _st1.enter_context(nc.sbuf_tensor("bmat_sb", [64, D], BF16, allow_name_mangling=True))
        lnst = _st1.enter_context(nc.sbuf_tensor("lnst", [128, 16], F32, allow_name_mangling=True))
        shift_sb = _st1.enter_context(nc.sbuf_tensor("shift_sb", [128, 1], F32, allow_name_mangling=True))
        h_pe = _st1.enter_context(nc.semaphore("sem_pe"))
        h_dve = _st1.enter_context(nc.semaphore("sem_dve"))
        h_act = _st1.enter_context(nc.semaphore("sem_act"))
        h_sync = _st1.enter_context(nc.semaphore("s_sync"))
        h_gp = _st1.enter_context(nc.semaphore("s_gp"))
        h_actq = _st1.enter_context(nc.semaphore("s_actq"))
        h_cc = _st1.enter_context(nc.semaphore("s_cc"))
        h_bar = _st1.enter_context(nc.semaphore("bar"))
        em = Em(nc)
        for nm, h in (("pe", h_pe), ("dve", h_dve), ("act", h_act),
                      ("s_sync", h_sync), ("s_gp", h_gp), ("s_actq", h_actq),
                      ("s_cc", h_cc), ("bar", h_bar)):
            em.add_sem(nm, h)

        def emit_ln_tile(src_tile, xn_stage, sb, extra_sq_wait=None):
            rs = lnst[:, sb + 0:sb + 1]
            qs = lnst[:, sb + 1:sb + 2]
            nmu = lnst[:, sb + 2:sb + 3]
            vv = lnst[:, sb + 3:sb + 4]
            rstd = lnst[:, sb + 4:sb + 5]
            nmr = lnst[:, sb + 5:sb + 6]
            v_rs = em.op("vector", lambda e: e.reduce_sum(rs, src_tile[:, :], axis=AX.X),
                         inc="dve")
            sq_w = [] if extra_sq_wait is None else [extra_sq_wait]
            v_qs = em.op("scalar", lambda e: e.activation(
                xn_stage[:, :], src_tile[:, :], AF.Square, accum_out=qs),
                wait=sq_w, inc="act")
            v_nmu = em.op("vector", lambda e: e.tensor_scalar(
                nmu, rs, -1.0 / D, None, ALU.mult), wait=[("dve", v_rs)], inc="dve")
            v_vv = em.op("vector", lambda e: e.scalar_tensor_tensor(
                vv, rs, nmu, qs, ALU.mult, ALU.add),
                wait=[("dve", v_nmu), ("act", v_qs)], inc="dve")
            v_vs = em.op("vector", lambda e: e.tensor_scalar(
                vv, vv, 1.0 / D, EPS, ALU.mult, ALU.add),
                wait=[("dve", v_vv)], inc="dve")
            v_ln = em.op("scalar", lambda e: e.activation(vv, vv, AF.Ln),
                wait=[("dve", v_vs)], inc="act")
            v_rst = em.op("scalar", lambda e: e.activation(
                rstd, vv, AF.Exp, scale=-0.5), wait=[("act", v_ln)], inc="act")
            v_nmr = em.op("vector", lambda e: e.tensor_mul(nmr, nmu, rstd),
                          wait=[("act", v_rst), ("dve", v_vv)], inc="dve")
            v_xn = em.op("vector", lambda e: e.tensor_scalar(
                xn_stage[:, :], src_tile[:, :], rstd, nmr, ALU.mult, ALU.add),
                wait=[("dve", v_nmr)], inc="dve")
            return v_xn

        def emit_tile_to_fm(src_bf, dst_fm, tau, pbf, pbf_free, src_wait):
            dst3 = dst_fm[:, :].rearrange("p (c r) -> p c r", c=NC)
            v_ev = None
            for f in range(4):
                bank = pbf[f % 2]
                waits = [src_wait]
                if pbf_free[f % 2] is not None:
                    waits.append(("dve", pbf_free[f % 2]))
                v_tp = None
                for k in range(8):
                    c = 8 * f + k
                    def bld(e, bank=bank, k=k, c=c):
                        return e.transpose(bank[:, 128 * k:128 * k + 128],
                                           src_bf[:, 128 * c:128 * c + 128],
                                           ident_sb[:, :])
                    v_tp = em.op("tensor", bld, wait=(waits if k == 0 else ()),
                                 inc=("pe" if k == 7 else None))
                def ev(e, bank=bank, f=f, tau=tau):
                    return e.tensor_copy(
                        dst3[:, 8 * f:8 * f + 8, 128 * tau:128 * tau + 128],
                        bank[:, :].rearrange("p (c r) -> p c r", c=8))
                v_ev = em.op("vector", ev, wait=[("pe", v_tp)], inc="dve")
                pbf_free[f % 2] = v_ev
            return v_ev

        def emit_residual_tiles(src_fm, base_in_ap, out_writer, pbf, reload_sem_name):
            raise NotImplementedError

        with ExitStack() as _st2:
            x_tm = _st2.enter_context(nc.sbuf_tensor("x_tm", [128, 2 * 4096], F32, allow_name_mangling=True))
            xn_fm = _st2.enter_context(nc.sbuf_tensor("xn_fm", [128, NC * 1024], BF16, allow_name_mangling=True))
            xn_stage = _st2.enter_context(nc.sbuf_tensor("xn_stage", [128, 4096], BF16, allow_name_mangling=True))
            pbf_0 = _st2.enter_context(nc.psum_tensor("pbf0", [128, 1024], BF16, allow_name_mangling=True))
            pbf_1 = _st2.enter_context(nc.psum_tensor("pbf1", [128, 1024], BF16, allow_name_mangling=True))
            pbf = [pbf_0, pbf_1]
            with nc.Block("p1") as block:
                em.op("sync", lambda e: e.dma_start(out=ident_sb[:, :], in_=ident_bf.ap()[:, :]),
                      inc="s_sync", inc_k=16)
                em.op("sync", lambda e: e.dma_start(out=ident32_sb[:, :], in_=ident_f32.ap()[:, :]),
                      inc="s_sync", inc_k=16)
                em.op("sync", lambda e: e.dma_start(out=tri_sb[:, :], in_=tri01.ap()[:, :]),
                      inc="s_sync", inc_k=16)
                em.op("sync", lambda e: e.dma_start(out=e4_sb[:, :], in_=e4c.ap()[:, :]),
                      inc="s_sync", inc_k=16)
                em.op("sync", lambda e: e.dma_start(out=shift_sb[:, :], in_=shiftc.ap()[:, :]),
                      inc="s_sync", inc_k=16)
                em.op("sync", lambda e: e.dma_start(out=bias_sb[:, 0:3 * NH], in_=bqkv[:, :]),
                      inc="s_sync", inc_k=16)
                em.op("sync", lambda e: e.dma_start(out=bias_sb[:, 3 * NH:3 * NH + NC], in_=bo4[:, :]),
                      inc="s_sync", inc_k=16)
                em.op("gpsimd", lambda e: e.dma_start(out=wr_sb[:, :], in_=wr4[:, :]),
                      inc="s_gp", inc_k=16)
                em.op("gpsimd", lambda e: e.dma_start(out=a_sb[:, :], in_=a4[:, :]),
                      inc="s_gp", inc_k=16)
                em.op("gpsimd", lambda e: e.dma_start(out=bmat_sb[:, :], in_=bt_in[:, :]),
                      inc="s_gp", inc_k=16)
                v_consts = em.val("s_sync")
                em.wait("tensor", [("s_sync", v_consts)])

                pbf_free = {0: None, 1: None}
                x_done = [None, None]
                v_tp_prev = None
                v_fm_done = None
                for tau in range(NT):
                    half = tau % 2
                    waits = []
                    if x_done[half] is not None:
                        waits.append(("dve", x_done[half]))
                    v_ld = em.op("sync", lambda e, tau=tau, half=half: e.dma_start(
                        out=x_tm[:, 4096 * half:4096 * half + 4096],
                        in_=x_in[128 * tau:128 * tau + 128, :]),
                        wait=waits, inc="s_sync", inc_k=16)
                    em.wait("vector", [("s_sync", v_ld)])
                    em.wait("scalar", [("s_sync", v_ld)])
                    sq_wait = ("pe", v_tp_prev) if v_tp_prev is not None else None
                    v_xn = emit_ln_tile(x_tm[:, 4096 * half:4096 * half + 4096],
                                        xn_stage, 6 * (tau % 2), sq_wait)
                    x_done[half] = v_xn
                    v_fm_done = emit_tile_to_fm(xn_stage, xn_fm, tau, pbf, pbf_free,
                                                ("dve", v_xn))
                    v_tp_prev = em.val("pe")

                v_pub = None
                for ic in range(NC):
                    v_pub = em.op("sync", lambda e, ic=ic: e.dma_start(
                        out=xn_pub[128 * ic:128 * ic + 128, :],
                        in_=xn_fm[:, 1024 * ic:1024 * ic + 1024]),
                        wait=([("dve", v_fm_done)] if ic == 0 else ()),
                        inc="s_sync", inc_k=16)
                em.op("gpsimd", lambda e: e.collective_compute(
                    "AllGather", ALU.bypass, replica_groups=[list(range(8))],
                    ins=[xn_pub.opt()], outs=[xn_gath.opt()]),
                    wait=[("s_sync", v_pub)], inc="s_cc", inc_k=1)
                em.drain_and_barrier(block)

        with ExitStack() as _st3:
            wt_sb = _st3.enter_context(nc.sbuf_tensor("wt_sb", [128, 12 * 4096], BF16, allow_name_mangling=True))
            xnb = _st3.enter_context(nc.sbuf_tensor("xnb", [128, 2 * 16 * 1024], BF16, allow_name_mangling=True))
            kv_stage = _st3.enter_context(nc.sbuf_tensor("kv_stage", [128, 2 * 1024], BF16, allow_name_mangling=True))
            pb_0 = _st3.enter_context(nc.psum_tensor("pb0", [128, 512], F32, allow_name_mangling=True))
            pb_1 = _st3.enter_context(nc.psum_tensor("pb1", [128, 512], F32, allow_name_mangling=True))
            pb_2 = _st3.enter_context(nc.psum_tensor("pb2", [128, 512], F32, allow_name_mangling=True))
            pb_3 = _st3.enter_context(nc.psum_tensor("pb3", [128, 512], F32, allow_name_mangling=True))
            pb = [pb_0, pb_1, pb_2, pb_3]
            wsrc = {0: wq4, 1: wk4, 2: wv4}
            wdst = {0: q_scr, 1: k_scr, 2: v_scr}
            with nc.Block("p2") as block:
                v_wt = None
                for w in range(3):
                    for oc in range(NH):
                        slot = 4 * w + oc
                        v_wt = em.op("gpsimd", lambda e, w=w, oc=oc, slot=slot: e.dma_start(
                            out=wt_sb[:, 4096 * slot:4096 * slot + 4096],
                            in_=wsrc[w][oc].rearrange("p c o -> p (c o)")),
                            inc="s_gp", inc_k=16)
                xg3 = xn_gath.rearrange("(b c p) t -> b c p t", b=8, p=128)
                xnb_ld = {}
                xnb_user = [None, None]
                bank_i = 0
                pb_free = [None] * 4
                stage_free = [None, None]
                st_i = 0
                for rb in range(NRB):
                    for hb in range(2):
                        par = (2 * rb + hb) % 2
                        waits = [("s_cc", 1)]
                        if xnb_user[par] is not None:
                            waits.append(("pe", xnb_user[par]))
                        v = em.op("sync", lambda e, rb=rb, hb=hb, par=par: e.dma_start(
                            out=xnb[:, 16384 * par:16384 * par + 16384].rearrange(
                                "p (c t) -> p c t", c=16),
                            in_=xg3[rb, 16 * hb:16 * hb + 16].rearrange(
                                "c p t -> p c t")),
                            wait=waits, inc="s_sync", inc_k=16)
                        xnb_ld[(rb, hb)] = v
                    for slot in range(12):
                        w, oc = slot // 4, slot % 4
                        for hf in range(2):
                            bank = bank_i % 4
                            bank_i += 1
                            w0 = [("s_gp", v_wt), ("s_sync", xnb_ld[(rb, 0)])]
                            if pb_free[bank] is not None:
                                w0.append(("act", pb_free[bank]))
                            w16 = [("s_sync", xnb_ld[(rb, 1)])]
                            v_mm = None
                            for ic in range(NC):
                                hb = ic // 16
                                par = (2 * rb + hb) % 2
                                def bld(e, bank=bank, slot=slot, ic=ic, hf=hf, par=par):
                                    return e.matmul(
                                        pb[bank][:, :],
                                        wt_sb[:, 4096 * slot + 128 * ic:4096 * slot + 128 * ic + 128],
                                        xnb[:, 16384 * par + 1024 * (ic % 16) + 512 * hf:
                                            16384 * par + 1024 * (ic % 16) + 512 * hf + 512],
                                        start=(ic == 0), stop=(ic == NC - 1))
                                ww = w0 if ic == 0 else (w16 if ic == 16 else ())
                                v_mm = em.op("tensor", bld, wait=ww,
                                             inc=("pe" if ic == NC - 1 else None))
                            xnb_user[(2 * rb) % 2] = v_mm
                            xnb_user[(2 * rb + 1) % 2] = v_mm
                            sslot = st_i % 2
                            ev_w = [("pe", v_mm)]
                            if stage_free[sslot] is not None:
                                ev_w.append(("s_sync", stage_free[sslot]))
                            def ev(e, bank=bank, sslot=sslot, hf=hf, w=w, oc=oc):
                                return e.activation(
                                    kv_stage[:, 1024 * sslot + 512 * hf:1024 * sslot + 512 * hf + 512],
                                    pb[bank][:, :], AF.Identity,
                                    bias=bias_sb[:, 4 * w + oc:4 * w + oc + 1], scale=1.0)
                            v_ev = em.op("scalar", ev, wait=ev_w, inc="act")
                            pb_free[bank] = v_ev
                            if hf == 1:
                                v_st = em.op("sync", lambda e, w=w, oc=oc, rb=rb, sslot=sslot: e.dma_start(
                                    out=wdst[w][128 * oc:128 * oc + 128, 1024 * rb:1024 * rb + 1024],
                                    in_=kv_stage[:, 1024 * sslot:1024 * sslot + 1024]),
                                    wait=[("act", v_ev)], inc="s_sync", inc_k=16)
                                stage_free[sslot] = v_st
                                st_i += 1
                em.drain_and_barrier(block)

        with ExitStack() as _st4:
            qbuf = _st4.enter_context(nc.sbuf_tensor("qbuf", [128, 2 * 4096], BF16, allow_name_mangling=True))
            kbuf = _st4.enter_context(nc.sbuf_tensor("kbuf", [128, 2 * 4096], BF16, allow_name_mangling=True))
            vbuf = _st4.enter_context(nc.sbuf_tensor("vbuf", [128, 2 * 4096], BF16, allow_name_mangling=True))
            pP = _st4.enter_context(nc.sbuf_tensor("pP", [128, 4 * 4096], BF16, allow_name_mangling=True))
            vtm = _st4.enter_context(nc.sbuf_tensor("vtm", [128, 4096], BF16, allow_name_mangling=True))
            ptst = _st4.enter_context(nc.sbuf_tensor("ptst", [128, 2 * 512], BF16, allow_name_mangling=True))
            lparts = _st4.enter_context(nc.sbuf_tensor("lparts", [128, 48], F32, allow_name_mangling=True))
            lsum = _st4.enter_context(nc.sbuf_tensor("lsum", [128, 8], F32, allow_name_mangling=True))
            att_stage = _st4.enter_context(nc.sbuf_tensor("att_stage", [128, 2 * 512], BF16, allow_name_mangling=True))
            pb_0 = _st4.enter_context(nc.psum_tensor("pb0", [128, 512], F32, allow_name_mangling=True))
            pb_1 = _st4.enter_context(nc.psum_tensor("pb1", [128, 512], F32, allow_name_mangling=True))
            pb_2 = _st4.enter_context(nc.psum_tensor("pb2", [128, 512], F32, allow_name_mangling=True))
            pb_3 = _st4.enter_context(nc.psum_tensor("pb3", [128, 512], F32, allow_name_mangling=True))
            pbo_0 = _st4.enter_context(nc.psum_tensor("pbo0", [128, 512], F32, allow_name_mangling=True))
            pbo_1 = _st4.enter_context(nc.psum_tensor("pbo1", [128, 512], F32, allow_name_mangling=True))
            pbf_0 = _st4.enter_context(nc.psum_tensor("pbf0", [128, 1024], BF16, allow_name_mangling=True))
            pbf_1 = _st4.enter_context(nc.psum_tensor("pbf1", [128, 1024], BF16, allow_name_mangling=True))
            pbs = [pb_0, pb_1, pb_2, pb_3]
            pbo = [pbo_0, pbo_1]
            pbf = [pbf_0, pbf_1]
            with nc.Block("p3") as block:
                sbank_free = [None] * 4
                sbank_i = 0
                pbf_free = [None, None]
                pbo_free = [None, None]
                buf_user = [None, None]
                stage_free = [None, None]
                v_att_st = None
                ug = 0
                for u in range(8):
                    hh, s = u // 2, u % 2
                    slot = u % 2
                    lw = []
                    if buf_user[slot] is not None:
                        lw.append(("pe", buf_user[slot]))
                    v_q = em.op("sync", lambda e, hh=hh, s=s, slot=slot: e.dma_start(
                        out=qbuf[:, 4096 * slot:4096 * slot + 4096],
                        in_=q_scr[128 * hh:128 * hh + 128, 4096 * s:4096 * s + 4096]),
                        wait=lw, inc="s_sync", inc_k=16)
                    v_k = em.op("sync", lambda e, hh=hh, s=s, slot=slot: e.dma_start(
                        out=kbuf[:, 4096 * slot:4096 * slot + 4096],
                        in_=k_scr[128 * hh:128 * hh + 128, 4096 * s:4096 * s + 4096]),
                        wait=lw, inc="s_sync", inc_k=16)
                    v_v = em.op("scalar", lambda e, hh=hh, s=s, slot=slot: e.dma_start(
                        out=vbuf[:, 4096 * slot:4096 * slot + 4096],
                        in_=v_scr[128 * hh:128 * hh + 128, 4096 * s:4096 * s + 4096]),
                        wait=lw, inc="s_actq", inc_k=16)

                    v_vt_ev = None
                    for f in range(4):
                        bank = pbf[f % 2]
                        tw = [("s_actq", v_v)]
                        if pbf_free[f % 2] is not None:
                            tw.append(("dve", pbf_free[f % 2]))
                        v_tp = None
                        for k in range(8):
                            c = 8 * f + k
                            def bld(e, bank=bank, k=k, c=c, slot=slot):
                                return e.transpose(
                                    bank[:, 128 * k:128 * k + 128],
                                    vbuf[:, 4096 * slot + 128 * c:4096 * slot + 128 * c + 128],
                                    ident_sb[:, :])
                            v_tp = em.op("tensor", bld, wait=(tw if k == 0 else ()),
                                         inc=("pe" if k == 7 else None))
                        def evv(e, bank=bank, f=f):
                            return e.tensor_copy(vtm[:, 1024 * f:1024 * f + 1024], bank[:, :])
                        v_vt_ev = em.op("vector", evv, wait=[("pe", v_tp)], inc="dve")
                        pbf_free[f % 2] = v_vt_ev

                    v_last_pe_unit = None
                    for qg in range(8):
                        sl = qg
                        prefix = 512 * (sl + 1)
                        ntc = prefix // 128
                        v_lp0 = em.op("vector", lambda e: e.memset(lparts[:, :], 0.0),
                                      wait=[("act", em.val("act")),
                                            ("dve", em.val("dve"))], inc="dve")
                        v_last_score = None
                        v_norm_done = [None] * 4
                        for qt in range(4):
                            for c in range(sl + 1):
                                bank = sbank_i % 4
                                sbank_i += 1
                                width = 512 if c < sl else 128 * (qt + 1)
                                mm_w = [("s_sync", v_k), ("s_sync", v_q)]
                                if sbank_free[bank] is not None:
                                    mm_w.append(("act", sbank_free[bank]))
                                def bld(e, bank=bank, qt=qt, c=c, width=width,
                                        slot=slot, qg=qg):
                                    return e.matmul(
                                        pbs[bank][:, 0:width],
                                        qbuf[:, 4096 * slot + 512 * qg + 128 * qt:
                                             4096 * slot + 512 * qg + 128 * qt + 128],
                                        kbuf[:, 4096 * slot + 512 * c:
                                             4096 * slot + 512 * c + width],
                                        start=True, stop=True)
                                v_mm = em.op("tensor", bld, wait=mm_w, inc="pe")
                                v_last_score = v_mm
                                pcol = 4096 * qt + 512 * c
                                if c < sl:
                                    def ev(e, bank=bank, pcol=pcol, qt=qt, c=c):
                                        return e.activation(
                                            pP[:, pcol:pcol + 512], pbs[bank][:, 0:512],
                                            AF.Exp, scale=EXP_SCALE, bias=shift_sb[:, 0:1],
                                            accum_out=lparts[:, 12 * qt + c:12 * qt + c + 1])
                                    v_ev = em.op("scalar", ev,
                                                 wait=[("pe", v_mm), ("dve", v_lp0)],
                                                 inc="act")
                                else:
                                    if qt > 0:
                                        def ev1(e, bank=bank, pcol=pcol, qt=qt, c=c):
                                            return e.activation(
                                                pP[:, pcol:pcol + 128 * qt],
                                                pbs[bank][:, 0:128 * qt],
                                                AF.Exp, scale=EXP_SCALE, bias=shift_sb[:, 0:1],
                                                accum_out=lparts[:, 12 * qt + c:12 * qt + c + 1])
                                        em.op("scalar", ev1,
                                              wait=[("pe", v_mm), ("dve", v_lp0)],
                                              inc="act")
                                    def ev2(e, bank=bank, pcol=pcol, qt=qt):
                                        return e.activation(
                                            pP[:, pcol + 128 * qt:pcol + 128 * qt + 128],
                                            pbs[bank][:, 128 * qt:128 * qt + 128],
                                            AF.Exp, scale=EXP_SCALE, bias=shift_sb[:, 0:1])
                                    v_ev2 = em.op("scalar", ev2, wait=[("pe", v_mm)],
                                                  inc="act")
                                    def msk(e, pcol=pcol, qt=qt):
                                        return e.tensor_mul(
                                            pP[:, pcol + 128 * qt:pcol + 128 * qt + 128],
                                            pP[:, pcol + 128 * qt:pcol + 128 * qt + 128],
                                            tri_sb[:, :])
                                    v_mk = em.op("vector", msk,
                                                 wait=[("act", v_ev2), ("dve", v_lp0)],
                                                 inc="dve")
                                    def dsum(e, pcol=pcol, qt=qt, c=c):
                                        return e.reduce_sum(
                                            lparts[:, 12 * qt + c + 1:12 * qt + c + 2],
                                            pP[:, pcol + 128 * qt:pcol + 128 * qt + 128],
                                            axis=AX.X)
                                    em.op("vector", dsum, wait=[("dve", v_mk)], inc="dve")
                                    v_ev = em.val("act")
                                sbank_free[bank] = em.val("act")
                            wq = 512 * sl + 128 * (qt + 1)
                            if wq < prefix:
                                def z(e, qt=qt, wq=wq, prefix=prefix):
                                    return e.memset(pP[:, 4096 * qt + wq:4096 * qt + prefix], 0.0)
                                em.op("vector", z, inc="dve")
                        act_all = em.val("act")
                        dve_all = em.val("dve")
                        for qt in range(4):
                            def lsb(e, qt=qt, sl=sl):
                                return e.reduce_sum(
                                    lsum[:, qt:qt + 1],
                                    lparts[:, 12 * qt:12 * qt + sl + 2], axis=AX.X)
                            v_ls = em.op("vector", lsb,
                                         wait=[("act", act_all), ("dve", dve_all)],
                                         inc="dve")
                            v_rc = em.op("vector", lambda e, qt=qt: e.reciprocal(
                                lsum[:, 4 + qt:5 + qt], lsum[:, qt:qt + 1]),
                                wait=[("dve", v_ls)], inc="dve")
                            wqv = 512 * sl + 128 * (qt + 1)
                            def nrm(e, qt=qt, wqv=wqv):
                                return e.tensor_scalar(
                                    pP[:, 4096 * qt:4096 * qt + wqv],
                                    pP[:, 4096 * qt:4096 * qt + wqv],
                                    lsum[:, 4 + qt:5 + qt], None, ALU.mult)
                            v_norm_done[qt] = em.op("vector", nrm,
                                                    wait=[("dve", v_rc)], inc="dve")

                        obank = pbo[ug % 2]
                        ptst_free = [None, None]
                        v_av = None
                        for c in range(ntc):
                            bank = pbf[c % 2]
                            v_tp = None
                            for qt in range(4):
                                w0 = [("dve", v_norm_done[3]), ("dve", em.val("dve"))]
                                if pbf_free[c % 2] is not None:
                                    w0.append(("dve", pbf_free[c % 2]))
                                def bld(e, bank=bank, qt=qt, c=c):
                                    return e.transpose(
                                        bank[:, 128 * qt:128 * qt + 128],
                                        pP[:, 4096 * qt + 128 * c:4096 * qt + 128 * c + 128],
                                        ident_sb[:, :])
                                v_tp = em.op("tensor", bld,
                                             wait=(w0 if qt == 0 else ()),
                                             inc=("pe" if qt == 3 else None))
                            pslot = c % 2
                            ev_w = [("pe", v_tp)]
                            if ptst_free[pslot] is not None:
                                ev_w.append(("pe", ptst_free[pslot]))
                            def evp(e, bank=bank, pslot=pslot):
                                return e.tensor_copy(
                                    ptst[:, 512 * pslot:512 * pslot + 512], bank[:, 0:512])
                            v_pt = em.op("vector", evp, wait=ev_w, inc="dve")
                            pbf_free[c % 2] = v_pt
                            av_w = [("dve", v_pt), ("dve", v_vt_ev)]
                            if c == 0 and pbo_free[ug % 2] is not None:
                                av_w.append(("dve", pbo_free[ug % 2]))
                            def av(e, obank=obank, c=c, pslot=pslot, ntc=ntc):
                                return e.matmul(
                                    obank[:, :], vtm[:, 128 * c:128 * c + 128],
                                    ptst[:, 512 * pslot:512 * pslot + 512],
                                    start=(c == 0), stop=(c == ntc - 1))
                            v_av = em.op("tensor", av, wait=av_w, inc="pe")
                            ptst_free[pslot] = v_av
                        v_last_pe_unit = v_av

                        ev_w = [("pe", v_av)]
                        if stage_free[ug % 2] is not None:
                            ev_w.append(("s_gp", stage_free[ug % 2]))
                        def oev(e, obank=obank, sslot=ug % 2):
                            return e.tensor_copy(
                                att_stage[:, 512 * sslot:512 * sslot + 512], obank[:, :])
                        v_oev = em.op("vector", oev, wait=ev_w, inc="dve")
                        pbo_free[ug % 2] = v_oev
                        r = 4 * s + qg // 2
                        colo = 512 * (qg % 2)
                        v_att_st = em.op("gpsimd", lambda e, r=r, hh=hh, colo=colo,
                                         sslot=ug % 2: e.dma_start(
                            out=att_pub[512 * r + 128 * hh:512 * r + 128 * hh + 128,
                                        colo:colo + 512],
                            in_=att_stage[:, 512 * sslot:512 * sslot + 512]),
                            wait=[("dve", v_oev)], inc="s_gp", inc_k=16)
                        stage_free[ug % 2] = v_att_st
                        ug += 1
                    buf_user[slot] = v_last_pe_unit

                def a2a(e):
                    with allow_ici_all_to_all():
                        return e.collective_compute(
                            "AllToAll", ALU.bypass, replica_groups=[list(range(8))],
                            ins=[att_pub.opt()], outs=[att_own.opt()])
                em.op("gpsimd", a2a, wait=[("s_gp", em.val("s_gp"))],
                      inc="s_cc", inc_k=1)
                em.drain_and_barrier(block)

        with nc.sbuf_tensor("outT", [128, NC * 1024], BF16, allow_name_mangling=True) as outT:
            with ExitStack() as _st5:
                att_full = _st5.enter_context(nc.sbuf_tensor("att_full", [128, NC * 1024], BF16, allow_name_mangling=True))
                wbuf = _st5.enter_context(nc.sbuf_tensor("wbuf", [128, 2 * 4096], BF16, allow_name_mangling=True))
                pb_0 = _st5.enter_context(nc.psum_tensor("pb0", [128, 512], F32, allow_name_mangling=True))
                pb_1 = _st5.enter_context(nc.psum_tensor("pb1", [128, 512], F32, allow_name_mangling=True))
                pb_2 = _st5.enter_context(nc.psum_tensor("pb2", [128, 512], F32, allow_name_mangling=True))
                pb_3 = _st5.enter_context(nc.psum_tensor("pb3", [128, 512], F32, allow_name_mangling=True))
                pb = [pb_0, pb_1, pb_2, pb_3]
                with nc.Block("p4") as block:
                    v_att_ld = None
                    for ic in range(NC):
                        v_att_ld = em.op("sync", lambda e, ic=ic: e.dma_start(
                            out=att_full[:, 1024 * ic:1024 * ic + 1024],
                            in_=att_own[128 * ic:128 * ic + 128, :]),
                            wait=([("s_cc", 2)] if ic == 0 else ()),
                            inc="s_sync", inc_k=16)
                    wb_user = [None, None]
                    pb_free = [None] * 4
                    bank_i = 0
                    for oc in range(NC):
                        wslot = oc % 2
                        ww = []
                        if wb_user[wslot] is not None:
                            ww.append(("pe", wb_user[wslot]))
                        v_w = em.op("gpsimd", lambda e, oc=oc, wslot=wslot: e.dma_start(
                            out=wbuf[:, 4096 * wslot:4096 * wslot + 4096],
                            in_=wo4[oc].rearrange("p c o -> p (c o)")),
                            wait=ww, inc="s_gp", inc_k=16)
                        for hf in range(2):
                            bank = bank_i % 4
                            bank_i += 1
                            mm_w = [("s_gp", v_w), ("s_sync", v_att_ld)]
                            if pb_free[bank] is not None:
                                mm_w.append(("act", pb_free[bank]))
                            v_mm = None
                            for ic in range(NC):
                                def bld(e, bank=bank, wslot=wslot, ic=ic, hf=hf):
                                    return e.matmul(
                                        pb[bank][:, :],
                                        wbuf[:, 4096 * wslot + 128 * ic:4096 * wslot + 128 * ic + 128],
                                        att_full[:, 1024 * ic + 512 * hf:1024 * ic + 512 * hf + 512],
                                        start=(ic == 0), stop=(ic == NC - 1))
                                v_mm = em.op("tensor", bld,
                                             wait=(mm_w if ic == 0 else ()),
                                             inc=("pe" if ic == NC - 1 else None))
                            wb_user[wslot] = v_mm
                            def ev(e, bank=bank, oc=oc, hf=hf):
                                return e.activation(
                                    outT[:, 1024 * oc + 512 * hf:1024 * oc + 512 * hf + 512],
                                    pb[bank][:, :], AF.Identity,
                                    bias=bias_sb[:, 3 * NH + oc:3 * NH + oc + 1],
                                    scale=1.0)
                            v_ev = em.op("scalar", ev, wait=[("pe", v_mm)], inc="act")
                            pb_free[bank] = v_ev
                    em.drain_and_barrier(block)

            with ExitStack() as _st6:
                x_tm = _st6.enter_context(nc.sbuf_tensor("x_tm", [128, 2 * 4096], F32, allow_name_mangling=True))
                x1_t = _st6.enter_context(nc.sbuf_tensor("x1_t", [128, 4096], F32, allow_name_mangling=True))
                xn2_stage = _st6.enter_context(nc.sbuf_tensor("xn2_stage", [128, 4096], BF16, allow_name_mangling=True))
                xn2_fm = _st6.enter_context(nc.sbuf_tensor("xn2_fm", [128, NC * 1024], BF16, allow_name_mangling=True))
                l4 = _st6.enter_context(nc.sbuf_tensor("l4", [128, NT * 4], F32, allow_name_mangling=True))
                gsc = _st6.enter_context(nc.sbuf_tensor("gsc", [128, NT * 8], F32, allow_name_mangling=True))
                g4 = _st6.enter_context(nc.sbuf_tensor("g4", [128, NT * 4], F32, allow_name_mangling=True))
                gt_sb = _st6.enter_context(nc.sbuf_tensor("gt_sb", [4, 1024], BF16, allow_name_mangling=True))
                gb_sb = _st6.enter_context(nc.sbuf_tensor("gb_sb", [64, 1024], BF16, allow_name_mangling=True))
                gmid = _st6.enter_context(nc.sbuf_tensor("gmid", [64, 1024], BF16, allow_name_mangling=True))
                pb_0 = _st6.enter_context(nc.psum_tensor("pb0", [128, 512], F32, allow_name_mangling=True))
                pb_1 = _st6.enter_context(nc.psum_tensor("pb1", [128, 512], F32, allow_name_mangling=True))
                pb_2 = _st6.enter_context(nc.psum_tensor("pb2", [128, 512], F32, allow_name_mangling=True))
                pb_3 = _st6.enter_context(nc.psum_tensor("pb3", [128, 512], F32, allow_name_mangling=True))
                pbm = _st6.enter_context(nc.psum_tensor("pbm", [64, 1024], F32, allow_name_mangling=True))
                pbf_0 = _st6.enter_context(nc.psum_tensor("pbf0", [128, 1024], BF16, allow_name_mangling=True))
                pbf_1 = _st6.enter_context(nc.psum_tensor("pbf1", [128, 1024], BF16, allow_name_mangling=True))
                pb = [pb_0, pb_1, pb_2, pb_3]
                pbf = [pbf_0, pbf_1]
                with nc.Block("p56") as block:
                    pbf_free = {0: None, 1: None}
                    x_done = [None, None]
                    x1_free = None
                    v_xn2_done = None
                    v_tp_prev = None
                    for tau in range(NT):
                        half = tau % 2
                        lw = []
                        if x_done[half] is not None:
                            lw.append(("dve", x_done[half]))
                        v_ld = em.op("sync", lambda e, tau=tau, half=half: e.dma_start(
                            out=x_tm[:, 4096 * half:4096 * half + 4096],
                            in_=x_in[128 * tau:128 * tau + 128, :]),
                            wait=lw, inc="s_sync", inc_k=16)
                        v_add = None
                        for f in range(4):
                            bank = pbf[f % 2]
                            tw = [("act", em.val("act"))]
                            if pbf_free[f % 2] is not None:
                                tw.append(("dve", pbf_free[f % 2]))
                            v_tp = None
                            for k in range(8):
                                oc = 8 * f + k
                                def bld(e, bank=bank, k=k, oc=oc, tau=tau):
                                    return e.transpose(
                                        bank[:, 128 * k:128 * k + 128],
                                        outT[:, 1024 * oc + 128 * tau:1024 * oc + 128 * tau + 128],
                                        ident_sb[:, :])
                                v_tp = em.op("tensor", bld, wait=(tw if k == 0 else ()),
                                             inc=("pe" if k == 7 else None))
                            aw = [("pe", v_tp), ("s_sync", v_ld)]
                            if x1_free is not None and f == 0:
                                aw += [("act", x1_free[0]), ("s_gp", x1_free[1]),
                                       ("dve", x1_free[2])]
                            def addb(e, bank=bank, f=f, half=half):
                                return e.tensor_add(
                                    x1_t[:, 1024 * f:1024 * f + 1024],
                                    x_tm[:, 4096 * half + 1024 * f:4096 * half + 1024 * f + 1024],
                                    bank[:, :])
                            v_add = em.op("vector", addb, wait=aw, inc="dve")
                            pbf_free[f % 2] = v_add
                        x_done[half] = v_add
                        v_st = em.op("gpsimd", lambda e, tau=tau: e.dma_start(
                            out=x1_dram[128 * tau:128 * tau + 128, :], in_=x1_t[:, :]),
                            wait=[("dve", v_add)], inc="s_gp", inc_k=16)
                        em.wait("scalar", [("dve", v_add)])
                        em.wait("vector", [("dve", v_add)])
                        sq_wait = ("pe", v_tp_prev) if v_tp_prev is not None else None
                        v_xn2 = emit_ln_tile(x1_t, xn2_stage, 6 * (tau % 2), sq_wait)
                        x1_free = (em.val("act"), v_st, v_xn2)
                        v_xn2_done = emit_tile_to_fm(xn2_stage, xn2_fm, tau, pbf,
                                                     pbf_free, ("dve", v_xn2))
                        v_tp_prev = em.val("pe")

                    v_gb_evs = []
                    for tau in range(NT):
                        bank = pb[tau % 2]
                        mm_w = [("dve", v_xn2_done)]
                        if tau >= 2:
                            mm_w.append(("dve", v_gb_evs[tau - 2][0]))
                        v_mm = None
                        for ic in range(NC):
                            def bld(e, bank=bank, ic=ic, tau=tau):
                                return e.matmul(
                                    bank[:, 0:4],
                                    xn2_fm[:, 1024 * ic + 128 * tau:1024 * ic + 128 * tau + 128],
                                    wr_sb[:, 4 * ic:4 * ic + 4],
                                    start=(ic == 0), stop=(ic == NC - 1))
                            v_mm = em.op("tensor", bld, wait=(mm_w if ic == 0 else ()),
                                         inc=("pe" if ic == NC - 1 else None))
                        lts = l4[:, 4 * tau:4 * tau + 4]
                        v_l = em.op("vector", lambda e, bank=bank, lts=lts: e.tensor_copy(
                            lts, bank[:, 0:4]), wait=[("pe", v_mm)], inc="dve")
                        m1 = gsc[:, 8 * tau + 0:8 * tau + 1]
                        nm1 = gsc[:, 8 * tau + 1:8 * tau + 2]
                        eq = gsc[:, 8 * tau + 2:8 * tau + 6]
                        m2 = gsc[:, 8 * tau + 6:8 * tau + 7]
                        z = gsc[:, 8 * tau + 7:8 * tau + 8]
                        ets = g4[:, 4 * tau:4 * tau + 4]
                        v_m1 = em.op("vector", lambda e, m1=m1, lts=lts: e.reduce_max(
                            m1, lts, axis=AX.X), wait=[("dve", v_l)], inc="dve")
                        v_nm1 = em.op("vector", lambda e, nm1=nm1, lts=lts: e.reduce_max(
                            nm1, lts, axis=AX.X, negate=True),
                            wait=[("dve", v_l)], inc="dve")
                        v_eq = em.op("vector", lambda e, eq=eq, lts=lts, m1=m1: e.tensor_scalar(
                            eq, lts, m1, None, ALU.is_ge),
                            wait=[("dve", v_m1)], inc="dve")
                        v_lm = em.op("vector", lambda e, eq=eq, lts=lts: e.scalar_tensor_tensor(
                            eq, eq, -1e30, lts, ALU.mult, ALU.add),
                            wait=[("dve", v_eq)], inc="dve")
                        v_m2 = em.op("vector", lambda e, m2=m2, eq=eq: e.reduce_max(
                            m2, eq, axis=AX.X), wait=[("dve", v_lm)], inc="dve")
                        v_e = em.op("scalar", lambda e, ets=ets, lts=lts, nm1=nm1: e.activation(
                            ets, lts, AF.Exp, bias=nm1, scale=1.0),
                            wait=[("dve", v_nm1)], inc="act")
                        v_ge = em.op("vector", lambda e, eq=eq, lts=lts, m2=m2: e.tensor_scalar(
                            eq, lts, m2, None, ALU.is_ge),
                            wait=[("dve", v_m2)], inc="dve")
                        v_g = em.op("vector", lambda e, ets=ets, eq=eq: e.tensor_mul(
                            ets, ets, eq), wait=[("dve", v_ge), ("act", v_e)], inc="dve")
                        v_z = em.op("vector", lambda e, z=z, ets=ets: e.reduce_sum(
                            z, ets, axis=AX.X), wait=[("dve", v_g)], inc="dve")
                        v_rz = em.op("vector", lambda e, m1=m1, z=z: e.reciprocal(m1, z),
                                     wait=[("dve", v_z)], inc="dve")
                        v_gn = em.op("vector", lambda e, ets=ets, m1=m1: e.tensor_scalar(
                            ets, ets, m1, 4.0, ALU.mult, ALU.mult),
                            wait=[("dve", v_rz)], inc="dve")
                        v_gt = em.op("tensor", lambda e, ets=ets: e.transpose(
                            pb[2][0:4, 0:128], ets, ident32_sb[:, :]),
                            wait=[("dve", v_gn)], inc="pe")
                        v_gte = em.op("vector", lambda e, tau=tau: e.tensor_copy(
                            gt_sb[:, 128 * tau:128 * tau + 128], pb[2][0:4, 0:128]),
                            wait=[("pe", v_gt)], inc="dve")
                        v_gb = em.op("tensor", lambda e, tau=tau: e.matmul(
                            pbm[0:64, 512:640], e4_sb[:, :],
                            gt_sb[:, 128 * tau:128 * tau + 128],
                            start=True, stop=True),
                            wait=[("dve", v_gte)], inc="pe")
                        v_gbe = em.op("vector", lambda e, tau=tau: e.tensor_copy(
                            gb_sb[:, 128 * tau:128 * tau + 128], pbm[0:64, 512:640]),
                            wait=[("pe", v_gb)], inc="dve")
                        v_gb_evs.append((v_gbe, v_gb))
                    v_gb_all = em.val("dve")

                    v_mid_ev = []
                    for hf in range(2):
                        v_mm = None
                        for ic in range(NC):
                            w0 = [("dve", v_gb_all)] if ic == 0 else ()
                            def bld(e, ic=ic, hf=hf):
                                return e.matmul(
                                    pbm[0:64, 512 * hf:512 * hf + 512],
                                    a_sb[:, 64 * ic:64 * ic + 64],
                                    xn2_fm[:, 1024 * ic + 512 * hf:1024 * ic + 512 * hf + 512],
                                    start=(ic == 0), stop=(ic == NC - 1))
                            v_mm = em.op("tensor", bld, wait=w0,
                                         inc=("pe" if ic == NC - 1 else None))
                        def gm(e, hf=hf):
                            return e.tensor_mul(gmid[:, 512 * hf:512 * hf + 512],
                                                pbm[0:64, 512 * hf:512 * hf + 512],
                                                gb_sb[:, 512 * hf:512 * hf + 512])
                        v_gm = em.op("vector", gm, wait=[("pe", v_mm)], inc="dve")
                        v_mid_ev.append(v_gm)

                    v_p5_pe = v_tp_prev
                    pb_free2 = [None] * 4
                    bank_i = 0
                    for oc in range(NC):
                        for hf in range(2):
                            bank = bank_i % 4
                            bank_i += 1
                            mw = [("dve", v_mid_ev[hf])]
                            if pb_free2[bank] is not None:
                                mw.append(("act", pb_free2[bank]))
                            def bld(e, bank=bank, oc=oc, hf=hf):
                                return e.matmul(pb[bank][:, :],
                                                bmat_sb[:, 128 * oc:128 * oc + 128],
                                                gmid[:, 512 * hf:512 * hf + 512],
                                                start=True, stop=True)
                            v_mm = em.op("tensor", bld, wait=mw, inc="pe")
                            def ev(e, bank=bank, oc=oc, hf=hf):
                                return e.activation(
                                    outT[:, 1024 * oc + 512 * hf:1024 * oc + 512 * hf + 512],
                                    pb[bank][:, :], AF.Identity, scale=1.0)
                            v_dev = em.op("scalar", ev, wait=[("pe", v_mm)], inc="act")
                            pb_free2[bank] = v_dev
                    em.drain_and_barrier(block)

            with ExitStack() as _st7:
                x1b = _st7.enter_context(nc.sbuf_tensor("x1b", [128, 2 * 4096], F32, allow_name_mangling=True))
                x2_t = _st7.enter_context(nc.sbuf_tensor("x2_t", [128, 2 * 4096], F32, allow_name_mangling=True))
                pbf_0 = _st7.enter_context(nc.psum_tensor("pbf0", [128, 1024], BF16, allow_name_mangling=True))
                pbf_1 = _st7.enter_context(nc.psum_tensor("pbf1", [128, 1024], BF16, allow_name_mangling=True))
                pbf = [pbf_0, pbf_1]
                with nc.Block("p7") as block:
                    pbf_free = {0: None, 1: None}
                    x1_done = [None, None]
                    out_vals = []
                    for tau in range(NT):
                        half = tau % 2
                        lw = []
                        if x1_done[half] is not None:
                            lw.append(("dve", x1_done[half]))
                        v_ld = em.op("sync", lambda e, tau=tau, half=half: e.dma_start(
                            out=x1b[:, 4096 * half:4096 * half + 4096],
                            in_=x1_dram[128 * tau:128 * tau + 128, :]),
                            wait=lw, inc="s_sync", inc_k=16)
                        v_add = None
                        for f in range(4):
                            bank = pbf[f % 2]
                            tw = [("act", em.val("act"))]
                            if pbf_free[f % 2] is not None:
                                tw.append(("dve", pbf_free[f % 2]))
                            v_tp = None
                            for k in range(8):
                                oc = 8 * f + k
                                def bld(e, bank=bank, k=k, oc=oc, tau=tau):
                                    return e.transpose(
                                        bank[:, 128 * k:128 * k + 128],
                                        outT[:, 1024 * oc + 128 * tau:1024 * oc + 128 * tau + 128],
                                        ident_sb[:, :])
                                v_tp = em.op("tensor", bld, wait=(tw if k == 0 else ()),
                                             inc=("pe" if k == 7 else None))
                            aw = [("pe", v_tp), ("s_sync", v_ld)]
                            if len(out_vals) >= 2 and f == 0:
                                aw.append(("s_gp", out_vals[-2]))
                            def addb(e, bank=bank, f=f, half=half):
                                return e.tensor_add(
                                    x2_t[:, 4096 * half + 1024 * f:4096 * half + 1024 * f + 1024],
                                    x1b[:, 4096 * half + 1024 * f:4096 * half + 1024 * f + 1024],
                                    bank[:, :])
                            v_add = em.op("vector", addb, wait=aw, inc="dve")
                            pbf_free[f % 2] = v_add
                        x1_done[half] = v_add
                        v_st = em.op("gpsimd", lambda e, tau=tau, half=half: e.dma_start(
                            out=y_out[128 * tau:128 * tau + 128, :],
                            in_=x2_t[:, 4096 * half:4096 * half + 4096]),
                            wait=[("dve", v_add)], inc="s_gp", inc_k=16)
                        out_vals.append(v_st)
                    for t, (dst, src) in taps.items():
                        em.op("gpsimd", lambda e, dst=dst, src=src: e.dma_start(
                            out=dst[:, :], in_=src[:, :]), inc="s_gp", inc_k=16)
                    em.wait("gpsimd", [("s_gp", em.val("s_gp"))])
                    em.wait("sync", [("s_gp", em.val("s_gp"))])
                    em.flush(block)

    return nc


def _host_prepare(inputs):
    x = np.asarray(inputs["L0_attn_x"], dtype=np.float32).reshape(8192, D)
    wt = {}
    for nm, key in (("wq", "L0_attn_Wq"), ("wk", "L0_attn_Wk"), ("wv", "L0_attn_Wv")):
        W = np.asarray(inputs[key], dtype=np.float32)
        wt[nm] = W.T.reshape(NC, 128, NC, 128)
    Wo = np.asarray(inputs["L0_attn_Wo"], dtype=np.float32)
    wo4 = np.ascontiguousarray(Wo.T.reshape(NC, 128, NC, 128).transpose(2, 1, 0, 3))
    bq = np.asarray(inputs["L0_attn_bq"], dtype=np.float32).reshape(NC, 128)
    bk = np.asarray(inputs["L0_attn_bk"], dtype=np.float32).reshape(NC, 128)
    bv = np.asarray(inputs["L0_attn_bv"], dtype=np.float32).reshape(NC, 128)
    bo = np.asarray(inputs["L0_attn_bo"], dtype=np.float32)
    bo4 = np.ascontiguousarray(bo.reshape(NC, 128).T)
    A = np.asarray(inputs["L0_ffn_A"], dtype=np.float32)
    B = np.asarray(inputs["L0_ffn_B"], dtype=np.float32)
    Wr = np.asarray(inputs["L0_ffn_Wr"], dtype=np.float32)
    a_allt = A.transpose(2, 0, 1).reshape(D, 64)
    a4 = np.ascontiguousarray(
        a_allt.reshape(NC, 128, 64).transpose(1, 0, 2)).reshape(128, NC * 64)
    bt = np.ascontiguousarray(B.transpose(0, 2, 1).reshape(64, D))
    wr4 = np.ascontiguousarray(
        Wr.T.reshape(NC, 128, 4).transpose(1, 0, 2)).reshape(128, NC * 4)
    shared = dict(a4=a4, bt=bt, wr4=wr4, wo4=wo4, bo4=bo4)
    in_maps = []
    for j in range(8):
        m = dict(shared)
        m["x_own"] = np.ascontiguousarray(x[1024 * j:1024 * (j + 1)])
        hc = slice(4 * j, 4 * j + 4)
        for nm in ("wq", "wk", "wv"):
            m[nm + "4"] = np.ascontiguousarray(
                wt[nm][:, :, hc, :].transpose(2, 1, 0, 3))
        m["bqkv"] = np.ascontiguousarray(
            np.concatenate([bq[hc].T, bk[hc].T, bv[hc].T], axis=1))
        in_maps.append(m)
    return in_maps


def kernel(**inputs):
    in_maps = _host_prepare(inputs)
    nc = build_core()
    res = run_bass_kernel_spmd(nc, in_maps, core_ids=list(range(8)))
    out = np.empty((8192, D), dtype=np.float32)
    for j in range(8):
        out[1024 * j:1024 * (j + 1)] = res.results[j]["y"]
    return out.reshape(2, 4096, D)


# version 03
# speedup vs baseline: 1.2105x; 1.2105x over previous
from contextlib import ExitStack

import numpy as np
import ml_dtypes
import concourse.bass as bass
import concourse.mybir as mybir
from concourse.bass_utils import run_bass_kernel_spmd
from concourse.aot_env import allow_ici_all_to_all

F32 = mybir.dt.float32
BF16 = mybir.dt.bfloat16
AF = mybir.ActivationFunctionType
ALU = mybir.AluOpType
AX = mybir.AxisListType

D = 4096
NC = 32
NH = 4
NRB = 8
NT = 8
EXP_SCALE = 1.0 / np.sqrt(128.0)
EXP_SHIFT = -12.0
EPS = 1e-5

ENGINES = ("tensor", "vector", "scalar", "sync", "gpsimd")


class Em:

    def __init__(self, nc):
        self.nc = nc
        self.ops = {e: [] for e in ENGINES}
        self.sems = {}

    def add_sem(self, name, handle):
        self.sems[name] = [handle, 0]

    def val(self, name):
        return self.sems[name][1]

    def op(self, eng, build, wait=(), inc=None, inc_k=1):
        for (w, v) in wait:
            assert v <= self.sems[w][1], (w, v, self.sems[w][1])
        targets = [(self.sems[w][0], v) for (w, v) in wait if v > 0]
        handle = None
        if inc is not None:
            self.sems[inc][1] += inc_k
            handle = self.sems[inc][0]

        def f(e, targets=targets, build=build, handle=handle, inc_k=inc_k):
            for (h, v) in targets:
                e.wait_ge(h, v)
            ins = build(e)
            if handle is not None:
                ins.then_inc(handle, inc_k)

        self.ops[eng].append(f)
        return self.sems[inc][1] if inc is not None else None

    def wait(self, eng, waits):
        for (w, v) in waits:
            assert v <= self.sems[w][1], (w, v, self.sems[w][1])
        targets = [(self.sems[w][0], v) for (w, v) in waits if v > 0]

        def f(e, targets=targets):
            for (h, v) in targets:
                e.wait_ge(h, v)

        self.ops[eng].append(f)

    def flush(self, block):
        for e in ENGINES:
            if self.ops[e]:
                fns = self.ops[e]
                getattr(block, e)(lambda engine, fns=fns: [f(engine) for f in fns])
        self.ops = {e: [] for e in ENGINES}

    def drain_and_barrier(self, block):
        self.wait("sync", [("s_sync", self.val("s_sync"))])
        self.wait("gpsimd", [("s_gp", self.val("s_gp"))])
        self.wait("scalar", [("s_actq", self.val("s_actq")), ("act", self.val("act"))])
        self.wait("vector", [("dve", self.val("dve"))])
        self.wait("tensor", [("pe", self.val("pe"))])
        bar = self.sems["bar"]
        bar[1] += 5
        target = bar[1]
        for e in ENGINES:
            def f(eng, h=bar[0], t=target):
                eng.nop().then_inc(h, 1)
                eng.wait_ge(h, t)
            self.ops[e].append(f)
        self.flush(block)


def build_core(debug_taps=()):
    nc = bass.Bass("TRN2", target_bir_lowering=False)

    x_in = nc.dram_tensor("x_own", [NT * 128, D], F32, kind="ExternalInput").ap()
    wq4 = nc.dram_tensor("wq4", [NH, 128, NC, 128], F32, kind="ExternalInput").ap()
    wk4 = nc.dram_tensor("wk4", [NH, 128, NC, 128], F32, kind="ExternalInput").ap()
    wv4 = nc.dram_tensor("wv4", [NH, 128, NC, 128], F32, kind="ExternalInput").ap()
    wo4 = nc.dram_tensor("wo4", [NC, 128, NC, 128], F32, kind="ExternalInput").ap()
    bqkv = nc.dram_tensor("bqkv", [128, 3 * NH], F32, kind="ExternalInput").ap()
    bo4 = nc.dram_tensor("bo4", [128, NC], F32, kind="ExternalInput").ap()
    a4 = nc.dram_tensor("a4", [128, NC * 64], F32, kind="ExternalInput").ap()
    bt_in = nc.dram_tensor("bt", [64, D], F32, kind="ExternalInput").ap()
    wr4 = nc.dram_tensor("wr4", [128, NC * 4], F32, kind="ExternalInput").ap()
    y_out = nc.dram_tensor("y", [NT * 128, D], F32, kind="ExternalOutput").ap()

    xn_pub = nc.dram_tensor("xn_pub", [D, 1024], BF16).ap()
    xn_gath = nc.dram_tensor("xn_gath", [8 * D, 1024], BF16).ap()
    q_scr = nc.dram_tensor("q_scr", [NH * 128, 8192], BF16).ap()
    k_scr = nc.dram_tensor("k_scr", [NH * 128, 8192], BF16).ap()
    v_scr = nc.dram_tensor("v_scr", [NH * 128, 8192], BF16).ap()
    att_pub = nc.dram_tensor("att_pub", [D, 1024], BF16).ap()
    att_own = nc.dram_tensor("att_own", [D, 1024], BF16).ap()
    x1_dram = nc.dram_tensor("x1_scr", [NT * 128, D], F32).ap()

    ident_bf = nc.inline_tensor(np.eye(128, dtype=ml_dtypes.bfloat16), "c_ident_bf")
    ident_f32 = nc.inline_tensor(np.eye(128, dtype=np.float32), "c_ident_f32")
    qi = np.arange(128)[:, None]
    ti = np.arange(128)[None, :]
    tri01 = nc.inline_tensor((ti <= qi).astype(ml_dtypes.bfloat16), "c_tri01")
    e4c = nc.inline_tensor(
        np.repeat(np.eye(4, dtype=ml_dtypes.bfloat16), 16, axis=1), "c_e4")
    shiftc = nc.inline_tensor(
        np.full((128, 1), EXP_SHIFT, dtype=np.float32), "c_shift")

    taps = {}
    for t in debug_taps:
        src = {"xn_gath": xn_gath, "q_scr": q_scr, "k_scr": k_scr,
               "v_scr": v_scr, "att_pub": att_pub, "att_own": att_own,
               "x1": x1_dram}[t]
        shp = list(src.tensor.shape)
        taps[t] = (nc.dram_tensor("tap_" + t, shp, src.tensor.dtype,
                                  kind="ExternalOutput").ap(), src)

    with ExitStack() as _st1:
        ident_sb = _st1.enter_context(nc.sbuf_tensor("ident_sb", [128, 128], BF16, allow_name_mangling=True))
        ident32_sb = _st1.enter_context(nc.sbuf_tensor("ident32_sb", [128, 128], F32, allow_name_mangling=True))
        tri_sb = _st1.enter_context(nc.sbuf_tensor("tri_sb", [128, 128], BF16, allow_name_mangling=True))
        e4_sb = _st1.enter_context(nc.sbuf_tensor("e4_sb", [4, 64], BF16, allow_name_mangling=True))
        bias_sb = _st1.enter_context(nc.sbuf_tensor("bias_sb", [128, 3 * NH + NC], F32, allow_name_mangling=True))
        wr_sb = _st1.enter_context(nc.sbuf_tensor("wr_sb", [128, NC * 4], BF16, allow_name_mangling=True))
        a_sb = _st1.enter_context(nc.sbuf_tensor("a_sb", [128, NC * 64], BF16, allow_name_mangling=True))
        bmat_sb = _st1.enter_context(nc.sbuf_tensor("bmat_sb", [64, D], BF16, allow_name_mangling=True))
        lnst = _st1.enter_context(nc.sbuf_tensor("lnst", [128, 16], F32, allow_name_mangling=True))
        shift_sb = _st1.enter_context(nc.sbuf_tensor("shift_sb", [128, 1], F32, allow_name_mangling=True))
        h_pe = _st1.enter_context(nc.semaphore("sem_pe"))
        h_dve = _st1.enter_context(nc.semaphore("sem_dve"))
        h_act = _st1.enter_context(nc.semaphore("sem_act"))
        h_sync = _st1.enter_context(nc.semaphore("s_sync"))
        h_gp = _st1.enter_context(nc.semaphore("s_gp"))
        h_actq = _st1.enter_context(nc.semaphore("s_actq"))
        h_cc = _st1.enter_context(nc.semaphore("s_cc"))
        h_bar = _st1.enter_context(nc.semaphore("bar"))
        em = Em(nc)
        for nm, h in (("pe", h_pe), ("dve", h_dve), ("act", h_act),
                      ("s_sync", h_sync), ("s_gp", h_gp), ("s_actq", h_actq),
                      ("s_cc", h_cc), ("bar", h_bar)):
            em.add_sem(nm, h)

        def emit_ln_tile(src_tile, xn_stage, sb, extra_sq_wait=None):
            rs = lnst[:, sb + 0:sb + 1]
            qs = lnst[:, sb + 1:sb + 2]
            nmu = lnst[:, sb + 2:sb + 3]
            vv = lnst[:, sb + 3:sb + 4]
            rstd = lnst[:, sb + 4:sb + 5]
            nmr = lnst[:, sb + 5:sb + 6]
            v_rs = em.op("vector", lambda e: e.reduce_sum(rs, src_tile[:, :], axis=AX.X),
                         inc="dve")
            sq_w = [] if extra_sq_wait is None else [extra_sq_wait]
            v_qs = em.op("scalar", lambda e: e.activation(
                xn_stage[:, :], src_tile[:, :], AF.Square, accum_out=qs),
                wait=sq_w, inc="act")
            v_nmu = em.op("vector", lambda e: e.tensor_scalar(
                nmu, rs, -1.0 / D, None, ALU.mult), wait=[("dve", v_rs)], inc="dve")
            v_vv = em.op("vector", lambda e: e.scalar_tensor_tensor(
                vv, rs, nmu, qs, ALU.mult, ALU.add),
                wait=[("dve", v_nmu), ("act", v_qs)], inc="dve")
            v_vs = em.op("vector", lambda e: e.tensor_scalar(
                vv, vv, 1.0 / D, EPS, ALU.mult, ALU.add),
                wait=[("dve", v_vv)], inc="dve")
            v_ln = em.op("scalar", lambda e: e.activation(vv, vv, AF.Ln),
                wait=[("dve", v_vs)], inc="act")
            v_rst = em.op("scalar", lambda e: e.activation(
                rstd, vv, AF.Exp, scale=-0.5), wait=[("act", v_ln)], inc="act")
            v_nmr = em.op("vector", lambda e: e.tensor_mul(nmr, nmu, rstd),
                          wait=[("act", v_rst), ("dve", v_vv)], inc="dve")
            v_xn = em.op("vector", lambda e: e.tensor_scalar(
                xn_stage[:, :], src_tile[:, :], rstd, nmr, ALU.mult, ALU.add),
                wait=[("dve", v_nmr)], inc="dve")
            return v_xn

        def emit_tile_to_fm(src_bf, dst_fm, tau, pbf, pbf_free, src_wait):
            dst3 = dst_fm[:, :].rearrange("p (c r) -> p c r", c=NC)
            v_ev = None
            for f in range(4):
                bank = pbf[f % 2]
                waits = [src_wait]
                if pbf_free[f % 2] is not None:
                    waits.append(("dve", pbf_free[f % 2]))
                v_tp = None
                for k in range(8):
                    c = 8 * f + k
                    def bld(e, bank=bank, k=k, c=c):
                        return e.transpose(bank[:, 128 * k:128 * k + 128],
                                           src_bf[:, 128 * c:128 * c + 128],
                                           ident_sb[:, :])
                    v_tp = em.op("tensor", bld, wait=(waits if k == 0 else ()),
                                 inc=("pe" if k == 7 else None))
                def ev(e, bank=bank, f=f, tau=tau):
                    return e.tensor_copy(
                        dst3[:, 8 * f:8 * f + 8, 128 * tau:128 * tau + 128],
                        bank[:, :].rearrange("p (c r) -> p c r", c=8))
                v_ev = em.op("vector", ev, wait=[("pe", v_tp)], inc="dve")
                pbf_free[f % 2] = v_ev
            return v_ev

        def emit_residual_tiles(src_fm, base_in_ap, out_writer, pbf, reload_sem_name):
            raise NotImplementedError

        with ExitStack() as _st2:
            x_tm = _st2.enter_context(nc.sbuf_tensor("x_tm", [128, 2 * 4096], F32, allow_name_mangling=True))
            xn_fm = _st2.enter_context(nc.sbuf_tensor("xn_fm", [128, NC * 1024], BF16, allow_name_mangling=True))
            xn_stage = _st2.enter_context(nc.sbuf_tensor("xn_stage", [128, 4096], BF16, allow_name_mangling=True))
            pbf_0 = _st2.enter_context(nc.psum_tensor("pbf0", [128, 1024], BF16, allow_name_mangling=True))
            pbf_1 = _st2.enter_context(nc.psum_tensor("pbf1", [128, 1024], BF16, allow_name_mangling=True))
            pbf = [pbf_0, pbf_1]
            with nc.Block("p1") as block:
                em.op("sync", lambda e: e.dma_start(out=ident_sb[:, :], in_=ident_bf.ap()[:, :]),
                      inc="s_sync", inc_k=16)
                em.op("sync", lambda e: e.dma_start(out=ident32_sb[:, :], in_=ident_f32.ap()[:, :]),
                      inc="s_sync", inc_k=16)
                em.op("sync", lambda e: e.dma_start(out=tri_sb[:, :], in_=tri01.ap()[:, :]),
                      inc="s_sync", inc_k=16)
                em.op("sync", lambda e: e.dma_start(out=e4_sb[:, :], in_=e4c.ap()[:, :]),
                      inc="s_sync", inc_k=16)
                em.op("sync", lambda e: e.dma_start(out=shift_sb[:, :], in_=shiftc.ap()[:, :]),
                      inc="s_sync", inc_k=16)
                em.op("sync", lambda e: e.dma_start(out=bias_sb[:, 0:3 * NH], in_=bqkv[:, :]),
                      inc="s_sync", inc_k=16)
                em.op("sync", lambda e: e.dma_start(out=bias_sb[:, 3 * NH:3 * NH + NC], in_=bo4[:, :]),
                      inc="s_sync", inc_k=16)
                em.op("gpsimd", lambda e: e.dma_start(out=wr_sb[:, :], in_=wr4[:, :]),
                      inc="s_gp", inc_k=16)
                em.op("gpsimd", lambda e: e.dma_start(out=a_sb[:, :], in_=a4[:, :]),
                      inc="s_gp", inc_k=16)
                em.op("gpsimd", lambda e: e.dma_start(out=bmat_sb[:, :], in_=bt_in[:, :]),
                      inc="s_gp", inc_k=16)
                v_consts = em.val("s_sync")
                em.wait("tensor", [("s_sync", v_consts)])

                pbf_free = {0: None, 1: None}
                x_done = [None, None]
                v_tp_prev = None
                v_fm_done = None
                for tau in range(NT):
                    half = tau % 2
                    waits = []
                    if x_done[half] is not None:
                        waits.append(("dve", x_done[half]))
                    v_ld = em.op("sync", lambda e, tau=tau, half=half: e.dma_start(
                        out=x_tm[:, 4096 * half:4096 * half + 4096],
                        in_=x_in[128 * tau:128 * tau + 128, :]),
                        wait=waits, inc="s_sync", inc_k=16)
                    em.wait("vector", [("s_sync", v_ld)])
                    em.wait("scalar", [("s_sync", v_ld)])
                    sq_wait = ("pe", v_tp_prev) if v_tp_prev is not None else None
                    v_xn = emit_ln_tile(x_tm[:, 4096 * half:4096 * half + 4096],
                                        xn_stage, 6 * (tau % 2), sq_wait)
                    x_done[half] = v_xn
                    v_fm_done = emit_tile_to_fm(xn_stage, xn_fm, tau, pbf, pbf_free,
                                                ("dve", v_xn))
                    v_tp_prev = em.val("pe")

                v_pub = None
                for ic in range(NC):
                    v_pub = em.op("sync", lambda e, ic=ic: e.dma_start(
                        out=xn_pub[128 * ic:128 * ic + 128, :],
                        in_=xn_fm[:, 1024 * ic:1024 * ic + 1024]),
                        wait=([("dve", v_fm_done)] if ic == 0 else ()),
                        inc="s_sync", inc_k=16)
                em.op("gpsimd", lambda e: e.collective_compute(
                    "AllGather", ALU.bypass, replica_groups=[list(range(8))],
                    ins=[xn_pub.opt()], outs=[xn_gath.opt()]),
                    wait=[("s_sync", v_pub)], inc="s_cc", inc_k=1)
                em.drain_and_barrier(block)

        with ExitStack() as _st3:
            wt_sb = _st3.enter_context(nc.sbuf_tensor("wt_sb", [128, 12 * 4096], BF16, allow_name_mangling=True))
            xnb = _st3.enter_context(nc.sbuf_tensor("xnb", [128, 2 * 16 * 1024], BF16, allow_name_mangling=True))
            kv_stage = _st3.enter_context(nc.sbuf_tensor("kv_stage", [128, 2 * 1024], BF16, allow_name_mangling=True))
            pb_0 = _st3.enter_context(nc.psum_tensor("pb0", [128, 512], F32, allow_name_mangling=True))
            pb_1 = _st3.enter_context(nc.psum_tensor("pb1", [128, 512], F32, allow_name_mangling=True))
            pb_2 = _st3.enter_context(nc.psum_tensor("pb2", [128, 512], F32, allow_name_mangling=True))
            pb_3 = _st3.enter_context(nc.psum_tensor("pb3", [128, 512], F32, allow_name_mangling=True))
            pb = [pb_0, pb_1, pb_2, pb_3]
            wsrc = {0: wq4, 1: wk4, 2: wv4}
            wdst = {0: q_scr, 1: k_scr, 2: v_scr}
            with nc.Block("p2") as block:
                v_wt = None
                for w in range(3):
                    for oc in range(NH):
                        slot = 4 * w + oc
                        v_wt = em.op("gpsimd", lambda e, w=w, oc=oc, slot=slot: e.dma_start(
                            out=wt_sb[:, 4096 * slot:4096 * slot + 4096],
                            in_=wsrc[w][oc].rearrange("p c o -> p (c o)")),
                            inc="s_gp", inc_k=16)
                xg3 = xn_gath.rearrange("(b c p) t -> b c p t", b=8, p=128)
                xnb_ld = {}
                xnb_user = [None, None]
                bank_i = 0
                pb_free = [None] * 4
                stage_free = [None, None]
                st_i = 0
                for rb in range(NRB):
                    for hb in range(2):
                        par = (2 * rb + hb) % 2
                        waits = [("s_cc", 1)]
                        if xnb_user[par] is not None:
                            waits.append(("pe", xnb_user[par]))
                        v = em.op("sync", lambda e, rb=rb, hb=hb, par=par: e.dma_start(
                            out=xnb[:, 16384 * par:16384 * par + 16384].rearrange(
                                "p (c t) -> p c t", c=16),
                            in_=xg3[rb, 16 * hb:16 * hb + 16].rearrange(
                                "c p t -> p c t")),
                            wait=waits, inc="s_sync", inc_k=16)
                        xnb_ld[(rb, hb)] = v
                    for slot in range(12):
                        w, oc = slot // 4, slot % 4
                        for hf in range(2):
                            bank = bank_i % 4
                            bank_i += 1
                            w0 = [("s_gp", v_wt), ("s_sync", xnb_ld[(rb, 0)])]
                            if pb_free[bank] is not None:
                                w0.append(("act", pb_free[bank]))
                            w16 = [("s_sync", xnb_ld[(rb, 1)])]
                            v_mm = None
                            for ic in range(NC):
                                hb = ic // 16
                                par = (2 * rb + hb) % 2
                                def bld(e, bank=bank, slot=slot, ic=ic, hf=hf, par=par):
                                    return e.matmul(
                                        pb[bank][:, :],
                                        wt_sb[:, 4096 * slot + 128 * ic:4096 * slot + 128 * ic + 128],
                                        xnb[:, 16384 * par + 1024 * (ic % 16) + 512 * hf:
                                            16384 * par + 1024 * (ic % 16) + 512 * hf + 512],
                                        start=(ic == 0), stop=(ic == NC - 1))
                                ww = w0 if ic == 0 else (w16 if ic == 16 else ())
                                v_mm = em.op("tensor", bld, wait=ww,
                                             inc=("pe" if ic == NC - 1 else None))
                            xnb_user[(2 * rb) % 2] = v_mm
                            xnb_user[(2 * rb + 1) % 2] = v_mm
                            sslot = st_i % 2
                            ev_w = [("pe", v_mm)]
                            if stage_free[sslot] is not None:
                                ev_w.append(("s_sync", stage_free[sslot]))
                            def ev(e, bank=bank, sslot=sslot, hf=hf, w=w, oc=oc):
                                return e.activation(
                                    kv_stage[:, 1024 * sslot + 512 * hf:1024 * sslot + 512 * hf + 512],
                                    pb[bank][:, :], AF.Identity,
                                    bias=bias_sb[:, 4 * w + oc:4 * w + oc + 1], scale=1.0)
                            v_ev = em.op("scalar", ev, wait=ev_w, inc="act")
                            pb_free[bank] = v_ev
                            if hf == 1:
                                v_st = em.op("sync", lambda e, w=w, oc=oc, rb=rb, sslot=sslot: e.dma_start(
                                    out=wdst[w][128 * oc:128 * oc + 128, 1024 * rb:1024 * rb + 1024],
                                    in_=kv_stage[:, 1024 * sslot:1024 * sslot + 1024]),
                                    wait=[("act", v_ev)], inc="s_sync", inc_k=16)
                                stage_free[sslot] = v_st
                                st_i += 1
                em.drain_and_barrier(block)

        with ExitStack() as _st4:
            qbuf = _st4.enter_context(nc.sbuf_tensor("qbuf", [128, 2 * 4096], BF16, allow_name_mangling=True))
            kbuf = _st4.enter_context(nc.sbuf_tensor("kbuf", [128, 2 * 4096], BF16, allow_name_mangling=True))
            vbuf = _st4.enter_context(nc.sbuf_tensor("vbuf", [128, 2 * 4096], BF16, allow_name_mangling=True))
            pP = _st4.enter_context(nc.sbuf_tensor("pP", [128, 2 * 4 * 4096], BF16, allow_name_mangling=True))
            vtm = _st4.enter_context(nc.sbuf_tensor("vtm", [128, 2 * 4096], BF16, allow_name_mangling=True))
            ptst = _st4.enter_context(nc.sbuf_tensor("ptst", [128, 2 * 512], BF16, allow_name_mangling=True))
            lparts = _st4.enter_context(nc.sbuf_tensor("lparts", [128, 2 * 48], F32, allow_name_mangling=True))
            lsum = _st4.enter_context(nc.sbuf_tensor("lsum", [128, 16], F32, allow_name_mangling=True))
            att_stage = _st4.enter_context(nc.sbuf_tensor("att_stage", [128, 2 * 512], BF16, allow_name_mangling=True))
            pb_0 = _st4.enter_context(nc.psum_tensor("pb0", [128, 512], F32, allow_name_mangling=True))
            pb_1 = _st4.enter_context(nc.psum_tensor("pb1", [128, 512], F32, allow_name_mangling=True))
            pb_2 = _st4.enter_context(nc.psum_tensor("pb2", [128, 512], F32, allow_name_mangling=True))
            pb_3 = _st4.enter_context(nc.psum_tensor("pb3", [128, 512], F32, allow_name_mangling=True))
            pbo_0 = _st4.enter_context(nc.psum_tensor("pbo0", [128, 512], F32, allow_name_mangling=True))
            pbo_1 = _st4.enter_context(nc.psum_tensor("pbo1", [128, 512], F32, allow_name_mangling=True))
            pbf_0 = _st4.enter_context(nc.psum_tensor("pbf0", [128, 1024], BF16, allow_name_mangling=True))
            pbf_1 = _st4.enter_context(nc.psum_tensor("pbf1", [128, 1024], BF16, allow_name_mangling=True))
            pbs = [pb_0, pb_1, pb_2, pb_3]
            pbo = [pbo_0, pbo_1]
            pbf = [pbf_0, pbf_1]
            with nc.Block("p3") as block:
                st = {
                    "sbank_free": [None] * 4, "sbank_i": 0,
                    "pbf_free": [None, None], "pbo_free": [None, None],
                    "ptst_free": [None, None],
                    "pP_user": [None, None],
                    "lp_done": [None, None],
                    "buf_user": [None, None],
                    "stage_free": [None, None],
                    "loads": {},
                }

                def emit_loads(u):
                    hh, s = u // 2, u % 2
                    slot = u % 2
                    lw = []
                    if st["buf_user"][slot] is not None:
                        lw.append(("pe", st["buf_user"][slot]))
                    v_q = em.op("sync", lambda e, hh=hh, s=s, slot=slot: e.dma_start(
                        out=qbuf[:, 4096 * slot:4096 * slot + 4096],
                        in_=q_scr[128 * hh:128 * hh + 128, 4096 * s:4096 * s + 4096]),
                        wait=lw, inc="s_sync", inc_k=16)
                    v_k = em.op("sync", lambda e, hh=hh, s=s, slot=slot: e.dma_start(
                        out=kbuf[:, 4096 * slot:4096 * slot + 4096],
                        in_=k_scr[128 * hh:128 * hh + 128, 4096 * s:4096 * s + 4096]),
                        wait=lw, inc="s_sync", inc_k=16)
                    v_v = em.op("scalar", lambda e, hh=hh, s=s, slot=slot: e.dma_start(
                        out=vbuf[:, 4096 * slot:4096 * slot + 4096],
                        in_=v_scr[128 * hh:128 * hh + 128, 4096 * s:4096 * s + 4096]),
                        wait=lw, inc="s_actq", inc_k=16)
                    st["loads"][u] = (v_q, v_k, v_v)

                def emit_vt(u):
                    slot = u % 2
                    v_v = st["loads"][u][2]
                    v_vt_ev = None
                    vt_evac = {0: None, 1: None}
                    for f in range(4):
                        bank = pbf[f % 2]
                        tw = [("s_actq", v_v)]
                        if st["buf_user"][slot] is not None:
                            tw.append(("pe", st["buf_user"][slot]))
                        if st["pbf_free"][f % 2] is not None:
                            tw.append(("dve", st["pbf_free"][f % 2]))
                        if vt_evac[f % 2] is not None:
                            tw.append(("act", vt_evac[f % 2]))
                        v_tp = None
                        for k in range(8):
                            c = 8 * f + k
                            def bld(e, bank=bank, k=k, c=c, slot=slot):
                                return e.transpose(
                                    bank[:, 128 * k:128 * k + 128],
                                    vbuf[:, 4096 * slot + 128 * c:4096 * slot + 128 * c + 128],
                                    ident_sb[:, :])
                            v_tp = em.op("tensor", bld, wait=(tw if k == 0 else ()),
                                         inc=("pe" if k == 7 else None))
                        def evv(e, bank=bank, f=f, slot=slot):
                            return e.copy(
                                vtm[:, 4096 * slot + 1024 * f:4096 * slot + 1024 * f + 1024],
                                bank[:, :])
                        v_vt_ev = em.op("scalar", evv, wait=[("pe", v_tp)], inc="act")
                        vt_evac[f % 2] = v_vt_ev
                        st["pbf_free"][f % 2] = None
                        st["pbf_vt"] = v_vt_ev
                    return v_vt_ev

                def emit_scores_softmax(ui):
                    u, qg = ui // 8, ui % 8
                    hh, s = u // 2, u % 2
                    slot, par = u % 2, ui % 2
                    sl = qg
                    prefix = 512 * (sl + 1)
                    v_q, v_k, v_v = st["loads"][u]
                    pbase = 16384 * par
                    lbase = 48 * par
                    sbase = 8 * par
                    lw = []
                    if st["lp_done"][par] is not None:
                        lw = [("act", st["lp_done"][par][0]), ("dve", st["lp_done"][par][1])]
                    v_lp0 = em.op("vector", lambda e, lbase=lbase: e.memset(
                        lparts[:, lbase:lbase + 48], 0.0), wait=lw, inc="dve")
                    ppw = [] if st["pP_user"][par] is None else [("pe", st["pP_user"][par])]
                    v_norm_done = [None] * 4
                    for qt in range(4):
                        for c in range(sl + 1):
                            bank = st["sbank_i"] % 4
                            st["sbank_i"] += 1
                            width = 512 if c < sl else 128 * (qt + 1)
                            mm_w = [("s_sync", v_k)]
                            if st["sbank_free"][bank] is not None:
                                mm_w.append(("act", st["sbank_free"][bank]))
                            def bld(e, bank=bank, qt=qt, c=c, width=width, slot=slot, qg=qg):
                                return e.matmul(
                                    pbs[bank][:, 0:width],
                                    qbuf[:, 4096 * slot + 512 * qg + 128 * qt:
                                         4096 * slot + 512 * qg + 128 * qt + 128],
                                    kbuf[:, 4096 * slot + 512 * c:
                                         4096 * slot + 512 * c + width],
                                    start=True, stop=True)
                            v_mm = em.op("tensor", bld, wait=mm_w, inc="pe")
                            pcol = pbase + 4096 * qt + 512 * c
                            if c < sl:
                                def ev(e, bank=bank, pcol=pcol, qt=qt, c=c, lbase=lbase):
                                    return e.activation(
                                        pP[:, pcol:pcol + 512], pbs[bank][:, 0:512],
                                        AF.Exp, scale=EXP_SCALE, bias=shift_sb[:, 0:1],
                                        accum_out=lparts[:, lbase + 12 * qt + c:lbase + 12 * qt + c + 1])
                                em.op("scalar", ev,
                                      wait=[("pe", v_mm), ("dve", v_lp0)] + ppw, inc="act")
                            else:
                                if qt > 0:
                                    def ev1(e, bank=bank, pcol=pcol, qt=qt, c=c, lbase=lbase):
                                        return e.activation(
                                            pP[:, pcol:pcol + 128 * qt],
                                            pbs[bank][:, 0:128 * qt],
                                            AF.Exp, scale=EXP_SCALE, bias=shift_sb[:, 0:1],
                                            accum_out=lparts[:, lbase + 12 * qt + c:lbase + 12 * qt + c + 1])
                                    em.op("scalar", ev1,
                                          wait=[("pe", v_mm), ("dve", v_lp0)] + ppw, inc="act")
                                def ev2(e, bank=bank, pcol=pcol, qt=qt):
                                    return e.activation(
                                        pP[:, pcol + 128 * qt:pcol + 128 * qt + 128],
                                        pbs[bank][:, 128 * qt:128 * qt + 128],
                                        AF.Exp, scale=EXP_SCALE, bias=shift_sb[:, 0:1])
                                v_ev2 = em.op("scalar", ev2,
                                              wait=[("pe", v_mm)] + ppw, inc="act")
                                def msk(e, pcol=pcol, qt=qt):
                                    return e.tensor_mul(
                                        pP[:, pcol + 128 * qt:pcol + 128 * qt + 128],
                                        pP[:, pcol + 128 * qt:pcol + 128 * qt + 128],
                                        tri_sb[:, :])
                                v_mk = em.op("vector", msk,
                                             wait=[("act", v_ev2), ("dve", v_lp0)], inc="dve")
                                def dsum(e, pcol=pcol, qt=qt, c=c, lbase=lbase):
                                    return e.reduce_sum(
                                        lparts[:, lbase + 12 * qt + c + 1:lbase + 12 * qt + c + 2],
                                        pP[:, pcol + 128 * qt:pcol + 128 * qt + 128],
                                        axis=AX.X)
                                em.op("vector", dsum, wait=[("dve", v_mk)], inc="dve")
                            st["sbank_free"][bank] = em.val("act")
                        wq = 512 * sl + 128 * (qt + 1)
                        if wq < prefix:
                            def z(e, qt=qt, wq=wq, prefix=prefix, pbase=pbase):
                                return e.memset(
                                    pP[:, pbase + 4096 * qt + wq:pbase + 4096 * qt + prefix], 0.0)
                            em.op("vector", z, wait=ppw, inc="dve")
                    act_all = em.val("act")
                    dve_all = em.val("dve")
                    for qt in range(4):
                        def lsb(e, qt=qt, sl=sl, lbase=lbase, sbase=sbase):
                            return e.reduce_sum(
                                lsum[:, sbase + qt:sbase + qt + 1],
                                lparts[:, lbase + 12 * qt:lbase + 12 * qt + sl + 2], axis=AX.X)
                        v_ls = em.op("vector", lsb,
                                     wait=[("act", act_all), ("dve", dve_all)], inc="dve")
                        v_rc = em.op("vector", lambda e, qt=qt, sbase=sbase: e.reciprocal(
                            lsum[:, sbase + 4 + qt:sbase + 5 + qt],
                            lsum[:, sbase + qt:sbase + qt + 1]),
                            wait=[("dve", v_ls)], inc="dve")
                        wqv = 512 * sl + 128 * (qt + 1)
                        def nrm(e, qt=qt, wqv=wqv, pbase=pbase, sbase=sbase):
                            return e.tensor_scalar(
                                pP[:, pbase + 4096 * qt:pbase + 4096 * qt + wqv],
                                pP[:, pbase + 4096 * qt:pbase + 4096 * qt + wqv],
                                lsum[:, sbase + 4 + qt:sbase + 5 + qt], None, ALU.mult)
                        v_norm_done[qt] = em.op("vector", nrm, wait=[("dve", v_rc)], inc="dve")
                    st["lp_done"][par] = (act_all, em.val("dve"))
                    return dict(ui=ui, u=u, qg=qg, hh=hh, s=s, slot=slot, par=par,
                                sl=sl, prefix=prefix, ntc=prefix // 128,
                                norm=v_norm_done)

                def emit_ptav(info, v_vt_ev_by_u):
                    ui, u, qg = info["ui"], info["u"], info["qg"]
                    slot, par = info["slot"], info["par"]
                    ntc = info["ntc"]
                    pbase = 16384 * par
                    obank = pbo[par]
                    v_vt = v_vt_ev_by_u[u]
                    v_ev_c = [None] * ntc
                    v_av = None
                    v_last_pt = None

                    def emit_pt(c):
                        nonlocal v_last_pt
                        bank = pbf[c % 2]
                        w0 = [("dve", info["norm"][3])]
                        if st["pbf_free"][c % 2] is not None:
                            w0.append(("dve", st["pbf_free"][c % 2]))
                        if c < 2 and st.get("pbf_vt") is not None:
                            w0.append(("act", st["pbf_vt"]))
                        v_tp = None
                        for qt in range(4):
                            def bld(e, bank=bank, qt=qt, c=c, pbase=pbase):
                                return e.transpose(
                                    bank[:, 128 * qt:128 * qt + 128],
                                    pP[:, pbase + 4096 * qt + 128 * c:pbase + 4096 * qt + 128 * c + 128],
                                    ident_sb[:, :])
                            v_tp = em.op("tensor", bld, wait=(w0 if qt == 0 else ()),
                                         inc=("pe" if qt == 3 else None))
                        v_last_pt = v_tp
                        pslot = c % 2
                        ev_w = [("pe", v_tp)]
                        if st["ptst_free"][pslot] is not None:
                            ev_w.append(("pe", st["ptst_free"][pslot]))
                        def evp(e, bank=bank, pslot=pslot):
                            return e.tensor_copy(
                                ptst[:, 512 * pslot:512 * pslot + 512], bank[:, 0:512])
                        v_ev = em.op("vector", evp, wait=ev_w, inc="dve")
                        st["pbf_free"][c % 2] = v_ev
                        v_ev_c[c] = v_ev

                    def emit_av(c):
                        nonlocal v_av
                        pslot = c % 2
                        av_w = [("dve", v_ev_c[c]), ("act", v_vt)]
                        if c == 0 and st["pbo_free"][par] is not None:
                            av_w.append(("dve", st["pbo_free"][par]))
                        def av(e, c=c, pslot=pslot, ntc=ntc, slot=slot):
                            return e.matmul(
                                obank[:, :],
                                vtm[:, 4096 * slot + 128 * c:4096 * slot + 128 * c + 128],
                                ptst[:, 512 * pslot:512 * pslot + 512],
                                start=(c == 0), stop=(c == ntc - 1))
                        v_av = em.op("tensor", av, wait=av_w, inc="pe")
                        st["ptst_free"][pslot] = v_av

                    for c in range(ntc):
                        emit_pt(c)
                        if c >= 1:
                            emit_av(c - 1)
                    emit_av(ntc - 1)
                    st["pP_user"][par] = v_last_pt

                    ev_w = [("pe", v_av)]
                    if st["stage_free"][par] is not None:
                        ev_w.append(("s_gp", st["stage_free"][par]))
                    def oev(e, obank=obank, sslot=par):
                        return e.tensor_copy(
                            att_stage[:, 512 * sslot:512 * sslot + 512], obank[:, :])
                    v_oev = em.op("vector", oev, wait=ev_w, inc="dve")
                    st["pbo_free"][par] = v_oev
                    r = 4 * info["s"] + qg // 2
                    colo = 512 * (qg % 2)
                    v_att_st = em.op("gpsimd", lambda e, r=r, hh=info["hh"], colo=colo,
                                     sslot=par: e.dma_start(
                        out=att_pub[512 * r + 128 * hh:512 * r + 128 * hh + 128,
                                    colo:colo + 512],
                        in_=att_stage[:, 512 * sslot:512 * sslot + 512]),
                        wait=[("dve", v_oev)], inc="s_gp", inc_k=16)
                    st["stage_free"][par] = v_att_st
                    if qg == 7:
                        st["buf_user"][slot] = v_av

                emit_loads(0)
                emit_loads(1)
                v_vt_by_u = {}
                infos = []
                for ui in range(64):
                    u, qg = ui // 8, ui % 8
                    if qg == 0:
                        v_vt_by_u[u] = emit_vt(u)
                    if qg == 1 and u + 1 < 8 and (u + 1) not in st["loads"]:
                        emit_loads(u + 1)
                    infos.append(emit_scores_softmax(ui))
                    if ui > 0:
                        emit_ptav(infos[ui - 1], v_vt_by_u)
                emit_ptav(infos[63], v_vt_by_u)

                def a2a(e):
                    with allow_ici_all_to_all():
                        return e.collective_compute(
                            "AllToAll", ALU.bypass, replica_groups=[list(range(8))],
                            ins=[att_pub.opt()], outs=[att_own.opt()])
                em.op("gpsimd", a2a, wait=[("s_gp", em.val("s_gp"))],
                      inc="s_cc", inc_k=1)
                em.drain_and_barrier(block)

        with nc.sbuf_tensor("outT", [128, NC * 1024], BF16, allow_name_mangling=True) as outT:
            with ExitStack() as _st5:
                att_full = _st5.enter_context(nc.sbuf_tensor("att_full", [128, NC * 1024], BF16, allow_name_mangling=True))
                wbuf = _st5.enter_context(nc.sbuf_tensor("wbuf", [128, 2 * 4096], BF16, allow_name_mangling=True))
                pb_0 = _st5.enter_context(nc.psum_tensor("pb0", [128, 512], F32, allow_name_mangling=True))
                pb_1 = _st5.enter_context(nc.psum_tensor("pb1", [128, 512], F32, allow_name_mangling=True))
                pb_2 = _st5.enter_context(nc.psum_tensor("pb2", [128, 512], F32, allow_name_mangling=True))
                pb_3 = _st5.enter_context(nc.psum_tensor("pb3", [128, 512], F32, allow_name_mangling=True))
                pb = [pb_0, pb_1, pb_2, pb_3]
                with nc.Block("p4") as block:
                    v_att_ld = None
                    for ic in range(NC):
                        v_att_ld = em.op("sync", lambda e, ic=ic: e.dma_start(
                            out=att_full[:, 1024 * ic:1024 * ic + 1024],
                            in_=att_own[128 * ic:128 * ic + 128, :]),
                            wait=([("s_cc", 2)] if ic == 0 else ()),
                            inc="s_sync", inc_k=16)
                    wb_user = [None, None]
                    pb_free = [None] * 4
                    bank_i = 0
                    for oc in range(NC):
                        wslot = oc % 2
                        ww = []
                        if wb_user[wslot] is not None:
                            ww.append(("pe", wb_user[wslot]))
                        v_w = em.op("gpsimd", lambda e, oc=oc, wslot=wslot: e.dma_start(
                            out=wbuf[:, 4096 * wslot:4096 * wslot + 4096],
                            in_=wo4[oc].rearrange("p c o -> p (c o)")),
                            wait=ww, inc="s_gp", inc_k=16)
                        for hf in range(2):
                            bank = bank_i % 4
                            bank_i += 1
                            mm_w = [("s_gp", v_w), ("s_sync", v_att_ld)]
                            if pb_free[bank] is not None:
                                mm_w.append(("act", pb_free[bank]))
                            v_mm = None
                            for ic in range(NC):
                                def bld(e, bank=bank, wslot=wslot, ic=ic, hf=hf):
                                    return e.matmul(
                                        pb[bank][:, :],
                                        wbuf[:, 4096 * wslot + 128 * ic:4096 * wslot + 128 * ic + 128],
                                        att_full[:, 1024 * ic + 512 * hf:1024 * ic + 512 * hf + 512],
                                        start=(ic == 0), stop=(ic == NC - 1))
                                v_mm = em.op("tensor", bld,
                                             wait=(mm_w if ic == 0 else ()),
                                             inc=("pe" if ic == NC - 1 else None))
                            wb_user[wslot] = v_mm
                            def ev(e, bank=bank, oc=oc, hf=hf):
                                return e.activation(
                                    outT[:, 1024 * oc + 512 * hf:1024 * oc + 512 * hf + 512],
                                    pb[bank][:, :], AF.Identity,
                                    bias=bias_sb[:, 3 * NH + oc:3 * NH + oc + 1],
                                    scale=1.0)
                            v_ev = em.op("scalar", ev, wait=[("pe", v_mm)], inc="act")
                            pb_free[bank] = v_ev
                    em.drain_and_barrier(block)

            with ExitStack() as _st6:
                x_tm = _st6.enter_context(nc.sbuf_tensor("x_tm", [128, 2 * 4096], F32, allow_name_mangling=True))
                x1_t = _st6.enter_context(nc.sbuf_tensor("x1_t", [128, 4096], F32, allow_name_mangling=True))
                xn2_stage = _st6.enter_context(nc.sbuf_tensor("xn2_stage", [128, 4096], BF16, allow_name_mangling=True))
                xn2_fm = _st6.enter_context(nc.sbuf_tensor("xn2_fm", [128, NC * 1024], BF16, allow_name_mangling=True))
                l4 = _st6.enter_context(nc.sbuf_tensor("l4", [128, NT * 4], F32, allow_name_mangling=True))
                gsc = _st6.enter_context(nc.sbuf_tensor("gsc", [128, NT * 8], F32, allow_name_mangling=True))
                g4 = _st6.enter_context(nc.sbuf_tensor("g4", [128, NT * 4], F32, allow_name_mangling=True))
                gt_sb = _st6.enter_context(nc.sbuf_tensor("gt_sb", [4, 1024], BF16, allow_name_mangling=True))
                gb_sb = _st6.enter_context(nc.sbuf_tensor("gb_sb", [64, 1024], BF16, allow_name_mangling=True))
                gmid = _st6.enter_context(nc.sbuf_tensor("gmid", [64, 1024], BF16, allow_name_mangling=True))
                pb_0 = _st6.enter_context(nc.psum_tensor("pb0", [128, 512], F32, allow_name_mangling=True))
                pb_1 = _st6.enter_context(nc.psum_tensor("pb1", [128, 512], F32, allow_name_mangling=True))
                pb_2 = _st6.enter_context(nc.psum_tensor("pb2", [128, 512], F32, allow_name_mangling=True))
                pb_3 = _st6.enter_context(nc.psum_tensor("pb3", [128, 512], F32, allow_name_mangling=True))
                pbm = _st6.enter_context(nc.psum_tensor("pbm", [64, 1024], F32, allow_name_mangling=True))
                pbf_0 = _st6.enter_context(nc.psum_tensor("pbf0", [128, 1024], BF16, allow_name_mangling=True))
                pbf_1 = _st6.enter_context(nc.psum_tensor("pbf1", [128, 1024], BF16, allow_name_mangling=True))
                pb = [pb_0, pb_1, pb_2, pb_3]
                pbf = [pbf_0, pbf_1]
                with nc.Block("p56") as block:
                    pbf_free = {0: None, 1: None}
                    x_done = [None, None]
                    x1_free = None
                    v_xn2_done = None
                    v_tp_prev = None
                    for tau in range(NT):
                        half = tau % 2
                        lw = []
                        if x_done[half] is not None:
                            lw.append(("dve", x_done[half]))
                        v_ld = em.op("sync", lambda e, tau=tau, half=half: e.dma_start(
                            out=x_tm[:, 4096 * half:4096 * half + 4096],
                            in_=x_in[128 * tau:128 * tau + 128, :]),
                            wait=lw, inc="s_sync", inc_k=16)
                        v_add = None
                        for f in range(4):
                            bank = pbf[f % 2]
                            tw = [("act", em.val("act"))]
                            if pbf_free[f % 2] is not None:
                                tw.append(("dve", pbf_free[f % 2]))
                            v_tp = None
                            for k in range(8):
                                oc = 8 * f + k
                                def bld(e, bank=bank, k=k, oc=oc, tau=tau):
                                    return e.transpose(
                                        bank[:, 128 * k:128 * k + 128],
                                        outT[:, 1024 * oc + 128 * tau:1024 * oc + 128 * tau + 128],
                                        ident_sb[:, :])
                                v_tp = em.op("tensor", bld, wait=(tw if k == 0 else ()),
                                             inc=("pe" if k == 7 else None))
                            aw = [("pe", v_tp), ("s_sync", v_ld)]
                            if x1_free is not None and f == 0:
                                aw += [("act", x1_free[0]), ("s_gp", x1_free[1]),
                                       ("dve", x1_free[2])]
                            def addb(e, bank=bank, f=f, half=half):
                                return e.tensor_add(
                                    x1_t[:, 1024 * f:1024 * f + 1024],
                                    x_tm[:, 4096 * half + 1024 * f:4096 * half + 1024 * f + 1024],
                                    bank[:, :])
                            v_add = em.op("vector", addb, wait=aw, inc="dve")
                            pbf_free[f % 2] = v_add
                        x_done[half] = v_add
                        v_st = em.op("gpsimd", lambda e, tau=tau: e.dma_start(
                            out=x1_dram[128 * tau:128 * tau + 128, :], in_=x1_t[:, :]),
                            wait=[("dve", v_add)], inc="s_gp", inc_k=16)
                        em.wait("scalar", [("dve", v_add)])
                        em.wait("vector", [("dve", v_add)])
                        sq_wait = ("pe", v_tp_prev) if v_tp_prev is not None else None
                        v_xn2 = emit_ln_tile(x1_t, xn2_stage, 6 * (tau % 2), sq_wait)
                        x1_free = (em.val("act"), v_st, v_xn2)
                        v_xn2_done = emit_tile_to_fm(xn2_stage, xn2_fm, tau, pbf,
                                                     pbf_free, ("dve", v_xn2))
                        v_tp_prev = em.val("pe")

                    v_gb_evs = []
                    for tau in range(NT):
                        bank = pb[tau % 2]
                        mm_w = [("dve", v_xn2_done)]
                        if tau >= 2:
                            mm_w.append(("dve", v_gb_evs[tau - 2][0]))
                        v_mm = None
                        for ic in range(NC):
                            def bld(e, bank=bank, ic=ic, tau=tau):
                                return e.matmul(
                                    bank[:, 0:4],
                                    xn2_fm[:, 1024 * ic + 128 * tau:1024 * ic + 128 * tau + 128],
                                    wr_sb[:, 4 * ic:4 * ic + 4],
                                    start=(ic == 0), stop=(ic == NC - 1))
                            v_mm = em.op("tensor", bld, wait=(mm_w if ic == 0 else ()),
                                         inc=("pe" if ic == NC - 1 else None))
                        lts = l4[:, 4 * tau:4 * tau + 4]
                        v_l = em.op("vector", lambda e, bank=bank, lts=lts: e.tensor_copy(
                            lts, bank[:, 0:4]), wait=[("pe", v_mm)], inc="dve")
                        m1 = gsc[:, 8 * tau + 0:8 * tau + 1]
                        nm1 = gsc[:, 8 * tau + 1:8 * tau + 2]
                        eq = gsc[:, 8 * tau + 2:8 * tau + 6]
                        m2 = gsc[:, 8 * tau + 6:8 * tau + 7]
                        z = gsc[:, 8 * tau + 7:8 * tau + 8]
                        ets = g4[:, 4 * tau:4 * tau + 4]
                        v_m1 = em.op("vector", lambda e, m1=m1, lts=lts: e.reduce_max(
                            m1, lts, axis=AX.X), wait=[("dve", v_l)], inc="dve")
                        v_nm1 = em.op("vector", lambda e, nm1=nm1, lts=lts: e.reduce_max(
                            nm1, lts, axis=AX.X, negate=True),
                            wait=[("dve", v_l)], inc="dve")
                        v_eq = em.op("vector", lambda e, eq=eq, lts=lts, m1=m1: e.tensor_scalar(
                            eq, lts, m1, None, ALU.is_ge),
                            wait=[("dve", v_m1)], inc="dve")
                        v_lm = em.op("vector", lambda e, eq=eq, lts=lts: e.scalar_tensor_tensor(
                            eq, eq, -1e30, lts, ALU.mult, ALU.add),
                            wait=[("dve", v_eq)], inc="dve")
                        v_m2 = em.op("vector", lambda e, m2=m2, eq=eq: e.reduce_max(
                            m2, eq, axis=AX.X), wait=[("dve", v_lm)], inc="dve")
                        v_e = em.op("scalar", lambda e, ets=ets, lts=lts, nm1=nm1: e.activation(
                            ets, lts, AF.Exp, bias=nm1, scale=1.0),
                            wait=[("dve", v_nm1)], inc="act")
                        v_ge = em.op("vector", lambda e, eq=eq, lts=lts, m2=m2: e.tensor_scalar(
                            eq, lts, m2, None, ALU.is_ge),
                            wait=[("dve", v_m2)], inc="dve")
                        v_g = em.op("vector", lambda e, ets=ets, eq=eq: e.tensor_mul(
                            ets, ets, eq), wait=[("dve", v_ge), ("act", v_e)], inc="dve")
                        v_z = em.op("vector", lambda e, z=z, ets=ets: e.reduce_sum(
                            z, ets, axis=AX.X), wait=[("dve", v_g)], inc="dve")
                        v_rz = em.op("vector", lambda e, m1=m1, z=z: e.reciprocal(m1, z),
                                     wait=[("dve", v_z)], inc="dve")
                        v_gn = em.op("vector", lambda e, ets=ets, m1=m1: e.tensor_scalar(
                            ets, ets, m1, 4.0, ALU.mult, ALU.mult),
                            wait=[("dve", v_rz)], inc="dve")
                        v_gt = em.op("tensor", lambda e, ets=ets: e.transpose(
                            pb[2][0:4, 0:128], ets, ident32_sb[:, :]),
                            wait=[("dve", v_gn)], inc="pe")
                        v_gte = em.op("vector", lambda e, tau=tau: e.tensor_copy(
                            gt_sb[:, 128 * tau:128 * tau + 128], pb[2][0:4, 0:128]),
                            wait=[("pe", v_gt)], inc="dve")
                        v_gb = em.op("tensor", lambda e, tau=tau: e.matmul(
                            pbm[0:64, 512:640], e4_sb[:, :],
                            gt_sb[:, 128 * tau:128 * tau + 128],
                            start=True, stop=True),
                            wait=[("dve", v_gte)], inc="pe")
                        v_gbe = em.op("vector", lambda e, tau=tau: e.tensor_copy(
                            gb_sb[:, 128 * tau:128 * tau + 128], pbm[0:64, 512:640]),
                            wait=[("pe", v_gb)], inc="dve")
                        v_gb_evs.append((v_gbe, v_gb))
                    v_gb_all = em.val("dve")

                    v_mid_ev = []
                    for hf in range(2):
                        v_mm = None
                        for ic in range(NC):
                            w0 = [("dve", v_gb_all)] if ic == 0 else ()
                            def bld(e, ic=ic, hf=hf):
                                return e.matmul(
                                    pbm[0:64, 512 * hf:512 * hf + 512],
                                    a_sb[:, 64 * ic:64 * ic + 64],
                                    xn2_fm[:, 1024 * ic + 512 * hf:1024 * ic + 512 * hf + 512],
                                    start=(ic == 0), stop=(ic == NC - 1))
                            v_mm = em.op("tensor", bld, wait=w0,
                                         inc=("pe" if ic == NC - 1 else None))
                        def gm(e, hf=hf):
                            return e.tensor_mul(gmid[:, 512 * hf:512 * hf + 512],
                                                pbm[0:64, 512 * hf:512 * hf + 512],
                                                gb_sb[:, 512 * hf:512 * hf + 512])
                        v_gm = em.op("vector", gm, wait=[("pe", v_mm)], inc="dve")
                        v_mid_ev.append(v_gm)

                    v_p5_pe = v_tp_prev
                    pb_free2 = [None] * 4
                    bank_i = 0
                    for oc in range(NC):
                        for hf in range(2):
                            bank = bank_i % 4
                            bank_i += 1
                            mw = [("dve", v_mid_ev[hf])]
                            if pb_free2[bank] is not None:
                                mw.append(("act", pb_free2[bank]))
                            def bld(e, bank=bank, oc=oc, hf=hf):
                                return e.matmul(pb[bank][:, :],
                                                bmat_sb[:, 128 * oc:128 * oc + 128],
                                                gmid[:, 512 * hf:512 * hf + 512],
                                                start=True, stop=True)
                            v_mm = em.op("tensor", bld, wait=mw, inc="pe")
                            def ev(e, bank=bank, oc=oc, hf=hf):
                                return e.activation(
                                    outT[:, 1024 * oc + 512 * hf:1024 * oc + 512 * hf + 512],
                                    pb[bank][:, :], AF.Identity, scale=1.0)
                            v_dev = em.op("scalar", ev, wait=[("pe", v_mm)], inc="act")
                            pb_free2[bank] = v_dev
                    em.drain_and_barrier(block)

            with ExitStack() as _st7:
                x1b = _st7.enter_context(nc.sbuf_tensor("x1b", [128, 2 * 4096], F32, allow_name_mangling=True))
                x2_t = _st7.enter_context(nc.sbuf_tensor("x2_t", [128, 2 * 4096], F32, allow_name_mangling=True))
                pbf_0 = _st7.enter_context(nc.psum_tensor("pbf0", [128, 1024], BF16, allow_name_mangling=True))
                pbf_1 = _st7.enter_context(nc.psum_tensor("pbf1", [128, 1024], BF16, allow_name_mangling=True))
                pbf = [pbf_0, pbf_1]
                with nc.Block("p7") as block:
                    pbf_free = {0: None, 1: None}
                    x1_done = [None, None]
                    out_vals = []
                    for tau in range(NT):
                        half = tau % 2
                        lw = []
                        if x1_done[half] is not None:
                            lw.append(("dve", x1_done[half]))
                        v_ld = em.op("sync", lambda e, tau=tau, half=half: e.dma_start(
                            out=x1b[:, 4096 * half:4096 * half + 4096],
                            in_=x1_dram[128 * tau:128 * tau + 128, :]),
                            wait=lw, inc="s_sync", inc_k=16)
                        v_add = None
                        for f in range(4):
                            bank = pbf[f % 2]
                            tw = [("act", em.val("act"))]
                            if pbf_free[f % 2] is not None:
                                tw.append(("dve", pbf_free[f % 2]))
                            v_tp = None
                            for k in range(8):
                                oc = 8 * f + k
                                def bld(e, bank=bank, k=k, oc=oc, tau=tau):
                                    return e.transpose(
                                        bank[:, 128 * k:128 * k + 128],
                                        outT[:, 1024 * oc + 128 * tau:1024 * oc + 128 * tau + 128],
                                        ident_sb[:, :])
                                v_tp = em.op("tensor", bld, wait=(tw if k == 0 else ()),
                                             inc=("pe" if k == 7 else None))
                            aw = [("pe", v_tp), ("s_sync", v_ld)]
                            if len(out_vals) >= 2 and f == 0:
                                aw.append(("s_gp", out_vals[-2]))
                            def addb(e, bank=bank, f=f, half=half):
                                return e.tensor_add(
                                    x2_t[:, 4096 * half + 1024 * f:4096 * half + 1024 * f + 1024],
                                    x1b[:, 4096 * half + 1024 * f:4096 * half + 1024 * f + 1024],
                                    bank[:, :])
                            v_add = em.op("vector", addb, wait=aw, inc="dve")
                            pbf_free[f % 2] = v_add
                        x1_done[half] = v_add
                        v_st = em.op("gpsimd", lambda e, tau=tau, half=half: e.dma_start(
                            out=y_out[128 * tau:128 * tau + 128, :],
                            in_=x2_t[:, 4096 * half:4096 * half + 4096]),
                            wait=[("dve", v_add)], inc="s_gp", inc_k=16)
                        out_vals.append(v_st)
                    for t, (dst, src) in taps.items():
                        em.op("gpsimd", lambda e, dst=dst, src=src: e.dma_start(
                            out=dst[:, :], in_=src[:, :]), inc="s_gp", inc_k=16)
                    em.wait("gpsimd", [("s_gp", em.val("s_gp"))])
                    em.wait("sync", [("s_gp", em.val("s_gp"))])
                    em.flush(block)

    return nc


def _host_prepare(inputs):
    x = np.asarray(inputs["L0_attn_x"], dtype=np.float32).reshape(8192, D)
    wt = {}
    for nm, key in (("wq", "L0_attn_Wq"), ("wk", "L0_attn_Wk"), ("wv", "L0_attn_Wv")):
        W = np.asarray(inputs[key], dtype=np.float32)
        wt[nm] = W.T.reshape(NC, 128, NC, 128)
    Wo = np.asarray(inputs["L0_attn_Wo"], dtype=np.float32)
    wo4 = np.ascontiguousarray(Wo.T.reshape(NC, 128, NC, 128).transpose(2, 1, 0, 3))
    bq = np.asarray(inputs["L0_attn_bq"], dtype=np.float32).reshape(NC, 128)
    bk = np.asarray(inputs["L0_attn_bk"], dtype=np.float32).reshape(NC, 128)
    bv = np.asarray(inputs["L0_attn_bv"], dtype=np.float32).reshape(NC, 128)
    bo = np.asarray(inputs["L0_attn_bo"], dtype=np.float32)
    bo4 = np.ascontiguousarray(bo.reshape(NC, 128).T)
    A = np.asarray(inputs["L0_ffn_A"], dtype=np.float32)
    B = np.asarray(inputs["L0_ffn_B"], dtype=np.float32)
    Wr = np.asarray(inputs["L0_ffn_Wr"], dtype=np.float32)
    a_allt = A.transpose(2, 0, 1).reshape(D, 64)
    a4 = np.ascontiguousarray(
        a_allt.reshape(NC, 128, 64).transpose(1, 0, 2)).reshape(128, NC * 64)
    bt = np.ascontiguousarray(B.transpose(0, 2, 1).reshape(64, D))
    wr4 = np.ascontiguousarray(
        Wr.T.reshape(NC, 128, 4).transpose(1, 0, 2)).reshape(128, NC * 4)
    shared = dict(a4=a4, bt=bt, wr4=wr4, wo4=wo4, bo4=bo4)
    in_maps = []
    for j in range(8):
        m = dict(shared)
        m["x_own"] = np.ascontiguousarray(x[1024 * j:1024 * (j + 1)])
        hc = slice(4 * j, 4 * j + 4)
        for nm in ("wq", "wk", "wv"):
            m[nm + "4"] = np.ascontiguousarray(
                wt[nm][:, :, hc, :].transpose(2, 1, 0, 3))
        m["bqkv"] = np.ascontiguousarray(
            np.concatenate([bq[hc].T, bk[hc].T, bv[hc].T], axis=1))
        in_maps.append(m)
    return in_maps


def kernel(**inputs):
    in_maps = _host_prepare(inputs)
    nc = build_core()
    res = run_bass_kernel_spmd(nc, in_maps, core_ids=list(range(8)))
    out = np.empty((8192, D), dtype=np.float32)
    for j in range(8):
        out[1024 * j:1024 * (j + 1)] = res.results[j]["y"]
    return out.reshape(2, 4096, D)


# version 04
# speedup vs baseline: 1.2156x; 1.0042x over previous
from contextlib import ExitStack

import numpy as np
import ml_dtypes
import concourse.bass as bass
import concourse.mybir as mybir
from concourse.bass_utils import run_bass_kernel_spmd
from concourse.aot_env import allow_ici_all_to_all

F32 = mybir.dt.float32
BF16 = mybir.dt.bfloat16
AF = mybir.ActivationFunctionType
ALU = mybir.AluOpType
AX = mybir.AxisListType

D = 4096
NC = 32
NH = 4
NRB = 8
NT = 8
EXP_SCALE = 1.0 / np.sqrt(128.0)
EXP_SHIFT = -12.0
EPS = 1e-5

ENGINES = ("tensor", "vector", "scalar", "sync", "gpsimd")


class Em:

    def __init__(self, nc):
        self.nc = nc
        self.ops = {e: [] for e in ENGINES}
        self.sems = {}

    def add_sem(self, name, handle):
        self.sems[name] = [handle, 0]

    def val(self, name):
        return self.sems[name][1]

    def op(self, eng, build, wait=(), inc=None, inc_k=1):
        for (w, v) in wait:
            assert v <= self.sems[w][1], (w, v, self.sems[w][1])
        targets = [(self.sems[w][0], v) for (w, v) in wait if v > 0]
        handle = None
        if inc is not None:
            self.sems[inc][1] += inc_k
            handle = self.sems[inc][0]

        def f(e, targets=targets, build=build, handle=handle, inc_k=inc_k):
            for (h, v) in targets:
                e.wait_ge(h, v)
            ins = build(e)
            if handle is not None:
                ins.then_inc(handle, inc_k)

        self.ops[eng].append(f)
        return self.sems[inc][1] if inc is not None else None

    def wait(self, eng, waits):
        for (w, v) in waits:
            assert v <= self.sems[w][1], (w, v, self.sems[w][1])
        targets = [(self.sems[w][0], v) for (w, v) in waits if v > 0]

        def f(e, targets=targets):
            for (h, v) in targets:
                e.wait_ge(h, v)

        self.ops[eng].append(f)

    def flush(self, block):
        for e in ENGINES:
            if self.ops[e]:
                fns = self.ops[e]
                getattr(block, e)(lambda engine, fns=fns: [f(engine) for f in fns])
        self.ops = {e: [] for e in ENGINES}

    def drain_and_barrier(self, block):
        self.wait("sync", [("s_sync", self.val("s_sync"))])
        self.wait("gpsimd", [("s_gp", self.val("s_gp"))])
        self.wait("scalar", [("s_actq", self.val("s_actq")), ("act", self.val("act"))])
        self.wait("vector", [("dve", self.val("dve"))])
        self.wait("tensor", [("pe", self.val("pe"))])
        bar = self.sems["bar"]
        bar[1] += 5
        target = bar[1]
        for e in ENGINES:
            def f(eng, h=bar[0], t=target):
                eng.nop().then_inc(h, 1)
                eng.wait_ge(h, t)
            self.ops[e].append(f)
        self.flush(block)


def build_core(debug_taps=()):
    nc = bass.Bass("TRN2", target_bir_lowering=False)

    x_in = nc.dram_tensor("x_own", [NT * 128, D], F32, kind="ExternalInput").ap()
    wq4 = nc.dram_tensor("wq4", [NH, 128, NC, 128], F32, kind="ExternalInput").ap()
    wk4 = nc.dram_tensor("wk4", [NH, 128, NC, 128], F32, kind="ExternalInput").ap()
    wv4 = nc.dram_tensor("wv4", [NH, 128, NC, 128], F32, kind="ExternalInput").ap()
    wo4 = nc.dram_tensor("wo4", [NC, 128, NC, 128], F32, kind="ExternalInput").ap()
    bqkv = nc.dram_tensor("bqkv", [128, 3 * NH], F32, kind="ExternalInput").ap()
    bo4 = nc.dram_tensor("bo4", [128, NC], F32, kind="ExternalInput").ap()
    a4 = nc.dram_tensor("a4", [128, NC * 64], F32, kind="ExternalInput").ap()
    bt_in = nc.dram_tensor("bt", [64, D], F32, kind="ExternalInput").ap()
    wr4 = nc.dram_tensor("wr4", [128, NC * 4], F32, kind="ExternalInput").ap()
    y_out = nc.dram_tensor("y", [NT * 128, D], F32, kind="ExternalOutput").ap()

    xn_pub = nc.dram_tensor("xn_pub", [D, 1024], BF16).ap()
    xn_gath = [nc.dram_tensor("xn_gath%d" % g, [8 * 1024, 1024], BF16).ap()
               for g in range(4)]
    q_scr = nc.dram_tensor("q_scr", [NH * 128, 8192], BF16).ap()
    k_scr = nc.dram_tensor("k_scr", [NH * 128, 8192], BF16).ap()
    v_scr = nc.dram_tensor("v_scr", [NH * 128, 8192], BF16).ap()
    att_pub = nc.dram_tensor("att_pub", [D, 1024], BF16).ap()
    att_own = nc.dram_tensor("att_own", [D, 1024], BF16).ap()
    x1_dram = nc.dram_tensor("x1_scr", [NT * 128, D], F32).ap()

    ident_bf = nc.inline_tensor(np.eye(128, dtype=ml_dtypes.bfloat16), "c_ident_bf")
    ident_f32 = nc.inline_tensor(np.eye(128, dtype=np.float32), "c_ident_f32")
    qi = np.arange(128)[:, None]
    ti = np.arange(128)[None, :]
    tri01 = nc.inline_tensor((ti <= qi).astype(ml_dtypes.bfloat16), "c_tri01")
    e4c = nc.inline_tensor(
        np.repeat(np.eye(4, dtype=ml_dtypes.bfloat16), 16, axis=1), "c_e4")
    shiftc = nc.inline_tensor(
        np.full((128, 1), EXP_SHIFT, dtype=np.float32), "c_shift")

    taps = {}
    for t in debug_taps:
        src = {"q_scr": q_scr, "k_scr": k_scr,
               "v_scr": v_scr, "att_pub": att_pub, "att_own": att_own,
               "x1": x1_dram}[t]
        shp = list(src.tensor.shape)
        taps[t] = (nc.dram_tensor("tap_" + t, shp, src.tensor.dtype,
                                  kind="ExternalOutput").ap(), src)

    with ExitStack() as _st1:
        ident_sb = _st1.enter_context(nc.sbuf_tensor("ident_sb", [128, 128], BF16, allow_name_mangling=True))
        ident32_sb = _st1.enter_context(nc.sbuf_tensor("ident32_sb", [128, 128], F32, allow_name_mangling=True))
        tri_sb = _st1.enter_context(nc.sbuf_tensor("tri_sb", [128, 128], BF16, allow_name_mangling=True))
        e4_sb = _st1.enter_context(nc.sbuf_tensor("e4_sb", [4, 64], BF16, allow_name_mangling=True))
        bias_sb = _st1.enter_context(nc.sbuf_tensor("bias_sb", [128, 3 * NH + NC], F32, allow_name_mangling=True))
        wr_sb = _st1.enter_context(nc.sbuf_tensor("wr_sb", [128, NC * 4], BF16, allow_name_mangling=True))
        a_sb = _st1.enter_context(nc.sbuf_tensor("a_sb", [128, NC * 64], BF16, allow_name_mangling=True))
        bmat_sb = _st1.enter_context(nc.sbuf_tensor("bmat_sb", [64, D], BF16, allow_name_mangling=True))
        lnst = _st1.enter_context(nc.sbuf_tensor("lnst", [128, 16], F32, allow_name_mangling=True))
        shift_sb = _st1.enter_context(nc.sbuf_tensor("shift_sb", [128, 1], F32, allow_name_mangling=True))
        h_pe = _st1.enter_context(nc.semaphore("sem_pe"))
        h_dve = _st1.enter_context(nc.semaphore("sem_dve"))
        h_act = _st1.enter_context(nc.semaphore("sem_act"))
        h_sync = _st1.enter_context(nc.semaphore("s_sync"))
        h_gp = _st1.enter_context(nc.semaphore("s_gp"))
        h_actq = _st1.enter_context(nc.semaphore("s_actq"))
        h_cc = _st1.enter_context(nc.semaphore("s_cc"))
        h_bar = _st1.enter_context(nc.semaphore("bar"))
        em = Em(nc)
        for nm, h in (("pe", h_pe), ("dve", h_dve), ("act", h_act),
                      ("s_sync", h_sync), ("s_gp", h_gp), ("s_actq", h_actq),
                      ("s_cc", h_cc), ("bar", h_bar)):
            em.add_sem(nm, h)

        def emit_ln_tile(src_tile, xn_stage, sb, extra_sq_wait=None):
            rs = lnst[:, sb + 0:sb + 1]
            qs = lnst[:, sb + 1:sb + 2]
            nmu = lnst[:, sb + 2:sb + 3]
            vv = lnst[:, sb + 3:sb + 4]
            rstd = lnst[:, sb + 4:sb + 5]
            nmr = lnst[:, sb + 5:sb + 6]
            v_rs = em.op("vector", lambda e: e.reduce_sum(rs, src_tile[:, :], axis=AX.X),
                         inc="dve")
            sq_w = [] if extra_sq_wait is None else [extra_sq_wait]
            v_qs = em.op("scalar", lambda e: e.activation(
                xn_stage[:, :], src_tile[:, :], AF.Square, accum_out=qs),
                wait=sq_w, inc="act")
            v_nmu = em.op("vector", lambda e: e.tensor_scalar(
                nmu, rs, -1.0 / D, None, ALU.mult), wait=[("dve", v_rs)], inc="dve")
            v_vv = em.op("vector", lambda e: e.scalar_tensor_tensor(
                vv, rs, nmu, qs, ALU.mult, ALU.add),
                wait=[("dve", v_nmu), ("act", v_qs)], inc="dve")
            v_vs = em.op("vector", lambda e: e.tensor_scalar(
                vv, vv, 1.0 / D, EPS, ALU.mult, ALU.add),
                wait=[("dve", v_vv)], inc="dve")
            v_ln = em.op("scalar", lambda e: e.activation(vv, vv, AF.Ln),
                wait=[("dve", v_vs)], inc="act")
            v_rst = em.op("scalar", lambda e: e.activation(
                rstd, vv, AF.Exp, scale=-0.5), wait=[("act", v_ln)], inc="act")
            v_nmr = em.op("vector", lambda e: e.tensor_mul(nmr, nmu, rstd),
                          wait=[("act", v_rst), ("dve", v_vv)], inc="dve")
            v_xn = em.op("vector", lambda e: e.tensor_scalar(
                xn_stage[:, :], src_tile[:, :], rstd, nmr, ALU.mult, ALU.add),
                wait=[("dve", v_nmr)], inc="dve")
            return v_xn

        def emit_tile_to_fm(src_bf, dst_fm, tau, pbf, pbf_free, src_wait):
            dst3 = dst_fm[:, :].rearrange("p (c r) -> p c r", c=NC)
            v_ev = None
            for f in range(4):
                bank = pbf[f % 2]
                waits = [src_wait]
                if pbf_free[f % 2] is not None:
                    waits.append(("dve", pbf_free[f % 2]))
                v_tp = None
                for k in range(8):
                    c = 8 * f + k
                    def bld(e, bank=bank, k=k, c=c):
                        return e.transpose(bank[:, 128 * k:128 * k + 128],
                                           src_bf[:, 128 * c:128 * c + 128],
                                           ident_sb[:, :])
                    v_tp = em.op("tensor", bld, wait=(waits if k == 0 else ()),
                                 inc=("pe" if k == 7 else None))
                def ev(e, bank=bank, f=f, tau=tau):
                    return e.tensor_copy(
                        dst3[:, 8 * f:8 * f + 8, 128 * tau:128 * tau + 128],
                        bank[:, :].rearrange("p (c r) -> p c r", c=8))
                v_ev = em.op("vector", ev, wait=[("pe", v_tp)], inc="dve")
                pbf_free[f % 2] = v_ev
            return v_ev

        def emit_residual_tiles(src_fm, base_in_ap, out_writer, pbf, reload_sem_name):
            raise NotImplementedError

        with ExitStack() as _st2:
            x_tm = _st2.enter_context(nc.sbuf_tensor("x_tm", [128, 2 * 4096], F32, allow_name_mangling=True))
            xn_fm = _st2.enter_context(nc.sbuf_tensor("xn_fm", [128, NC * 1024], BF16, allow_name_mangling=True))
            xn_stage = _st2.enter_context(nc.sbuf_tensor("xn_stage", [128, 4096], BF16, allow_name_mangling=True))
            pbf_0 = _st2.enter_context(nc.psum_tensor("pbf0", [128, 1024], BF16, allow_name_mangling=True))
            pbf_1 = _st2.enter_context(nc.psum_tensor("pbf1", [128, 1024], BF16, allow_name_mangling=True))
            pbf = [pbf_0, pbf_1]
            with nc.Block("p1") as block:
                em.op("sync", lambda e: e.dma_start(out=ident_sb[:, :], in_=ident_bf.ap()[:, :]),
                      inc="s_sync", inc_k=16)
                em.op("sync", lambda e: e.dma_start(out=ident32_sb[:, :], in_=ident_f32.ap()[:, :]),
                      inc="s_sync", inc_k=16)
                em.op("sync", lambda e: e.dma_start(out=tri_sb[:, :], in_=tri01.ap()[:, :]),
                      inc="s_sync", inc_k=16)
                em.op("sync", lambda e: e.dma_start(out=e4_sb[:, :], in_=e4c.ap()[:, :]),
                      inc="s_sync", inc_k=16)
                em.op("sync", lambda e: e.dma_start(out=shift_sb[:, :], in_=shiftc.ap()[:, :]),
                      inc="s_sync", inc_k=16)
                em.op("sync", lambda e: e.dma_start(out=bias_sb[:, 0:3 * NH], in_=bqkv[:, :]),
                      inc="s_sync", inc_k=16)
                em.op("sync", lambda e: e.dma_start(out=bias_sb[:, 3 * NH:3 * NH + NC], in_=bo4[:, :]),
                      inc="s_sync", inc_k=16)
                em.op("gpsimd", lambda e: e.dma_start(out=wr_sb[:, :], in_=wr4[:, :]),
                      inc="s_gp", inc_k=16)
                em.op("gpsimd", lambda e: e.dma_start(out=a_sb[:, :], in_=a4[:, :]),
                      inc="s_gp", inc_k=16)
                em.op("gpsimd", lambda e: e.dma_start(out=bmat_sb[:, :], in_=bt_in[:, :]),
                      inc="s_gp", inc_k=16)
                v_consts = em.val("s_sync")
                em.wait("tensor", [("s_sync", v_consts)])

                pbf_free = {0: None, 1: None}
                x_done = [None, None]
                v_tp_prev = None
                v_fm_done = None
                for tau in range(NT):
                    half = tau % 2
                    waits = []
                    if x_done[half] is not None:
                        waits.append(("dve", x_done[half]))
                    v_ld = em.op("sync", lambda e, tau=tau, half=half: e.dma_start(
                        out=x_tm[:, 4096 * half:4096 * half + 4096],
                        in_=x_in[128 * tau:128 * tau + 128, :]),
                        wait=waits, inc="s_sync", inc_k=16)
                    em.wait("vector", [("s_sync", v_ld)])
                    em.wait("scalar", [("s_sync", v_ld)])
                    sq_wait = ("pe", v_tp_prev) if v_tp_prev is not None else None
                    v_xn = emit_ln_tile(x_tm[:, 4096 * half:4096 * half + 4096],
                                        xn_stage, 6 * (tau % 2), sq_wait)
                    x_done[half] = v_xn
                    v_fm_done = emit_tile_to_fm(xn_stage, xn_fm, tau, pbf, pbf_free,
                                                ("dve", v_xn))
                    v_tp_prev = em.val("pe")

                v_pub_g = []
                for ic in range(NC):
                    v_pub = em.op("sync", lambda e, ic=ic: e.dma_start(
                        out=xn_pub[128 * ic:128 * ic + 128, :],
                        in_=xn_fm[:, 1024 * ic:1024 * ic + 1024]),
                        wait=([("dve", v_fm_done)] if ic == 0 else ()),
                        inc="s_sync", inc_k=16)
                    if ic % 8 == 7:
                        v_pub_g.append(v_pub)
                for g in range(4):
                    em.op("gpsimd", lambda e, g=g: e.collective_compute(
                        "AllGather", ALU.bypass, replica_groups=[list(range(8))],
                        ins=[xn_pub[1024 * g:1024 * (g + 1), :].opt()],
                        outs=[xn_gath[g].opt()]),
                        wait=[("s_sync", v_pub_g[g])], inc="s_cc", inc_k=1)
                em.drain_and_barrier(block)

        with ExitStack() as _st3:
            wt_sb = _st3.enter_context(nc.sbuf_tensor("wt_sb", [128, 12 * 4096], BF16, allow_name_mangling=True))
            xnb = _st3.enter_context(nc.sbuf_tensor("xnb", [128, 2 * 16 * 1024], BF16, allow_name_mangling=True))
            kv_stage = _st3.enter_context(nc.sbuf_tensor("kv_stage", [128, 2 * 1024], BF16, allow_name_mangling=True))
            pb_0 = _st3.enter_context(nc.psum_tensor("pb0", [128, 512], F32, allow_name_mangling=True))
            pb_1 = _st3.enter_context(nc.psum_tensor("pb1", [128, 512], F32, allow_name_mangling=True))
            pb_2 = _st3.enter_context(nc.psum_tensor("pb2", [128, 512], F32, allow_name_mangling=True))
            pb_3 = _st3.enter_context(nc.psum_tensor("pb3", [128, 512], F32, allow_name_mangling=True))
            pb = [pb_0, pb_1, pb_2, pb_3]
            wsrc = {0: wq4, 1: wk4, 2: wv4}
            wdst = {0: q_scr, 1: k_scr, 2: v_scr}
            with nc.Block("p2") as block:
                v_wt = None
                for w in range(3):
                    for oc in range(NH):
                        slot = 4 * w + oc
                        v_wt = em.op("gpsimd", lambda e, w=w, oc=oc, slot=slot: e.dma_start(
                            out=wt_sb[:, 4096 * slot:4096 * slot + 4096],
                            in_=wsrc[w][oc].rearrange("p c o -> p (c o)")),
                            inc="s_gp", inc_k=16)
                xg3 = [xn_gath[g].rearrange("(b c p) t -> b c p t", b=8, p=128)
                       for g in range(4)]
                xnb_ld = {}
                xnb_user = [None]
                bank_i = 0
                pb_free = [None] * 4
                stage_free = [None, None]
                st_i = 0
                for rb in range(NRB):
                    for qb in range(4):
                        waits = [("s_cc", qb + 1)]
                        if xnb_user[0] is not None:
                            waits.append(("pe", xnb_user[0]))
                        v = em.op("sync", lambda e, rb=rb, qb=qb: e.dma_start(
                            out=xnb[:, 8192 * qb:8192 * qb + 8192].rearrange(
                                "p (c t) -> p c t", c=8),
                            in_=xg3[qb][rb].rearrange("c p t -> p c t")),
                            wait=waits, inc="s_sync", inc_k=16)
                        xnb_ld[(rb, qb)] = v
                    for slot in range(12):
                        w, oc = slot // 4, slot % 4
                        for hf in range(2):
                            bank = bank_i % 4
                            bank_i += 1
                            w0 = [("s_gp", v_wt), ("s_sync", xnb_ld[(rb, 0)])]
                            if pb_free[bank] is not None:
                                w0.append(("act", pb_free[bank]))
                            v_mm = None
                            for ic in range(NC):
                                def bld(e, bank=bank, slot=slot, ic=ic, hf=hf):
                                    return e.matmul(
                                        pb[bank][:, :],
                                        wt_sb[:, 4096 * slot + 128 * ic:4096 * slot + 128 * ic + 128],
                                        xnb[:, 8192 * (ic // 8) + 1024 * (ic % 8) + 512 * hf:
                                            8192 * (ic // 8) + 1024 * (ic % 8) + 512 * hf + 512],
                                        start=(ic == 0), stop=(ic == NC - 1))
                                if ic == 0:
                                    ww = w0
                                elif ic % 8 == 0:
                                    ww = [("s_sync", xnb_ld[(rb, ic // 8)])]
                                else:
                                    ww = ()
                                v_mm = em.op("tensor", bld, wait=ww,
                                             inc=("pe" if ic == NC - 1 else None))
                            xnb_user[0] = v_mm
                            sslot = st_i % 2
                            ev_w = [("pe", v_mm)]
                            if stage_free[sslot] is not None:
                                ev_w.append(("s_sync", stage_free[sslot]))
                            def ev(e, bank=bank, sslot=sslot, hf=hf, w=w, oc=oc):
                                return e.activation(
                                    kv_stage[:, 1024 * sslot + 512 * hf:1024 * sslot + 512 * hf + 512],
                                    pb[bank][:, :], AF.Identity,
                                    bias=bias_sb[:, 4 * w + oc:4 * w + oc + 1], scale=1.0)
                            v_ev = em.op("scalar", ev, wait=ev_w, inc="act")
                            pb_free[bank] = v_ev
                            if hf == 1:
                                v_st = em.op("sync", lambda e, w=w, oc=oc, rb=rb, sslot=sslot: e.dma_start(
                                    out=wdst[w][128 * oc:128 * oc + 128, 1024 * rb:1024 * rb + 1024],
                                    in_=kv_stage[:, 1024 * sslot:1024 * sslot + 1024]),
                                    wait=[("act", v_ev)], inc="s_sync", inc_k=16)
                                stage_free[sslot] = v_st
                                st_i += 1
                em.drain_and_barrier(block)

        with ExitStack() as _st4:
            qbuf = _st4.enter_context(nc.sbuf_tensor("qbuf", [128, 2 * 4096], BF16, allow_name_mangling=True))
            kbuf = _st4.enter_context(nc.sbuf_tensor("kbuf", [128, 2 * 4096], BF16, allow_name_mangling=True))
            vbuf = _st4.enter_context(nc.sbuf_tensor("vbuf", [128, 2 * 4096], BF16, allow_name_mangling=True))
            pP = _st4.enter_context(nc.sbuf_tensor("pP", [128, 2 * 4 * 4096], BF16, allow_name_mangling=True))
            vtm = _st4.enter_context(nc.sbuf_tensor("vtm", [128, 2 * 4096], BF16, allow_name_mangling=True))
            ptst = _st4.enter_context(nc.sbuf_tensor("ptst", [128, 2 * 512], BF16, allow_name_mangling=True))
            lparts = _st4.enter_context(nc.sbuf_tensor("lparts", [128, 2 * 48], F32, allow_name_mangling=True))
            lsum = _st4.enter_context(nc.sbuf_tensor("lsum", [128, 16], F32, allow_name_mangling=True))
            att_stage = _st4.enter_context(nc.sbuf_tensor("att_stage", [128, 2 * 512], BF16, allow_name_mangling=True))
            pb_0 = _st4.enter_context(nc.psum_tensor("pb0", [128, 512], F32, allow_name_mangling=True))
            pb_1 = _st4.enter_context(nc.psum_tensor("pb1", [128, 512], F32, allow_name_mangling=True))
            pb_2 = _st4.enter_context(nc.psum_tensor("pb2", [128, 512], F32, allow_name_mangling=True))
            pb_3 = _st4.enter_context(nc.psum_tensor("pb3", [128, 512], F32, allow_name_mangling=True))
            pbo_0 = _st4.enter_context(nc.psum_tensor("pbo0", [128, 512], F32, allow_name_mangling=True))
            pbo_1 = _st4.enter_context(nc.psum_tensor("pbo1", [128, 512], F32, allow_name_mangling=True))
            pbf_0 = _st4.enter_context(nc.psum_tensor("pbf0", [128, 1024], BF16, allow_name_mangling=True))
            pbf_1 = _st4.enter_context(nc.psum_tensor("pbf1", [128, 1024], BF16, allow_name_mangling=True))
            pbs = [pb_0, pb_1, pb_2, pb_3]
            pbo = [pbo_0, pbo_1]
            pbf = [pbf_0, pbf_1]
            with nc.Block("p3") as block:
                st = {
                    "sbank_free": [None] * 4, "sbank_i": 0,
                    "pbf_free": [None, None], "pbo_free": [None, None],
                    "ptst_free": [None, None],
                    "pP_user": [None, None],
                    "lp_done": [None, None],
                    "buf_user": [None, None],
                    "stage_free": [None, None],
                    "loads": {},
                }

                def emit_loads(u):
                    hh, s = u // 2, u % 2
                    slot = u % 2
                    lw = []
                    if st["buf_user"][slot] is not None:
                        lw.append(("pe", st["buf_user"][slot]))
                    v_q = em.op("sync", lambda e, hh=hh, s=s, slot=slot: e.dma_start(
                        out=qbuf[:, 4096 * slot:4096 * slot + 4096],
                        in_=q_scr[128 * hh:128 * hh + 128, 4096 * s:4096 * s + 4096]),
                        wait=lw, inc="s_sync", inc_k=16)
                    v_k = em.op("sync", lambda e, hh=hh, s=s, slot=slot: e.dma_start(
                        out=kbuf[:, 4096 * slot:4096 * slot + 4096],
                        in_=k_scr[128 * hh:128 * hh + 128, 4096 * s:4096 * s + 4096]),
                        wait=lw, inc="s_sync", inc_k=16)
                    v_v = em.op("scalar", lambda e, hh=hh, s=s, slot=slot: e.dma_start(
                        out=vbuf[:, 4096 * slot:4096 * slot + 4096],
                        in_=v_scr[128 * hh:128 * hh + 128, 4096 * s:4096 * s + 4096]),
                        wait=lw, inc="s_actq", inc_k=16)
                    st["loads"][u] = (v_q, v_k, v_v)

                def emit_vt(u):
                    slot = u % 2
                    v_v = st["loads"][u][2]
                    v_vt_ev = None
                    vt_evac = {0: None, 1: None}
                    for f in range(4):
                        bank = pbf[f % 2]
                        tw = [("s_actq", v_v)]
                        if st["buf_user"][slot] is not None:
                            tw.append(("pe", st["buf_user"][slot]))
                        if st["pbf_free"][f % 2] is not None:
                            tw.append(("dve", st["pbf_free"][f % 2]))
                        if vt_evac[f % 2] is not None:
                            tw.append(("act", vt_evac[f % 2]))
                        v_tp = None
                        for k in range(8):
                            c = 8 * f + k
                            def bld(e, bank=bank, k=k, c=c, slot=slot):
                                return e.transpose(
                                    bank[:, 128 * k:128 * k + 128],
                                    vbuf[:, 4096 * slot + 128 * c:4096 * slot + 128 * c + 128],
                                    ident_sb[:, :])
                            v_tp = em.op("tensor", bld, wait=(tw if k == 0 else ()),
                                         inc=("pe" if k == 7 else None))
                        def evv(e, bank=bank, f=f, slot=slot):
                            return e.copy(
                                vtm[:, 4096 * slot + 1024 * f:4096 * slot + 1024 * f + 1024],
                                bank[:, :])
                        v_vt_ev = em.op("scalar", evv, wait=[("pe", v_tp)], inc="act")
                        vt_evac[f % 2] = v_vt_ev
                        st["pbf_free"][f % 2] = None
                        st["pbf_vt"] = v_vt_ev
                    return v_vt_ev

                def emit_scores_softmax(ui):
                    u, qg = ui // 8, ui % 8
                    hh, s = u // 2, u % 2
                    slot, par = u % 2, ui % 2
                    sl = qg
                    prefix = 512 * (sl + 1)
                    v_q, v_k, v_v = st["loads"][u]
                    pbase = 16384 * par
                    lbase = 48 * par
                    sbase = 8 * par
                    lw = []
                    if st["lp_done"][par] is not None:
                        lw = [("act", st["lp_done"][par][0]), ("dve", st["lp_done"][par][1])]
                    v_lp0 = em.op("vector", lambda e, lbase=lbase: e.memset(
                        lparts[:, lbase:lbase + 48], 0.0), wait=lw, inc="dve")
                    ppw = [] if st["pP_user"][par] is None else [("pe", st["pP_user"][par])]
                    v_norm_done = [None] * 4
                    for qt in range(4):
                        for c in range(sl + 1):
                            bank = st["sbank_i"] % 4
                            st["sbank_i"] += 1
                            width = 512 if c < sl else 128 * (qt + 1)
                            mm_w = [("s_sync", v_k)]
                            if st["sbank_free"][bank] is not None:
                                mm_w.append(("act", st["sbank_free"][bank]))
                            def bld(e, bank=bank, qt=qt, c=c, width=width, slot=slot, qg=qg):
                                return e.matmul(
                                    pbs[bank][:, 0:width],
                                    qbuf[:, 4096 * slot + 512 * qg + 128 * qt:
                                         4096 * slot + 512 * qg + 128 * qt + 128],
                                    kbuf[:, 4096 * slot + 512 * c:
                                         4096 * slot + 512 * c + width],
                                    start=True, stop=True)
                            v_mm = em.op("tensor", bld, wait=mm_w, inc="pe")
                            pcol = pbase + 4096 * qt + 512 * c
                            if c < sl:
                                def ev(e, bank=bank, pcol=pcol, qt=qt, c=c, lbase=lbase):
                                    return e.activation(
                                        pP[:, pcol:pcol + 512], pbs[bank][:, 0:512],
                                        AF.Exp, scale=EXP_SCALE, bias=shift_sb[:, 0:1],
                                        accum_out=lparts[:, lbase + 12 * qt + c:lbase + 12 * qt + c + 1])
                                em.op("scalar", ev,
                                      wait=[("pe", v_mm), ("dve", v_lp0)] + ppw, inc="act")
                            else:
                                if qt > 0:
                                    def ev1(e, bank=bank, pcol=pcol, qt=qt, c=c, lbase=lbase):
                                        return e.activation(
                                            pP[:, pcol:pcol + 128 * qt],
                                            pbs[bank][:, 0:128 * qt],
                                            AF.Exp, scale=EXP_SCALE, bias=shift_sb[:, 0:1],
                                            accum_out=lparts[:, lbase + 12 * qt + c:lbase + 12 * qt + c + 1])
                                    em.op("scalar", ev1,
                                          wait=[("pe", v_mm), ("dve", v_lp0)] + ppw, inc="act")
                                def ev2(e, bank=bank, pcol=pcol, qt=qt):
                                    return e.activation(
                                        pP[:, pcol + 128 * qt:pcol + 128 * qt + 128],
                                        pbs[bank][:, 128 * qt:128 * qt + 128],
                                        AF.Exp, scale=EXP_SCALE, bias=shift_sb[:, 0:1])
                                v_ev2 = em.op("scalar", ev2,
                                              wait=[("pe", v_mm)] + ppw, inc="act")
                                def msk(e, pcol=pcol, qt=qt):
                                    return e.tensor_mul(
                                        pP[:, pcol + 128 * qt:pcol + 128 * qt + 128],
                                        pP[:, pcol + 128 * qt:pcol + 128 * qt + 128],
                                        tri_sb[:, :])
                                v_mk = em.op("vector", msk,
                                             wait=[("act", v_ev2), ("dve", v_lp0)], inc="dve")
                                def dsum(e, pcol=pcol, qt=qt, c=c, lbase=lbase):
                                    return e.reduce_sum(
                                        lparts[:, lbase + 12 * qt + c + 1:lbase + 12 * qt + c + 2],
                                        pP[:, pcol + 128 * qt:pcol + 128 * qt + 128],
                                        axis=AX.X)
                                em.op("vector", dsum, wait=[("dve", v_mk)], inc="dve")
                            st["sbank_free"][bank] = em.val("act")
                        wq = 512 * sl + 128 * (qt + 1)
                        if wq < prefix:
                            def z(e, qt=qt, wq=wq, prefix=prefix, pbase=pbase):
                                return e.memset(
                                    pP[:, pbase + 4096 * qt + wq:pbase + 4096 * qt + prefix], 0.0)
                            em.op("vector", z, wait=ppw, inc="dve")
                    act_all = em.val("act")
                    dve_all = em.val("dve")
                    for qt in range(4):
                        def lsb(e, qt=qt, sl=sl, lbase=lbase, sbase=sbase):
                            return e.reduce_sum(
                                lsum[:, sbase + qt:sbase + qt + 1],
                                lparts[:, lbase + 12 * qt:lbase + 12 * qt + sl + 2], axis=AX.X)
                        v_ls = em.op("vector", lsb,
                                     wait=[("act", act_all), ("dve", dve_all)], inc="dve")
                        v_rc = em.op("vector", lambda e, qt=qt, sbase=sbase: e.reciprocal(
                            lsum[:, sbase + 4 + qt:sbase + 5 + qt],
                            lsum[:, sbase + qt:sbase + qt + 1]),
                            wait=[("dve", v_ls)], inc="dve")
                        wqv = 512 * sl + 128 * (qt + 1)
                        def nrm(e, qt=qt, wqv=wqv, pbase=pbase, sbase=sbase):
                            return e.tensor_scalar(
                                pP[:, pbase + 4096 * qt:pbase + 4096 * qt + wqv],
                                pP[:, pbase + 4096 * qt:pbase + 4096 * qt + wqv],
                                lsum[:, sbase + 4 + qt:sbase + 5 + qt], None, ALU.mult)
                        v_norm_done[qt] = em.op("vector", nrm, wait=[("dve", v_rc)], inc="dve")
                    st["lp_done"][par] = (act_all, em.val("dve"))
                    return dict(ui=ui, u=u, qg=qg, hh=hh, s=s, slot=slot, par=par,
                                sl=sl, prefix=prefix, ntc=prefix // 128,
                                norm=v_norm_done)

                def emit_ptav(info, v_vt_ev_by_u):
                    ui, u, qg = info["ui"], info["u"], info["qg"]
                    slot, par = info["slot"], info["par"]
                    ntc = info["ntc"]
                    pbase = 16384 * par
                    obank = pbo[par]
                    v_vt = v_vt_ev_by_u[u]
                    v_ev_c = [None] * ntc
                    v_av = None
                    v_last_pt = None

                    def emit_pt(c):
                        nonlocal v_last_pt
                        bank = pbf[c % 2]
                        w0 = [("dve", info["norm"][3])]
                        if st["pbf_free"][c % 2] is not None:
                            w0.append(("dve", st["pbf_free"][c % 2]))
                        if c < 2 and st.get("pbf_vt") is not None:
                            w0.append(("act", st["pbf_vt"]))
                        v_tp = None
                        for qt in range(4):
                            def bld(e, bank=bank, qt=qt, c=c, pbase=pbase):
                                return e.transpose(
                                    bank[:, 128 * qt:128 * qt + 128],
                                    pP[:, pbase + 4096 * qt + 128 * c:pbase + 4096 * qt + 128 * c + 128],
                                    ident_sb[:, :])
                            v_tp = em.op("tensor", bld, wait=(w0 if qt == 0 else ()),
                                         inc=("pe" if qt == 3 else None))
                        v_last_pt = v_tp
                        pslot = c % 2
                        ev_w = [("pe", v_tp)]
                        if st["ptst_free"][pslot] is not None:
                            ev_w.append(("pe", st["ptst_free"][pslot]))
                        def evp(e, bank=bank, pslot=pslot):
                            return e.tensor_copy(
                                ptst[:, 512 * pslot:512 * pslot + 512], bank[:, 0:512])
                        v_ev = em.op("vector", evp, wait=ev_w, inc="dve")
                        st["pbf_free"][c % 2] = v_ev
                        v_ev_c[c] = v_ev

                    def emit_av(c):
                        nonlocal v_av
                        pslot = c % 2
                        av_w = [("dve", v_ev_c[c]), ("act", v_vt)]
                        if c == 0 and st["pbo_free"][par] is not None:
                            av_w.append(("dve", st["pbo_free"][par]))
                        def av(e, c=c, pslot=pslot, ntc=ntc, slot=slot):
                            return e.matmul(
                                obank[:, :],
                                vtm[:, 4096 * slot + 128 * c:4096 * slot + 128 * c + 128],
                                ptst[:, 512 * pslot:512 * pslot + 512],
                                start=(c == 0), stop=(c == ntc - 1))
                        v_av = em.op("tensor", av, wait=av_w, inc="pe")
                        st["ptst_free"][pslot] = v_av

                    for c in range(ntc):
                        emit_pt(c)
                        if c >= 1:
                            emit_av(c - 1)
                    emit_av(ntc - 1)
                    st["pP_user"][par] = v_last_pt

                    ev_w = [("pe", v_av)]
                    if st["stage_free"][par] is not None:
                        ev_w.append(("s_gp", st["stage_free"][par]))
                    def oev(e, obank=obank, sslot=par):
                        return e.tensor_copy(
                            att_stage[:, 512 * sslot:512 * sslot + 512], obank[:, :])
                    v_oev = em.op("vector", oev, wait=ev_w, inc="dve")
                    st["pbo_free"][par] = v_oev
                    r = 4 * info["s"] + qg // 2
                    colo = 512 * (qg % 2)
                    v_att_st = em.op("gpsimd", lambda e, r=r, hh=info["hh"], colo=colo,
                                     sslot=par: e.dma_start(
                        out=att_pub[512 * r + 128 * hh:512 * r + 128 * hh + 128,
                                    colo:colo + 512],
                        in_=att_stage[:, 512 * sslot:512 * sslot + 512]),
                        wait=[("dve", v_oev)], inc="s_gp", inc_k=16)
                    st["stage_free"][par] = v_att_st
                    if qg == 7:
                        st["buf_user"][slot] = v_av

                emit_loads(0)
                emit_loads(1)
                v_vt_by_u = {}
                infos = []
                for ui in range(64):
                    u, qg = ui // 8, ui % 8
                    if qg == 0:
                        v_vt_by_u[u] = emit_vt(u)
                    if qg == 1 and u + 1 < 8 and (u + 1) not in st["loads"]:
                        emit_loads(u + 1)
                    infos.append(emit_scores_softmax(ui))
                    if ui > 0:
                        emit_ptav(infos[ui - 1], v_vt_by_u)
                emit_ptav(infos[63], v_vt_by_u)

                def a2a(e):
                    with allow_ici_all_to_all():
                        return e.collective_compute(
                            "AllToAll", ALU.bypass, replica_groups=[list(range(8))],
                            ins=[att_pub.opt()], outs=[att_own.opt()])
                em.op("gpsimd", a2a, wait=[("s_gp", em.val("s_gp"))],
                      inc="s_cc", inc_k=1)
                em.drain_and_barrier(block)

        with nc.sbuf_tensor("outT", [128, NC * 1024], BF16, allow_name_mangling=True) as outT:
            with ExitStack() as _st5:
                att_full = _st5.enter_context(nc.sbuf_tensor("att_full", [128, NC * 1024], BF16, allow_name_mangling=True))
                wbuf = _st5.enter_context(nc.sbuf_tensor("wbuf", [128, 2 * 4096], BF16, allow_name_mangling=True))
                pb_0 = _st5.enter_context(nc.psum_tensor("pb0", [128, 512], F32, allow_name_mangling=True))
                pb_1 = _st5.enter_context(nc.psum_tensor("pb1", [128, 512], F32, allow_name_mangling=True))
                pb_2 = _st5.enter_context(nc.psum_tensor("pb2", [128, 512], F32, allow_name_mangling=True))
                pb_3 = _st5.enter_context(nc.psum_tensor("pb3", [128, 512], F32, allow_name_mangling=True))
                pb = [pb_0, pb_1, pb_2, pb_3]
                with nc.Block("p4") as block:
                    v_att_ld = None
                    for ic in range(NC):
                        v_att_ld = em.op("sync", lambda e, ic=ic: e.dma_start(
                            out=att_full[:, 1024 * ic:1024 * ic + 1024],
                            in_=att_own[128 * ic:128 * ic + 128, :]),
                            wait=([("s_cc", 5)] if ic == 0 else ()),
                            inc="s_sync", inc_k=16)
                    wb_user = [None, None]
                    pb_free = [None] * 4
                    bank_i = 0
                    for oc in range(NC):
                        wslot = oc % 2
                        ww = []
                        if wb_user[wslot] is not None:
                            ww.append(("pe", wb_user[wslot]))
                        v_w = em.op("gpsimd", lambda e, oc=oc, wslot=wslot: e.dma_start(
                            out=wbuf[:, 4096 * wslot:4096 * wslot + 4096],
                            in_=wo4[oc].rearrange("p c o -> p (c o)")),
                            wait=ww, inc="s_gp", inc_k=16)
                        for hf in range(2):
                            bank = bank_i % 4
                            bank_i += 1
                            mm_w = [("s_gp", v_w), ("s_sync", v_att_ld)]
                            if pb_free[bank] is not None:
                                mm_w.append(("act", pb_free[bank]))
                            v_mm = None
                            for ic in range(NC):
                                def bld(e, bank=bank, wslot=wslot, ic=ic, hf=hf):
                                    return e.matmul(
                                        pb[bank][:, :],
                                        wbuf[:, 4096 * wslot + 128 * ic:4096 * wslot + 128 * ic + 128],
                                        att_full[:, 1024 * ic + 512 * hf:1024 * ic + 512 * hf + 512],
                                        start=(ic == 0), stop=(ic == NC - 1))
                                v_mm = em.op("tensor", bld,
                                             wait=(mm_w if ic == 0 else ()),
                                             inc=("pe" if ic == NC - 1 else None))
                            wb_user[wslot] = v_mm
                            def ev(e, bank=bank, oc=oc, hf=hf):
                                return e.activation(
                                    outT[:, 1024 * oc + 512 * hf:1024 * oc + 512 * hf + 512],
                                    pb[bank][:, :], AF.Identity,
                                    bias=bias_sb[:, 3 * NH + oc:3 * NH + oc + 1],
                                    scale=1.0)
                            v_ev = em.op("scalar", ev, wait=[("pe", v_mm)], inc="act")
                            pb_free[bank] = v_ev
                    em.drain_and_barrier(block)

            with ExitStack() as _st6:
                x_tm = _st6.enter_context(nc.sbuf_tensor("x_tm", [128, 2 * 4096], F32, allow_name_mangling=True))
                x1_t = _st6.enter_context(nc.sbuf_tensor("x1_t", [128, 4096], F32, allow_name_mangling=True))
                xn2_stage = _st6.enter_context(nc.sbuf_tensor("xn2_stage", [128, 4096], BF16, allow_name_mangling=True))
                xn2_fm = _st6.enter_context(nc.sbuf_tensor("xn2_fm", [128, NC * 1024], BF16, allow_name_mangling=True))
                l4 = _st6.enter_context(nc.sbuf_tensor("l4", [128, NT * 4], F32, allow_name_mangling=True))
                gsc = _st6.enter_context(nc.sbuf_tensor("gsc", [128, NT * 8], F32, allow_name_mangling=True))
                g4 = _st6.enter_context(nc.sbuf_tensor("g4", [128, NT * 4], F32, allow_name_mangling=True))
                gt_sb = _st6.enter_context(nc.sbuf_tensor("gt_sb", [4, 1024], BF16, allow_name_mangling=True))
                gb_sb = _st6.enter_context(nc.sbuf_tensor("gb_sb", [64, 1024], BF16, allow_name_mangling=True))
                gmid = _st6.enter_context(nc.sbuf_tensor("gmid", [64, 1024], BF16, allow_name_mangling=True))
                pb_0 = _st6.enter_context(nc.psum_tensor("pb0", [128, 512], F32, allow_name_mangling=True))
                pb_1 = _st6.enter_context(nc.psum_tensor("pb1", [128, 512], F32, allow_name_mangling=True))
                pb_2 = _st6.enter_context(nc.psum_tensor("pb2", [128, 512], F32, allow_name_mangling=True))
                pb_3 = _st6.enter_context(nc.psum_tensor("pb3", [128, 512], F32, allow_name_mangling=True))
                pbm = _st6.enter_context(nc.psum_tensor("pbm", [64, 1024], F32, allow_name_mangling=True))
                pbf_0 = _st6.enter_context(nc.psum_tensor("pbf0", [128, 1024], BF16, allow_name_mangling=True))
                pbf_1 = _st6.enter_context(nc.psum_tensor("pbf1", [128, 1024], BF16, allow_name_mangling=True))
                pb = [pb_0, pb_1, pb_2, pb_3]
                pbf = [pbf_0, pbf_1]
                with nc.Block("p56") as block:
                    pbf_free = {0: None, 1: None}
                    x_done = [None, None]
                    x1_free = None
                    v_xn2_done = None
                    v_tp_prev = None
                    for tau in range(NT):
                        half = tau % 2
                        lw = []
                        if x_done[half] is not None:
                            lw.append(("dve", x_done[half]))
                        v_ld = em.op("sync", lambda e, tau=tau, half=half: e.dma_start(
                            out=x_tm[:, 4096 * half:4096 * half + 4096],
                            in_=x_in[128 * tau:128 * tau + 128, :]),
                            wait=lw, inc="s_sync", inc_k=16)
                        v_add = None
                        for f in range(4):
                            bank = pbf[f % 2]
                            tw = [("act", em.val("act"))]
                            if pbf_free[f % 2] is not None:
                                tw.append(("dve", pbf_free[f % 2]))
                            v_tp = None
                            for k in range(8):
                                oc = 8 * f + k
                                def bld(e, bank=bank, k=k, oc=oc, tau=tau):
                                    return e.transpose(
                                        bank[:, 128 * k:128 * k + 128],
                                        outT[:, 1024 * oc + 128 * tau:1024 * oc + 128 * tau + 128],
                                        ident_sb[:, :])
                                v_tp = em.op("tensor", bld, wait=(tw if k == 0 else ()),
                                             inc=("pe" if k == 7 else None))
                            aw = [("pe", v_tp), ("s_sync", v_ld)]
                            if x1_free is not None and f == 0:
                                aw += [("act", x1_free[0]), ("s_gp", x1_free[1]),
                                       ("dve", x1_free[2])]
                            def addb(e, bank=bank, f=f, half=half):
                                return e.tensor_add(
                                    x1_t[:, 1024 * f:1024 * f + 1024],
                                    x_tm[:, 4096 * half + 1024 * f:4096 * half + 1024 * f + 1024],
                                    bank[:, :])
                            v_add = em.op("vector", addb, wait=aw, inc="dve")
                            pbf_free[f % 2] = v_add
                        x_done[half] = v_add
                        v_st = em.op("gpsimd", lambda e, tau=tau: e.dma_start(
                            out=x1_dram[128 * tau:128 * tau + 128, :], in_=x1_t[:, :]),
                            wait=[("dve", v_add)], inc="s_gp", inc_k=16)
                        em.wait("scalar", [("dve", v_add)])
                        em.wait("vector", [("dve", v_add)])
                        sq_wait = ("pe", v_tp_prev) if v_tp_prev is not None else None
                        v_xn2 = emit_ln_tile(x1_t, xn2_stage, 6 * (tau % 2), sq_wait)
                        x1_free = (em.val("act"), v_st, v_xn2)
                        v_xn2_done = emit_tile_to_fm(xn2_stage, xn2_fm, tau, pbf,
                                                     pbf_free, ("dve", v_xn2))
                        v_tp_prev = em.val("pe")

                    v_gb_evs = []
                    for tau in range(NT):
                        bank = pb[tau % 2]
                        mm_w = [("dve", v_xn2_done)]
                        if tau >= 2:
                            mm_w.append(("dve", v_gb_evs[tau - 2][0]))
                        v_mm = None
                        for ic in range(NC):
                            def bld(e, bank=bank, ic=ic, tau=tau):
                                return e.matmul(
                                    bank[:, 0:4],
                                    xn2_fm[:, 1024 * ic + 128 * tau:1024 * ic + 128 * tau + 128],
                                    wr_sb[:, 4 * ic:4 * ic + 4],
                                    start=(ic == 0), stop=(ic == NC - 1))
                            v_mm = em.op("tensor", bld, wait=(mm_w if ic == 0 else ()),
                                         inc=("pe" if ic == NC - 1 else None))
                        lts = l4[:, 4 * tau:4 * tau + 4]
                        v_l = em.op("vector", lambda e, bank=bank, lts=lts: e.tensor_copy(
                            lts, bank[:, 0:4]), wait=[("pe", v_mm)], inc="dve")
                        m1 = gsc[:, 8 * tau + 0:8 * tau + 1]
                        nm1 = gsc[:, 8 * tau + 1:8 * tau + 2]
                        eq = gsc[:, 8 * tau + 2:8 * tau + 6]
                        m2 = gsc[:, 8 * tau + 6:8 * tau + 7]
                        z = gsc[:, 8 * tau + 7:8 * tau + 8]
                        ets = g4[:, 4 * tau:4 * tau + 4]
                        v_m1 = em.op("vector", lambda e, m1=m1, lts=lts: e.reduce_max(
                            m1, lts, axis=AX.X), wait=[("dve", v_l)], inc="dve")
                        v_nm1 = em.op("vector", lambda e, nm1=nm1, lts=lts: e.reduce_max(
                            nm1, lts, axis=AX.X, negate=True),
                            wait=[("dve", v_l)], inc="dve")
                        v_eq = em.op("vector", lambda e, eq=eq, lts=lts, m1=m1: e.tensor_scalar(
                            eq, lts, m1, None, ALU.is_ge),
                            wait=[("dve", v_m1)], inc="dve")
                        v_lm = em.op("vector", lambda e, eq=eq, lts=lts: e.scalar_tensor_tensor(
                            eq, eq, -1e30, lts, ALU.mult, ALU.add),
                            wait=[("dve", v_eq)], inc="dve")
                        v_m2 = em.op("vector", lambda e, m2=m2, eq=eq: e.reduce_max(
                            m2, eq, axis=AX.X), wait=[("dve", v_lm)], inc="dve")
                        v_e = em.op("scalar", lambda e, ets=ets, lts=lts, nm1=nm1: e.activation(
                            ets, lts, AF.Exp, bias=nm1, scale=1.0),
                            wait=[("dve", v_nm1)], inc="act")
                        v_ge = em.op("vector", lambda e, eq=eq, lts=lts, m2=m2: e.tensor_scalar(
                            eq, lts, m2, None, ALU.is_ge),
                            wait=[("dve", v_m2)], inc="dve")
                        v_g = em.op("vector", lambda e, ets=ets, eq=eq: e.tensor_mul(
                            ets, ets, eq), wait=[("dve", v_ge), ("act", v_e)], inc="dve")
                        v_z = em.op("vector", lambda e, z=z, ets=ets: e.reduce_sum(
                            z, ets, axis=AX.X), wait=[("dve", v_g)], inc="dve")
                        v_rz = em.op("vector", lambda e, m1=m1, z=z: e.reciprocal(m1, z),
                                     wait=[("dve", v_z)], inc="dve")
                        v_gn = em.op("vector", lambda e, ets=ets, m1=m1: e.tensor_scalar(
                            ets, ets, m1, 4.0, ALU.mult, ALU.mult),
                            wait=[("dve", v_rz)], inc="dve")
                        v_gt = em.op("tensor", lambda e, ets=ets: e.transpose(
                            pb[2][0:4, 0:128], ets, ident32_sb[:, :]),
                            wait=[("dve", v_gn)], inc="pe")
                        v_gte = em.op("vector", lambda e, tau=tau: e.tensor_copy(
                            gt_sb[:, 128 * tau:128 * tau + 128], pb[2][0:4, 0:128]),
                            wait=[("pe", v_gt)], inc="dve")
                        v_gb = em.op("tensor", lambda e, tau=tau: e.matmul(
                            pbm[0:64, 512:640], e4_sb[:, :],
                            gt_sb[:, 128 * tau:128 * tau + 128],
                            start=True, stop=True),
                            wait=[("dve", v_gte)], inc="pe")
                        v_gbe = em.op("vector", lambda e, tau=tau: e.tensor_copy(
                            gb_sb[:, 128 * tau:128 * tau + 128], pbm[0:64, 512:640]),
                            wait=[("pe", v_gb)], inc="dve")
                        v_gb_evs.append((v_gbe, v_gb))
                    v_gb_all = em.val("dve")

                    v_mid_ev = []
                    for hf in range(2):
                        v_mm = None
                        for ic in range(NC):
                            w0 = [("dve", v_gb_all)] if ic == 0 else ()
                            def bld(e, ic=ic, hf=hf):
                                return e.matmul(
                                    pbm[0:64, 512 * hf:512 * hf + 512],
                                    a_sb[:, 64 * ic:64 * ic + 64],
                                    xn2_fm[:, 1024 * ic + 512 * hf:1024 * ic + 512 * hf + 512],
                                    start=(ic == 0), stop=(ic == NC - 1))
                            v_mm = em.op("tensor", bld, wait=w0,
                                         inc=("pe" if ic == NC - 1 else None))
                        def gm(e, hf=hf):
                            return e.tensor_mul(gmid[:, 512 * hf:512 * hf + 512],
                                                pbm[0:64, 512 * hf:512 * hf + 512],
                                                gb_sb[:, 512 * hf:512 * hf + 512])
                        v_gm = em.op("vector", gm, wait=[("pe", v_mm)], inc="dve")
                        v_mid_ev.append(v_gm)

                    v_p5_pe = v_tp_prev
                    pb_free2 = [None] * 4
                    bank_i = 0
                    for oc in range(NC):
                        for hf in range(2):
                            bank = bank_i % 4
                            bank_i += 1
                            mw = [("dve", v_mid_ev[hf])]
                            if pb_free2[bank] is not None:
                                mw.append(("act", pb_free2[bank]))
                            def bld(e, bank=bank, oc=oc, hf=hf):
                                return e.matmul(pb[bank][:, :],
                                                bmat_sb[:, 128 * oc:128 * oc + 128],
                                                gmid[:, 512 * hf:512 * hf + 512],
                                                start=True, stop=True)
                            v_mm = em.op("tensor", bld, wait=mw, inc="pe")
                            def ev(e, bank=bank, oc=oc, hf=hf):
                                return e.activation(
                                    outT[:, 1024 * oc + 512 * hf:1024 * oc + 512 * hf + 512],
                                    pb[bank][:, :], AF.Identity, scale=1.0)
                            v_dev = em.op("scalar", ev, wait=[("pe", v_mm)], inc="act")
                            pb_free2[bank] = v_dev
                    em.drain_and_barrier(block)

            with ExitStack() as _st7:
                x1b = _st7.enter_context(nc.sbuf_tensor("x1b", [128, 2 * 4096], F32, allow_name_mangling=True))
                x2_t = _st7.enter_context(nc.sbuf_tensor("x2_t", [128, 2 * 4096], F32, allow_name_mangling=True))
                pbf_0 = _st7.enter_context(nc.psum_tensor("pbf0", [128, 1024], BF16, allow_name_mangling=True))
                pbf_1 = _st7.enter_context(nc.psum_tensor("pbf1", [128, 1024], BF16, allow_name_mangling=True))
                pbf = [pbf_0, pbf_1]
                with nc.Block("p7") as block:
                    pbf_free = {0: None, 1: None}
                    x1_done = [None, None]
                    out_vals = []
                    for tau in range(NT):
                        half = tau % 2
                        lw = []
                        if x1_done[half] is not None:
                            lw.append(("dve", x1_done[half]))
                        v_ld = em.op("sync", lambda e, tau=tau, half=half: e.dma_start(
                            out=x1b[:, 4096 * half:4096 * half + 4096],
                            in_=x1_dram[128 * tau:128 * tau + 128, :]),
                            wait=lw, inc="s_sync", inc_k=16)
                        v_add = None
                        for f in range(4):
                            bank = pbf[f % 2]
                            tw = [("act", em.val("act"))]
                            if pbf_free[f % 2] is not None:
                                tw.append(("dve", pbf_free[f % 2]))
                            v_tp = None
                            for k in range(8):
                                oc = 8 * f + k
                                def bld(e, bank=bank, k=k, oc=oc, tau=tau):
                                    return e.transpose(
                                        bank[:, 128 * k:128 * k + 128],
                                        outT[:, 1024 * oc + 128 * tau:1024 * oc + 128 * tau + 128],
                                        ident_sb[:, :])
                                v_tp = em.op("tensor", bld, wait=(tw if k == 0 else ()),
                                             inc=("pe" if k == 7 else None))
                            aw = [("pe", v_tp), ("s_sync", v_ld)]
                            if len(out_vals) >= 2 and f == 0:
                                aw.append(("s_gp", out_vals[-2]))
                            def addb(e, bank=bank, f=f, half=half):
                                return e.tensor_add(
                                    x2_t[:, 4096 * half + 1024 * f:4096 * half + 1024 * f + 1024],
                                    x1b[:, 4096 * half + 1024 * f:4096 * half + 1024 * f + 1024],
                                    bank[:, :])
                            v_add = em.op("vector", addb, wait=aw, inc="dve")
                            pbf_free[f % 2] = v_add
                        x1_done[half] = v_add
                        v_st = em.op("gpsimd", lambda e, tau=tau, half=half: e.dma_start(
                            out=y_out[128 * tau:128 * tau + 128, :],
                            in_=x2_t[:, 4096 * half:4096 * half + 4096]),
                            wait=[("dve", v_add)], inc="s_gp", inc_k=16)
                        out_vals.append(v_st)
                    for t, (dst, src) in taps.items():
                        em.op("gpsimd", lambda e, dst=dst, src=src: e.dma_start(
                            out=dst[:, :], in_=src[:, :]), inc="s_gp", inc_k=16)
                    em.wait("gpsimd", [("s_gp", em.val("s_gp"))])
                    em.wait("sync", [("s_gp", em.val("s_gp"))])
                    em.flush(block)

    return nc


def _host_prepare(inputs):
    x = np.asarray(inputs["L0_attn_x"], dtype=np.float32).reshape(8192, D)
    wt = {}
    for nm, key in (("wq", "L0_attn_Wq"), ("wk", "L0_attn_Wk"), ("wv", "L0_attn_Wv")):
        W = np.asarray(inputs[key], dtype=np.float32)
        wt[nm] = W.T.reshape(NC, 128, NC, 128)
    Wo = np.asarray(inputs["L0_attn_Wo"], dtype=np.float32)
    wo4 = np.ascontiguousarray(Wo.T.reshape(NC, 128, NC, 128).transpose(2, 1, 0, 3))
    bq = np.asarray(inputs["L0_attn_bq"], dtype=np.float32).reshape(NC, 128)
    bk = np.asarray(inputs["L0_attn_bk"], dtype=np.float32).reshape(NC, 128)
    bv = np.asarray(inputs["L0_attn_bv"], dtype=np.float32).reshape(NC, 128)
    bo = np.asarray(inputs["L0_attn_bo"], dtype=np.float32)
    bo4 = np.ascontiguousarray(bo.reshape(NC, 128).T)
    A = np.asarray(inputs["L0_ffn_A"], dtype=np.float32)
    B = np.asarray(inputs["L0_ffn_B"], dtype=np.float32)
    Wr = np.asarray(inputs["L0_ffn_Wr"], dtype=np.float32)
    a_allt = A.transpose(2, 0, 1).reshape(D, 64)
    a4 = np.ascontiguousarray(
        a_allt.reshape(NC, 128, 64).transpose(1, 0, 2)).reshape(128, NC * 64)
    bt = np.ascontiguousarray(B.transpose(0, 2, 1).reshape(64, D))
    wr4 = np.ascontiguousarray(
        Wr.T.reshape(NC, 128, 4).transpose(1, 0, 2)).reshape(128, NC * 4)
    shared = dict(a4=a4, bt=bt, wr4=wr4, wo4=wo4, bo4=bo4)
    in_maps = []
    for j in range(8):
        m = dict(shared)
        m["x_own"] = np.ascontiguousarray(x[1024 * j:1024 * (j + 1)])
        hc = slice(4 * j, 4 * j + 4)
        for nm in ("wq", "wk", "wv"):
            m[nm + "4"] = np.ascontiguousarray(
                wt[nm][:, :, hc, :].transpose(2, 1, 0, 3))
        m["bqkv"] = np.ascontiguousarray(
            np.concatenate([bq[hc].T, bk[hc].T, bv[hc].T], axis=1))
        in_maps.append(m)
    return in_maps


def kernel(**inputs):
    in_maps = _host_prepare(inputs)
    nc = build_core()
    res = run_bass_kernel_spmd(nc, in_maps, core_ids=list(range(8)))
    out = np.empty((8192, D), dtype=np.float32)
    for j in range(8):
        out[1024 * j:1024 * (j + 1)] = res.results[j]["y"]
    return out.reshape(2, 4096, D)


# version 06 (final)
# speedup vs baseline: 1.2208x; 1.0043x over previous
from contextlib import ExitStack

import numpy as np
import ml_dtypes
import concourse.bass as bass
import concourse.mybir as mybir
from concourse.bass_utils import run_bass_kernel_spmd
from concourse.aot_env import allow_ici_all_to_all

F32 = mybir.dt.float32
BF16 = mybir.dt.bfloat16
AF = mybir.ActivationFunctionType
ALU = mybir.AluOpType
AX = mybir.AxisListType

D = 4096
NC = 32
NH = 4
NRB = 8
NT = 8
EXP_SCALE = 1.0 / np.sqrt(128.0)
EXP_SHIFT = -12.0
EPS = 1e-5

ENGINES = ("tensor", "vector", "scalar", "sync", "gpsimd")


class Em:

    def __init__(self, nc):
        self.nc = nc
        self.ops = {e: [] for e in ENGINES}
        self.sems = {}

    def add_sem(self, name, handle):
        self.sems[name] = [handle, 0]

    def val(self, name):
        return self.sems[name][1]

    def op(self, eng, build, wait=(), inc=None, inc_k=1):
        for (w, v) in wait:
            assert v <= self.sems[w][1], (w, v, self.sems[w][1])
        targets = [(self.sems[w][0], v) for (w, v) in wait if v > 0]
        handle = None
        if inc is not None:
            self.sems[inc][1] += inc_k
            handle = self.sems[inc][0]

        def f(e, targets=targets, build=build, handle=handle, inc_k=inc_k):
            for (h, v) in targets:
                e.wait_ge(h, v)
            ins = build(e)
            if handle is not None:
                ins.then_inc(handle, inc_k)

        self.ops[eng].append(f)
        return self.sems[inc][1] if inc is not None else None

    def wait(self, eng, waits):
        for (w, v) in waits:
            assert v <= self.sems[w][1], (w, v, self.sems[w][1])
        targets = [(self.sems[w][0], v) for (w, v) in waits if v > 0]

        def f(e, targets=targets):
            for (h, v) in targets:
                e.wait_ge(h, v)

        self.ops[eng].append(f)

    def flush(self, block):
        for e in ENGINES:
            if self.ops[e]:
                fns = self.ops[e]
                getattr(block, e)(lambda engine, fns=fns: [f(engine) for f in fns])
        self.ops = {e: [] for e in ENGINES}

    def drain_and_barrier(self, block):
        self.wait("sync", [("s_sync", self.val("s_sync"))])
        self.wait("gpsimd", [("s_gp", self.val("s_gp"))])
        self.wait("scalar", [("s_actq", self.val("s_actq")), ("act", self.val("act"))])
        self.wait("vector", [("dve", self.val("dve"))])
        self.wait("tensor", [("pe", self.val("pe"))])
        bar = self.sems["bar"]
        bar[1] += 5
        target = bar[1]
        for e in ENGINES:
            def f(eng, h=bar[0], t=target):
                eng.nop().then_inc(h, 1)
                eng.wait_ge(h, t)
            self.ops[e].append(f)
        self.flush(block)


def build_core(debug_taps=()):
    nc = bass.Bass("TRN2", target_bir_lowering=False)

    x_in = nc.dram_tensor("x_own", [NT * 128, D], F32, kind="ExternalInput").ap()
    wq4 = nc.dram_tensor("wq4", [NH, 128, NC, 128], F32, kind="ExternalInput").ap()
    wk4 = nc.dram_tensor("wk4", [NH, 128, NC, 128], F32, kind="ExternalInput").ap()
    wv4 = nc.dram_tensor("wv4", [NH, 128, NC, 128], F32, kind="ExternalInput").ap()
    wo4 = nc.dram_tensor("wo4", [NC, 128, NC, 128], F32, kind="ExternalInput").ap()
    bqkv = nc.dram_tensor("bqkv", [128, 3 * NH], F32, kind="ExternalInput").ap()
    bo4 = nc.dram_tensor("bo4", [128, NC], F32, kind="ExternalInput").ap()
    a4 = nc.dram_tensor("a4", [128, NC * 64], F32, kind="ExternalInput").ap()
    bt_in = nc.dram_tensor("bt", [64, D], F32, kind="ExternalInput").ap()
    wr4 = nc.dram_tensor("wr4", [128, NC * 4], F32, kind="ExternalInput").ap()
    y_out = nc.dram_tensor("y", [NT * 128, D], F32, kind="ExternalOutput").ap()

    xn_pub = [nc.dram_tensor("xn_pubq%d" % g, [D, 256], BF16).ap()
              for g in range(4)]
    xn_gath = [nc.dram_tensor("xn_gath%d" % g, [8 * D, 256], BF16).ap()
               for g in range(4)]
    q_scr = nc.dram_tensor("q_scr", [NH * 128, 8192], BF16).ap()
    k_scr = nc.dram_tensor("k_scr", [NH * 128, 8192], BF16).ap()
    v_scr = nc.dram_tensor("v_scr", [NH * 128, 8192], BF16).ap()
    att_pub = nc.dram_tensor("att_pub", [D, 1024], BF16).ap()
    att_own = nc.dram_tensor("att_own", [D, 1024], BF16).ap()
    x1_dram = nc.dram_tensor("x1_scr", [NT * 128, D], F32).ap()

    ident_bf = nc.inline_tensor(np.eye(128, dtype=ml_dtypes.bfloat16), "c_ident_bf")
    ident_f32 = nc.inline_tensor(np.eye(128, dtype=np.float32), "c_ident_f32")
    qi = np.arange(128)[:, None]
    ti = np.arange(128)[None, :]
    tri01 = nc.inline_tensor((ti <= qi).astype(ml_dtypes.bfloat16), "c_tri01")
    e4c = nc.inline_tensor(
        np.repeat(np.eye(4, dtype=ml_dtypes.bfloat16), 16, axis=1), "c_e4")
    shiftc = nc.inline_tensor(
        np.full((128, 1), EXP_SHIFT, dtype=np.float32), "c_shift")

    taps = {}
    for t in debug_taps:
        src = {"q_scr": q_scr, "k_scr": k_scr,
               "v_scr": v_scr, "att_pub": att_pub, "att_own": att_own,
               "x1": x1_dram}[t]
        shp = list(src.tensor.shape)
        taps[t] = (nc.dram_tensor("tap_" + t, shp, src.tensor.dtype,
                                  kind="ExternalOutput").ap(), src)

    with ExitStack() as _st1:
        ident_sb = _st1.enter_context(nc.sbuf_tensor("ident_sb", [128, 128], BF16, allow_name_mangling=True))
        ident32_sb = _st1.enter_context(nc.sbuf_tensor("ident32_sb", [128, 128], F32, allow_name_mangling=True))
        tri_sb = _st1.enter_context(nc.sbuf_tensor("tri_sb", [128, 128], BF16, allow_name_mangling=True))
        e4_sb = _st1.enter_context(nc.sbuf_tensor("e4_sb", [4, 64], BF16, allow_name_mangling=True))
        bias_sb = _st1.enter_context(nc.sbuf_tensor("bias_sb", [128, 3 * NH + NC], F32, allow_name_mangling=True))
        wr_sb = _st1.enter_context(nc.sbuf_tensor("wr_sb", [128, NC * 4], BF16, allow_name_mangling=True))
        a_sb = _st1.enter_context(nc.sbuf_tensor("a_sb", [128, NC * 64], BF16, allow_name_mangling=True))
        bmat_sb = _st1.enter_context(nc.sbuf_tensor("bmat_sb", [64, D], BF16, allow_name_mangling=True))
        lnst = _st1.enter_context(nc.sbuf_tensor("lnst", [128, 16], F32, allow_name_mangling=True))
        shift_sb = _st1.enter_context(nc.sbuf_tensor("shift_sb", [128, 1], F32, allow_name_mangling=True))
        h_pe = _st1.enter_context(nc.semaphore("sem_pe"))
        h_dve = _st1.enter_context(nc.semaphore("sem_dve"))
        h_act = _st1.enter_context(nc.semaphore("sem_act"))
        h_sync = _st1.enter_context(nc.semaphore("s_sync"))
        h_gp = _st1.enter_context(nc.semaphore("s_gp"))
        h_actq = _st1.enter_context(nc.semaphore("s_actq"))
        h_cc = _st1.enter_context(nc.semaphore("s_cc"))
        h_bar = _st1.enter_context(nc.semaphore("bar"))
        em = Em(nc)
        for nm, h in (("pe", h_pe), ("dve", h_dve), ("act", h_act),
                      ("s_sync", h_sync), ("s_gp", h_gp), ("s_actq", h_actq),
                      ("s_cc", h_cc), ("bar", h_bar)):
            em.add_sem(nm, h)

        def emit_ln_tile(src_tile, xn_stage, sb, extra_sq_wait=None):
            rs = lnst[:, sb + 0:sb + 1]
            qs = lnst[:, sb + 1:sb + 2]
            nmu = lnst[:, sb + 2:sb + 3]
            vv = lnst[:, sb + 3:sb + 4]
            rstd = lnst[:, sb + 4:sb + 5]
            nmr = lnst[:, sb + 5:sb + 6]
            v_rs = em.op("vector", lambda e: e.reduce_sum(rs, src_tile[:, :], axis=AX.X),
                         inc="dve")
            sq_w = [] if extra_sq_wait is None else [extra_sq_wait]
            v_qs = em.op("scalar", lambda e: e.activation(
                xn_stage[:, :], src_tile[:, :], AF.Square, accum_out=qs),
                wait=sq_w, inc="act")
            v_nmu = em.op("vector", lambda e: e.tensor_scalar(
                nmu, rs, -1.0 / D, None, ALU.mult), wait=[("dve", v_rs)], inc="dve")
            v_vv = em.op("vector", lambda e: e.scalar_tensor_tensor(
                vv, rs, nmu, qs, ALU.mult, ALU.add),
                wait=[("dve", v_nmu), ("act", v_qs)], inc="dve")
            v_vs = em.op("vector", lambda e: e.tensor_scalar(
                vv, vv, 1.0 / D, EPS, ALU.mult, ALU.add),
                wait=[("dve", v_vv)], inc="dve")
            v_ln = em.op("scalar", lambda e: e.activation(vv, vv, AF.Ln),
                wait=[("dve", v_vs)], inc="act")
            v_rst = em.op("scalar", lambda e: e.activation(
                rstd, vv, AF.Exp, scale=-0.5), wait=[("act", v_ln)], inc="act")
            v_nmr = em.op("vector", lambda e: e.tensor_mul(nmr, nmu, rstd),
                          wait=[("act", v_rst), ("dve", v_vv)], inc="dve")
            v_xn = em.op("vector", lambda e: e.tensor_scalar(
                xn_stage[:, :], src_tile[:, :], rstd, nmr, ALU.mult, ALU.add),
                wait=[("dve", v_nmr)], inc="dve")
            return v_xn

        def emit_tile_to_fm(src_bf, dst_fm, tau, pbf, pbf_free, src_wait):
            dst3 = dst_fm[:, :].rearrange("p (c r) -> p c r", c=NC)
            v_ev = None
            for f in range(4):
                bank = pbf[f % 2]
                waits = [src_wait]
                if pbf_free[f % 2] is not None:
                    waits.append(("dve", pbf_free[f % 2]))
                v_tp = None
                for k in range(8):
                    c = 8 * f + k
                    def bld(e, bank=bank, k=k, c=c):
                        return e.transpose(bank[:, 128 * k:128 * k + 128],
                                           src_bf[:, 128 * c:128 * c + 128],
                                           ident_sb[:, :])
                    v_tp = em.op("tensor", bld, wait=(waits if k == 0 else ()),
                                 inc=("pe" if k == 7 else None))
                def ev(e, bank=bank, f=f, tau=tau):
                    return e.tensor_copy(
                        dst3[:, 8 * f:8 * f + 8, 128 * tau:128 * tau + 128],
                        bank[:, :].rearrange("p (c r) -> p c r", c=8))
                v_ev = em.op("vector", ev, wait=[("pe", v_tp)], inc="dve")
                pbf_free[f % 2] = v_ev
            return v_ev

        def emit_residual_tiles(src_fm, base_in_ap, out_writer, pbf, reload_sem_name):
            raise NotImplementedError

        with ExitStack() as _st2:
            x_tm = _st2.enter_context(nc.sbuf_tensor("x_tm", [128, 2 * 4096], F32, allow_name_mangling=True))
            xn_fm = _st2.enter_context(nc.sbuf_tensor("xn_fm", [128, NC * 1024], BF16, allow_name_mangling=True))
            xn_stage = _st2.enter_context(nc.sbuf_tensor("xn_stage", [128, 4096], BF16, allow_name_mangling=True))
            pbf_0 = _st2.enter_context(nc.psum_tensor("pbf0", [128, 1024], BF16, allow_name_mangling=True))
            pbf_1 = _st2.enter_context(nc.psum_tensor("pbf1", [128, 1024], BF16, allow_name_mangling=True))
            pbf = [pbf_0, pbf_1]
            with nc.Block("p1") as block:
                em.op("sync", lambda e: e.dma_start(out=ident_sb[:, :], in_=ident_bf.ap()[:, :]),
                      inc="s_sync", inc_k=16)
                em.op("sync", lambda e: e.dma_start(out=ident32_sb[:, :], in_=ident_f32.ap()[:, :]),
                      inc="s_sync", inc_k=16)
                em.op("sync", lambda e: e.dma_start(out=tri_sb[:, :], in_=tri01.ap()[:, :]),
                      inc="s_sync", inc_k=16)
                em.op("sync", lambda e: e.dma_start(out=e4_sb[:, :], in_=e4c.ap()[:, :]),
                      inc="s_sync", inc_k=16)
                em.op("sync", lambda e: e.dma_start(out=shift_sb[:, :], in_=shiftc.ap()[:, :]),
                      inc="s_sync", inc_k=16)
                em.op("sync", lambda e: e.dma_start(out=bias_sb[:, 0:3 * NH], in_=bqkv[:, :]),
                      inc="s_sync", inc_k=16)
                em.op("sync", lambda e: e.dma_start(out=bias_sb[:, 3 * NH:3 * NH + NC], in_=bo4[:, :]),
                      inc="s_sync", inc_k=16)
                em.op("gpsimd", lambda e: e.dma_start(out=wr_sb[:, :], in_=wr4[:, :]),
                      inc="s_gp", inc_k=16)
                em.op("gpsimd", lambda e: e.dma_start(out=a_sb[:, :], in_=a4[:, :]),
                      inc="s_gp", inc_k=16)
                em.op("gpsimd", lambda e: e.dma_start(out=bmat_sb[:, :], in_=bt_in[:, :]),
                      inc="s_gp", inc_k=16)
                v_consts = em.val("s_sync")
                em.wait("tensor", [("s_sync", v_consts)])

                pbf_free = {0: None, 1: None}
                x_done = [None, None]
                v_tp_prev = None
                v_fm_done = None
                for tau in range(NT):
                    half = tau % 2
                    waits = []
                    if x_done[half] is not None:
                        waits.append(("dve", x_done[half]))
                    v_ld = em.op("sync", lambda e, tau=tau, half=half: e.dma_start(
                        out=x_tm[:, 4096 * half:4096 * half + 4096],
                        in_=x_in[128 * tau:128 * tau + 128, :]),
                        wait=waits, inc="s_sync", inc_k=16)
                    em.wait("vector", [("s_sync", v_ld)])
                    em.wait("scalar", [("s_sync", v_ld)])
                    sq_wait = ("pe", v_tp_prev) if v_tp_prev is not None else None
                    v_xn = emit_ln_tile(x_tm[:, 4096 * half:4096 * half + 4096],
                                        xn_stage, 6 * (tau % 2), sq_wait)
                    x_done[half] = v_xn
                    v_fm_done = emit_tile_to_fm(xn_stage, xn_fm, tau, pbf, pbf_free,
                                                ("dve", v_xn))
                    v_tp_prev = em.val("pe")
                    if tau % 2 == 1:
                        q = tau // 2
                        v_pub = None
                        for ic in range(NC):
                            v_pub = em.op("sync", lambda e, ic=ic, q=q: e.dma_start(
                                out=xn_pub[q][128 * ic:128 * ic + 128, :],
                                in_=xn_fm[:, 1024 * ic + 256 * q:1024 * ic + 256 * q + 256]),
                                wait=([("dve", v_fm_done)] if ic == 0 else ()),
                                inc="s_sync", inc_k=16)
                        em.op("gpsimd", lambda e, q=q: e.collective_compute(
                            "AllGather", ALU.bypass, replica_groups=[list(range(8))],
                            ins=[xn_pub[q].opt()],
                            outs=[xn_gath[q].opt()]),
                            wait=[("s_sync", v_pub)], inc="s_cc", inc_k=1)
                em.drain_and_barrier(block)

        with ExitStack() as _st3:
            wt_sb = _st3.enter_context(nc.sbuf_tensor("wt_sb", [128, 12 * 4096], BF16, allow_name_mangling=True))
            xnb = _st3.enter_context(nc.sbuf_tensor("xnb", [128, 2 * 16 * 1024], BF16, allow_name_mangling=True))
            kv_stage = _st3.enter_context(nc.sbuf_tensor("kv_stage", [128, 2 * 1024], BF16, allow_name_mangling=True))
            pb_0 = _st3.enter_context(nc.psum_tensor("pb0", [128, 512], F32, allow_name_mangling=True))
            pb_1 = _st3.enter_context(nc.psum_tensor("pb1", [128, 512], F32, allow_name_mangling=True))
            pb_2 = _st3.enter_context(nc.psum_tensor("pb2", [128, 512], F32, allow_name_mangling=True))
            pb_3 = _st3.enter_context(nc.psum_tensor("pb3", [128, 512], F32, allow_name_mangling=True))
            pb = [pb_0, pb_1, pb_2, pb_3]
            wsrc = {0: wq4, 1: wk4, 2: wv4}
            wdst = {0: q_scr, 1: k_scr, 2: v_scr}
            with nc.Block("p2") as block:
                v_wt = None
                for w in range(3):
                    for oc in range(NH):
                        slot = 4 * w + oc
                        v_wt = em.op("gpsimd", lambda e, w=w, oc=oc, slot=slot: e.dma_start(
                            out=wt_sb[:, 4096 * slot:4096 * slot + 4096],
                            in_=wsrc[w][oc].rearrange("p c o -> p (c o)")),
                            inc="s_gp", inc_k=16)
                xg3 = [xn_gath[g].rearrange("(b c p) t -> b c p t", b=8, p=128)
                       for g in range(4)]
                xnb_user = [None, None]
                bank_i = 0
                pb_free = [None] * 4
                stage_free = [None, None]
                st_i = 0
                for th in range(2):
                    for rb in range(NRB):
                        bslot = (8 * th + rb) % 2
                        waits = [("s_cc", 2 * th + 2)]
                        if xnb_user[bslot] is not None:
                            waits.append(("pe", xnb_user[bslot]))
                        v_lds = []
                        for qq in range(2):
                            g = 2 * th + qq
                            v = em.op("sync", lambda e, rb=rb, g=g, qq=qq, bslot=bslot: e.dma_start(
                                out=xnb[:, 16384 * bslot:16384 * bslot + 16384].rearrange(
                                    "p (c t) -> p c t", c=NC)[:, :, 256 * qq:256 * qq + 256],
                                in_=xg3[g][rb].rearrange("c p t -> p c t")),
                                wait=(waits if qq == 0 else ()), inc="s_sync", inc_k=16)
                            v_lds.append(v)
                        v_ld = v_lds[-1]
                        for slot in range(12):
                            w, oc = slot // 4, slot % 4
                            bank = bank_i % 4
                            bank_i += 1
                            w0 = [("s_gp", v_wt), ("s_sync", v_ld)]
                            if pb_free[bank] is not None:
                                w0.append(("act", pb_free[bank]))
                            v_mm = None
                            for ic in range(NC):
                                def bld(e, bank=bank, slot=slot, ic=ic, bslot=bslot):
                                    return e.matmul(
                                        pb[bank][:, :],
                                        wt_sb[:, 4096 * slot + 128 * ic:4096 * slot + 128 * ic + 128],
                                        xnb[:, 16384 * bslot + 512 * ic:
                                            16384 * bslot + 512 * ic + 512],
                                        start=(ic == 0), stop=(ic == NC - 1))
                                v_mm = em.op("tensor", bld, wait=(w0 if ic == 0 else ()),
                                             inc=("pe" if ic == NC - 1 else None))
                            xnb_user[bslot] = v_mm
                            sslot = st_i % 2
                            st_i += 1
                            ev_w = [("pe", v_mm)]
                            if stage_free[sslot] is not None:
                                ev_w.append(("s_sync", stage_free[sslot]))
                            def ev(e, bank=bank, sslot=sslot, w=w, oc=oc):
                                return e.activation(
                                    kv_stage[:, 1024 * sslot:1024 * sslot + 512],
                                    pb[bank][:, :], AF.Identity,
                                    bias=bias_sb[:, 4 * w + oc:4 * w + oc + 1], scale=1.0)
                            v_ev = em.op("scalar", ev, wait=ev_w, inc="act")
                            pb_free[bank] = v_ev
                            v_st = em.op("sync", lambda e, w=w, oc=oc, rb=rb, th=th, sslot=sslot: e.dma_start(
                                out=wdst[w][128 * oc:128 * oc + 128,
                                            1024 * rb + 512 * th:1024 * rb + 512 * th + 512],
                                in_=kv_stage[:, 1024 * sslot:1024 * sslot + 512]),
                                wait=[("act", v_ev)], inc="s_sync", inc_k=16)
                            stage_free[sslot] = v_st
                em.drain_and_barrier(block)

        with ExitStack() as _st4:
            qbuf = _st4.enter_context(nc.sbuf_tensor("qbuf", [128, 2 * 4096], BF16, allow_name_mangling=True))
            kbuf = _st4.enter_context(nc.sbuf_tensor("kbuf", [128, 2 * 4096], BF16, allow_name_mangling=True))
            vbuf = _st4.enter_context(nc.sbuf_tensor("vbuf", [128, 2 * 4096], BF16, allow_name_mangling=True))
            pP = _st4.enter_context(nc.sbuf_tensor("pP", [128, 2 * 4 * 4096], BF16, allow_name_mangling=True))
            vtm = _st4.enter_context(nc.sbuf_tensor("vtm", [128, 2 * 4096], BF16, allow_name_mangling=True))
            ptst = _st4.enter_context(nc.sbuf_tensor("ptst", [128, 2 * 512], BF16, allow_name_mangling=True))
            lparts = _st4.enter_context(nc.sbuf_tensor("lparts", [128, 2 * 48], F32, allow_name_mangling=True))
            lsum = _st4.enter_context(nc.sbuf_tensor("lsum", [128, 16], F32, allow_name_mangling=True))
            att_stage = _st4.enter_context(nc.sbuf_tensor("att_stage", [128, 2 * 512], BF16, allow_name_mangling=True))
            pb_0 = _st4.enter_context(nc.psum_tensor("pb0", [128, 512], F32, allow_name_mangling=True))
            pb_1 = _st4.enter_context(nc.psum_tensor("pb1", [128, 512], F32, allow_name_mangling=True))
            pb_2 = _st4.enter_context(nc.psum_tensor("pb2", [128, 512], F32, allow_name_mangling=True))
            pb_3 = _st4.enter_context(nc.psum_tensor("pb3", [128, 512], F32, allow_name_mangling=True))
            pbo_0 = _st4.enter_context(nc.psum_tensor("pbo0", [128, 512], F32, allow_name_mangling=True))
            pbo_1 = _st4.enter_context(nc.psum_tensor("pbo1", [128, 512], F32, allow_name_mangling=True))
            pbf_0 = _st4.enter_context(nc.psum_tensor("pbf0", [128, 1024], BF16, allow_name_mangling=True))
            pbf_1 = _st4.enter_context(nc.psum_tensor("pbf1", [128, 1024], BF16, allow_name_mangling=True))
            pbs = [pb_0, pb_1, pb_2, pb_3]
            pbo = [pbo_0, pbo_1]
            pbf = [pbf_0, pbf_1]
            with nc.Block("p3") as block:
                st = {
                    "sbank_free": [None] * 4, "sbank_i": 0,
                    "pbf_free": [None, None], "pbo_free": [None, None],
                    "ptst_free": [None, None],
                    "pP_user": [None, None],
                    "lp_done": [None, None],
                    "buf_user": [None, None],
                    "stage_free": [None, None],
                    "loads": {},
                }

                def emit_loads(u):
                    hh, s = u // 2, u % 2
                    slot = u % 2
                    lw = []
                    if st["buf_user"][slot] is not None:
                        lw.append(("pe", st["buf_user"][slot]))
                    v_q = em.op("sync", lambda e, hh=hh, s=s, slot=slot: e.dma_start(
                        out=qbuf[:, 4096 * slot:4096 * slot + 4096],
                        in_=q_scr[128 * hh:128 * hh + 128, 4096 * s:4096 * s + 4096]),
                        wait=lw, inc="s_sync", inc_k=16)
                    v_k = em.op("sync", lambda e, hh=hh, s=s, slot=slot: e.dma_start(
                        out=kbuf[:, 4096 * slot:4096 * slot + 4096],
                        in_=k_scr[128 * hh:128 * hh + 128, 4096 * s:4096 * s + 4096]),
                        wait=lw, inc="s_sync", inc_k=16)
                    v_v = em.op("scalar", lambda e, hh=hh, s=s, slot=slot: e.dma_start(
                        out=vbuf[:, 4096 * slot:4096 * slot + 4096],
                        in_=v_scr[128 * hh:128 * hh + 128, 4096 * s:4096 * s + 4096]),
                        wait=lw, inc="s_actq", inc_k=16)
                    st["loads"][u] = (v_q, v_k, v_v)

                def emit_vt(u):
                    slot = u % 2
                    v_v = st["loads"][u][2]
                    v_vt_ev = None
                    vt_evac = {0: None, 1: None}
                    for f in range(4):
                        bank = pbf[f % 2]
                        tw = [("s_actq", v_v)]
                        if st["buf_user"][slot] is not None:
                            tw.append(("pe", st["buf_user"][slot]))
                        if st["pbf_free"][f % 2] is not None:
                            tw.append(("dve", st["pbf_free"][f % 2]))
                        if vt_evac[f % 2] is not None:
                            tw.append(("act", vt_evac[f % 2]))
                        v_tp = None
                        for k in range(8):
                            c = 8 * f + k
                            def bld(e, bank=bank, k=k, c=c, slot=slot):
                                return e.transpose(
                                    bank[:, 128 * k:128 * k + 128],
                                    vbuf[:, 4096 * slot + 128 * c:4096 * slot + 128 * c + 128],
                                    ident_sb[:, :])
                            v_tp = em.op("tensor", bld, wait=(tw if k == 0 else ()),
                                         inc=("pe" if k == 7 else None))
                        def evv(e, bank=bank, f=f, slot=slot):
                            return e.copy(
                                vtm[:, 4096 * slot + 1024 * f:4096 * slot + 1024 * f + 1024],
                                bank[:, :])
                        v_vt_ev = em.op("scalar", evv, wait=[("pe", v_tp)], inc="act")
                        vt_evac[f % 2] = v_vt_ev
                        st["pbf_free"][f % 2] = None
                        st["pbf_vt"] = v_vt_ev
                    return v_vt_ev

                def emit_scores_softmax(ui):
                    u, qg = ui // 8, ui % 8
                    hh, s = u // 2, u % 2
                    slot, par = u % 2, ui % 2
                    sl = qg
                    prefix = 512 * (sl + 1)
                    v_q, v_k, v_v = st["loads"][u]
                    pbase = 16384 * par
                    lbase = 48 * par
                    sbase = 8 * par
                    lw = []
                    if st["lp_done"][par] is not None:
                        lw = [("act", st["lp_done"][par][0]), ("dve", st["lp_done"][par][1])]
                    v_lp0 = em.op("vector", lambda e, lbase=lbase: e.memset(
                        lparts[:, lbase:lbase + 48], 0.0), wait=lw, inc="dve")
                    ppw = [] if st["pP_user"][par] is None else [("pe", st["pP_user"][par])]
                    v_norm_done = [None] * 4
                    for qt in range(4):
                        for c in range(sl + 1):
                            bank = st["sbank_i"] % 4
                            st["sbank_i"] += 1
                            width = 512 if c < sl else 128 * (qt + 1)
                            mm_w = [("s_sync", v_k)]
                            if st["sbank_free"][bank] is not None:
                                mm_w.append(("act", st["sbank_free"][bank]))
                            def bld(e, bank=bank, qt=qt, c=c, width=width, slot=slot, qg=qg):
                                return e.matmul(
                                    pbs[bank][:, 0:width],
                                    qbuf[:, 4096 * slot + 512 * qg + 128 * qt:
                                         4096 * slot + 512 * qg + 128 * qt + 128],
                                    kbuf[:, 4096 * slot + 512 * c:
                                         4096 * slot + 512 * c + width],
                                    start=True, stop=True)
                            v_mm = em.op("tensor", bld, wait=mm_w, inc="pe")
                            pcol = pbase + 4096 * qt + 512 * c
                            if c < sl:
                                def ev(e, bank=bank, pcol=pcol, qt=qt, c=c, lbase=lbase):
                                    return e.activation(
                                        pP[:, pcol:pcol + 512], pbs[bank][:, 0:512],
                                        AF.Exp, scale=EXP_SCALE, bias=shift_sb[:, 0:1],
                                        accum_out=lparts[:, lbase + 12 * qt + c:lbase + 12 * qt + c + 1])
                                em.op("scalar", ev,
                                      wait=[("pe", v_mm), ("dve", v_lp0)] + ppw, inc="act")
                            else:
                                if qt > 0:
                                    def ev1(e, bank=bank, pcol=pcol, qt=qt, c=c, lbase=lbase):
                                        return e.activation(
                                            pP[:, pcol:pcol + 128 * qt],
                                            pbs[bank][:, 0:128 * qt],
                                            AF.Exp, scale=EXP_SCALE, bias=shift_sb[:, 0:1],
                                            accum_out=lparts[:, lbase + 12 * qt + c:lbase + 12 * qt + c + 1])
                                    em.op("scalar", ev1,
                                          wait=[("pe", v_mm), ("dve", v_lp0)] + ppw, inc="act")
                                def ev2(e, bank=bank, pcol=pcol, qt=qt):
                                    return e.activation(
                                        pP[:, pcol + 128 * qt:pcol + 128 * qt + 128],
                                        pbs[bank][:, 128 * qt:128 * qt + 128],
                                        AF.Exp, scale=EXP_SCALE, bias=shift_sb[:, 0:1])
                                v_ev2 = em.op("scalar", ev2,
                                              wait=[("pe", v_mm)] + ppw, inc="act")
                                def msk(e, pcol=pcol, qt=qt):
                                    return e.tensor_mul(
                                        pP[:, pcol + 128 * qt:pcol + 128 * qt + 128],
                                        pP[:, pcol + 128 * qt:pcol + 128 * qt + 128],
                                        tri_sb[:, :])
                                v_mk = em.op("vector", msk,
                                             wait=[("act", v_ev2), ("dve", v_lp0)], inc="dve")
                                def dsum(e, pcol=pcol, qt=qt, c=c, lbase=lbase):
                                    return e.reduce_sum(
                                        lparts[:, lbase + 12 * qt + c + 1:lbase + 12 * qt + c + 2],
                                        pP[:, pcol + 128 * qt:pcol + 128 * qt + 128],
                                        axis=AX.X)
                                em.op("vector", dsum, wait=[("dve", v_mk)], inc="dve")
                            st["sbank_free"][bank] = em.val("act")
                        wq = 512 * sl + 128 * (qt + 1)
                        if wq < prefix:
                            def z(e, qt=qt, wq=wq, prefix=prefix, pbase=pbase):
                                return e.memset(
                                    pP[:, pbase + 4096 * qt + wq:pbase + 4096 * qt + prefix], 0.0)
                            em.op("vector", z, wait=ppw, inc="dve")
                    act_all = em.val("act")
                    dve_all = em.val("dve")
                    for qt in range(4):
                        def lsb(e, qt=qt, sl=sl, lbase=lbase, sbase=sbase):
                            return e.reduce_sum(
                                lsum[:, sbase + qt:sbase + qt + 1],
                                lparts[:, lbase + 12 * qt:lbase + 12 * qt + sl + 2], axis=AX.X)
                        v_ls = em.op("vector", lsb,
                                     wait=[("act", act_all), ("dve", dve_all)], inc="dve")
                        v_rc = em.op("vector", lambda e, qt=qt, sbase=sbase: e.reciprocal(
                            lsum[:, sbase + 4 + qt:sbase + 5 + qt],
                            lsum[:, sbase + qt:sbase + qt + 1]),
                            wait=[("dve", v_ls)], inc="dve")
                        wqv = 512 * sl + 128 * (qt + 1)
                        def nrm(e, qt=qt, wqv=wqv, pbase=pbase, sbase=sbase):
                            return e.tensor_scalar(
                                pP[:, pbase + 4096 * qt:pbase + 4096 * qt + wqv],
                                pP[:, pbase + 4096 * qt:pbase + 4096 * qt + wqv],
                                lsum[:, sbase + 4 + qt:sbase + 5 + qt], None, ALU.mult)
                        v_norm_done[qt] = em.op("vector", nrm, wait=[("dve", v_rc)], inc="dve")
                    st["lp_done"][par] = (act_all, em.val("dve"))
                    return dict(ui=ui, u=u, qg=qg, hh=hh, s=s, slot=slot, par=par,
                                sl=sl, prefix=prefix, ntc=prefix // 128,
                                norm=v_norm_done)

                def emit_ptav(info, v_vt_ev_by_u):
                    ui, u, qg = info["ui"], info["u"], info["qg"]
                    slot, par = info["slot"], info["par"]
                    ntc = info["ntc"]
                    pbase = 16384 * par
                    obank = pbo[par]
                    v_vt = v_vt_ev_by_u[u]
                    v_ev_c = [None] * ntc
                    v_av = None
                    v_last_pt = None

                    def emit_pt(c):
                        nonlocal v_last_pt
                        bank = pbf[c % 2]
                        w0 = [("dve", info["norm"][3])]
                        if st["pbf_free"][c % 2] is not None:
                            w0.append(("dve", st["pbf_free"][c % 2]))
                        if c < 2 and st.get("pbf_vt") is not None:
                            w0.append(("act", st["pbf_vt"]))
                        v_tp = None
                        for qt in range(4):
                            def bld(e, bank=bank, qt=qt, c=c, pbase=pbase):
                                return e.transpose(
                                    bank[:, 128 * qt:128 * qt + 128],
                                    pP[:, pbase + 4096 * qt + 128 * c:pbase + 4096 * qt + 128 * c + 128],
                                    ident_sb[:, :])
                            v_tp = em.op("tensor", bld, wait=(w0 if qt == 0 else ()),
                                         inc=("pe" if qt == 3 else None))
                        v_last_pt = v_tp
                        pslot = c % 2
                        ev_w = [("pe", v_tp)]
                        if st["ptst_free"][pslot] is not None:
                            ev_w.append(("pe", st["ptst_free"][pslot]))
                        def evp(e, bank=bank, pslot=pslot):
                            return e.tensor_copy(
                                ptst[:, 512 * pslot:512 * pslot + 512], bank[:, 0:512])
                        v_ev = em.op("vector", evp, wait=ev_w, inc="dve")
                        st["pbf_free"][c % 2] = v_ev
                        v_ev_c[c] = v_ev

                    def emit_av(c):
                        nonlocal v_av
                        pslot = c % 2
                        av_w = [("dve", v_ev_c[c]), ("act", v_vt)]
                        if c == 0 and st["pbo_free"][par] is not None:
                            av_w.append(("dve", st["pbo_free"][par]))
                        def av(e, c=c, pslot=pslot, ntc=ntc, slot=slot):
                            return e.matmul(
                                obank[:, :],
                                vtm[:, 4096 * slot + 128 * c:4096 * slot + 128 * c + 128],
                                ptst[:, 512 * pslot:512 * pslot + 512],
                                start=(c == 0), stop=(c == ntc - 1))
                        v_av = em.op("tensor", av, wait=av_w, inc="pe")
                        st["ptst_free"][pslot] = v_av

                    for c in range(ntc):
                        emit_pt(c)
                        if c >= 1:
                            emit_av(c - 1)
                    emit_av(ntc - 1)
                    st["pP_user"][par] = v_last_pt

                    ev_w = [("pe", v_av)]
                    if st["stage_free"][par] is not None:
                        ev_w.append(("s_gp", st["stage_free"][par]))
                    def oev(e, obank=obank, sslot=par):
                        return e.tensor_copy(
                            att_stage[:, 512 * sslot:512 * sslot + 512], obank[:, :])
                    v_oev = em.op("vector", oev, wait=ev_w, inc="dve")
                    st["pbo_free"][par] = v_oev
                    r = 4 * info["s"] + qg // 2
                    colo = 512 * (qg % 2)
                    v_att_st = em.op("gpsimd", lambda e, r=r, hh=info["hh"], colo=colo,
                                     sslot=par: e.dma_start(
                        out=att_pub[512 * r + 128 * hh:512 * r + 128 * hh + 128,
                                    colo:colo + 512],
                        in_=att_stage[:, 512 * sslot:512 * sslot + 512]),
                        wait=[("dve", v_oev)], inc="s_gp", inc_k=16)
                    st["stage_free"][par] = v_att_st
                    if qg == 7:
                        st["buf_user"][slot] = v_av

                emit_loads(0)
                emit_loads(1)
                v_vt_by_u = {}
                infos = []
                for ui in range(64):
                    u, qg = ui // 8, ui % 8
                    if qg == 0:
                        v_vt_by_u[u] = emit_vt(u)
                    if qg == 1 and u + 1 < 8 and (u + 1) not in st["loads"]:
                        emit_loads(u + 1)
                    infos.append(emit_scores_softmax(ui))
                    if ui > 0:
                        emit_ptav(infos[ui - 1], v_vt_by_u)
                emit_ptav(infos[63], v_vt_by_u)

                def a2a(e):
                    with allow_ici_all_to_all():
                        return e.collective_compute(
                            "AllToAll", ALU.bypass, replica_groups=[list(range(8))],
                            ins=[att_pub.opt()], outs=[att_own.opt()])
                em.op("gpsimd", a2a, wait=[("s_gp", em.val("s_gp"))],
                      inc="s_cc", inc_k=1)
                em.drain_and_barrier(block)

        with nc.sbuf_tensor("outT", [128, NC * 1024], BF16, allow_name_mangling=True) as outT:
            with ExitStack() as _st5:
                att_full = _st5.enter_context(nc.sbuf_tensor("att_full", [128, NC * 1024], BF16, allow_name_mangling=True))
                wbuf = _st5.enter_context(nc.sbuf_tensor("wbuf", [128, 2 * 4096], BF16, allow_name_mangling=True))
                pb_0 = _st5.enter_context(nc.psum_tensor("pb0", [128, 512], F32, allow_name_mangling=True))
                pb_1 = _st5.enter_context(nc.psum_tensor("pb1", [128, 512], F32, allow_name_mangling=True))
                pb_2 = _st5.enter_context(nc.psum_tensor("pb2", [128, 512], F32, allow_name_mangling=True))
                pb_3 = _st5.enter_context(nc.psum_tensor("pb3", [128, 512], F32, allow_name_mangling=True))
                pb = [pb_0, pb_1, pb_2, pb_3]
                with nc.Block("p4") as block:
                    v_att_ld = None
                    for ic in range(NC):
                        v_att_ld = em.op("sync", lambda e, ic=ic: e.dma_start(
                            out=att_full[:, 1024 * ic:1024 * ic + 1024],
                            in_=att_own[128 * ic:128 * ic + 128, :]),
                            wait=([("s_cc", 5)] if ic == 0 else ()),
                            inc="s_sync", inc_k=16)
                    wb_user = [None, None]
                    pb_free = [None] * 4
                    bank_i = 0
                    for oc in range(NC):
                        wslot = oc % 2
                        ww = []
                        if wb_user[wslot] is not None:
                            ww.append(("pe", wb_user[wslot]))
                        v_w = em.op("gpsimd", lambda e, oc=oc, wslot=wslot: e.dma_start(
                            out=wbuf[:, 4096 * wslot:4096 * wslot + 4096],
                            in_=wo4[oc].rearrange("p c o -> p (c o)")),
                            wait=ww, inc="s_gp", inc_k=16)
                        for hf in range(2):
                            bank = bank_i % 4
                            bank_i += 1
                            mm_w = [("s_gp", v_w), ("s_sync", v_att_ld)]
                            if pb_free[bank] is not None:
                                mm_w.append(("act", pb_free[bank]))
                            v_mm = None
                            for ic in range(NC):
                                def bld(e, bank=bank, wslot=wslot, ic=ic, hf=hf):
                                    return e.matmul(
                                        pb[bank][:, :],
                                        wbuf[:, 4096 * wslot + 128 * ic:4096 * wslot + 128 * ic + 128],
                                        att_full[:, 1024 * ic + 512 * hf:1024 * ic + 512 * hf + 512],
                                        start=(ic == 0), stop=(ic == NC - 1))
                                v_mm = em.op("tensor", bld,
                                             wait=(mm_w if ic == 0 else ()),
                                             inc=("pe" if ic == NC - 1 else None))
                            wb_user[wslot] = v_mm
                            def ev(e, bank=bank, oc=oc, hf=hf):
                                return e.activation(
                                    outT[:, 1024 * oc + 512 * hf:1024 * oc + 512 * hf + 512],
                                    pb[bank][:, :], AF.Identity,
                                    bias=bias_sb[:, 3 * NH + oc:3 * NH + oc + 1],
                                    scale=1.0)
                            v_ev = em.op("scalar", ev, wait=[("pe", v_mm)], inc="act")
                            pb_free[bank] = v_ev
                    em.drain_and_barrier(block)

            with ExitStack() as _st6:
                x_tm = _st6.enter_context(nc.sbuf_tensor("x_tm", [128, 2 * 4096], F32, allow_name_mangling=True))
                x1_t = _st6.enter_context(nc.sbuf_tensor("x1_t", [128, 4096], F32, allow_name_mangling=True))
                xn2_stage = _st6.enter_context(nc.sbuf_tensor("xn2_stage", [128, 4096], BF16, allow_name_mangling=True))
                xn2_fm = _st6.enter_context(nc.sbuf_tensor("xn2_fm", [128, NC * 1024], BF16, allow_name_mangling=True))
                l4 = _st6.enter_context(nc.sbuf_tensor("l4", [128, NT * 4], F32, allow_name_mangling=True))
                gsc = _st6.enter_context(nc.sbuf_tensor("gsc", [128, NT * 8], F32, allow_name_mangling=True))
                g4 = _st6.enter_context(nc.sbuf_tensor("g4", [128, NT * 4], F32, allow_name_mangling=True))
                gt_sb = _st6.enter_context(nc.sbuf_tensor("gt_sb", [4, 1024], BF16, allow_name_mangling=True))
                gb_sb = _st6.enter_context(nc.sbuf_tensor("gb_sb", [64, 1024], BF16, allow_name_mangling=True))
                gmid = _st6.enter_context(nc.sbuf_tensor("gmid", [64, 1024], BF16, allow_name_mangling=True))
                pb_0 = _st6.enter_context(nc.psum_tensor("pb0", [128, 512], F32, allow_name_mangling=True))
                pb_1 = _st6.enter_context(nc.psum_tensor("pb1", [128, 512], F32, allow_name_mangling=True))
                pb_2 = _st6.enter_context(nc.psum_tensor("pb2", [128, 512], F32, allow_name_mangling=True))
                pb_3 = _st6.enter_context(nc.psum_tensor("pb3", [128, 512], F32, allow_name_mangling=True))
                pbm = _st6.enter_context(nc.psum_tensor("pbm", [64, 1024], F32, allow_name_mangling=True))
                pbf_0 = _st6.enter_context(nc.psum_tensor("pbf0", [128, 1024], BF16, allow_name_mangling=True))
                pbf_1 = _st6.enter_context(nc.psum_tensor("pbf1", [128, 1024], BF16, allow_name_mangling=True))
                pb = [pb_0, pb_1, pb_2, pb_3]
                pbf = [pbf_0, pbf_1]
                with nc.Block("p56") as block:
                    pbf_free = {0: None, 1: None}
                    x_done = [None, None]
                    x1_free = None
                    v_xn2_done = None
                    v_tp_prev = None
                    for tau in range(NT):
                        half = tau % 2
                        lw = []
                        if x_done[half] is not None:
                            lw.append(("dve", x_done[half]))
                        v_ld = em.op("sync", lambda e, tau=tau, half=half: e.dma_start(
                            out=x_tm[:, 4096 * half:4096 * half + 4096],
                            in_=x_in[128 * tau:128 * tau + 128, :]),
                            wait=lw, inc="s_sync", inc_k=16)
                        v_add = None
                        for f in range(4):
                            bank = pbf[f % 2]
                            tw = [("act", em.val("act"))]
                            if pbf_free[f % 2] is not None:
                                tw.append(("dve", pbf_free[f % 2]))
                            v_tp = None
                            for k in range(8):
                                oc = 8 * f + k
                                def bld(e, bank=bank, k=k, oc=oc, tau=tau):
                                    return e.transpose(
                                        bank[:, 128 * k:128 * k + 128],
                                        outT[:, 1024 * oc + 128 * tau:1024 * oc + 128 * tau + 128],
                                        ident_sb[:, :])
                                v_tp = em.op("tensor", bld, wait=(tw if k == 0 else ()),
                                             inc=("pe" if k == 7 else None))
                            aw = [("pe", v_tp), ("s_sync", v_ld)]
                            if x1_free is not None and f == 0:
                                aw += [("act", x1_free[0]), ("s_gp", x1_free[1]),
                                       ("dve", x1_free[2])]
                            def addb(e, bank=bank, f=f, half=half):
                                return e.tensor_add(
                                    x1_t[:, 1024 * f:1024 * f + 1024],
                                    x_tm[:, 4096 * half + 1024 * f:4096 * half + 1024 * f + 1024],
                                    bank[:, :])
                            v_add = em.op("vector", addb, wait=aw, inc="dve")
                            pbf_free[f % 2] = v_add
                        x_done[half] = v_add
                        v_st = em.op("gpsimd", lambda e, tau=tau: e.dma_start(
                            out=x1_dram[128 * tau:128 * tau + 128, :], in_=x1_t[:, :]),
                            wait=[("dve", v_add)], inc="s_gp", inc_k=16)
                        em.wait("scalar", [("dve", v_add)])
                        em.wait("vector", [("dve", v_add)])
                        sq_wait = ("pe", v_tp_prev) if v_tp_prev is not None else None
                        v_xn2 = emit_ln_tile(x1_t, xn2_stage, 6 * (tau % 2), sq_wait)
                        x1_free = (em.val("act"), v_st, v_xn2)
                        v_xn2_done = emit_tile_to_fm(xn2_stage, xn2_fm, tau, pbf,
                                                     pbf_free, ("dve", v_xn2))
                        v_tp_prev = em.val("pe")

                    v_gb_evs = []
                    for tau in range(NT):
                        bank = pb[tau % 2]
                        mm_w = [("dve", v_xn2_done)]
                        if tau >= 2:
                            mm_w.append(("dve", v_gb_evs[tau - 2][0]))
                        v_mm = None
                        for ic in range(NC):
                            def bld(e, bank=bank, ic=ic, tau=tau):
                                return e.matmul(
                                    bank[:, 0:4],
                                    xn2_fm[:, 1024 * ic + 128 * tau:1024 * ic + 128 * tau + 128],
                                    wr_sb[:, 4 * ic:4 * ic + 4],
                                    start=(ic == 0), stop=(ic == NC - 1))
                            v_mm = em.op("tensor", bld, wait=(mm_w if ic == 0 else ()),
                                         inc=("pe" if ic == NC - 1 else None))
                        lts = l4[:, 4 * tau:4 * tau + 4]
                        v_l = em.op("vector", lambda e, bank=bank, lts=lts: e.tensor_copy(
                            lts, bank[:, 0:4]), wait=[("pe", v_mm)], inc="dve")
                        m1 = gsc[:, 8 * tau + 0:8 * tau + 1]
                        nm1 = gsc[:, 8 * tau + 1:8 * tau + 2]
                        eq = gsc[:, 8 * tau + 2:8 * tau + 6]
                        m2 = gsc[:, 8 * tau + 6:8 * tau + 7]
                        z = gsc[:, 8 * tau + 7:8 * tau + 8]
                        ets = g4[:, 4 * tau:4 * tau + 4]
                        v_m1 = em.op("vector", lambda e, m1=m1, lts=lts: e.reduce_max(
                            m1, lts, axis=AX.X), wait=[("dve", v_l)], inc="dve")
                        v_nm1 = em.op("vector", lambda e, nm1=nm1, lts=lts: e.reduce_max(
                            nm1, lts, axis=AX.X, negate=True),
                            wait=[("dve", v_l)], inc="dve")
                        v_eq = em.op("vector", lambda e, eq=eq, lts=lts, m1=m1: e.tensor_scalar(
                            eq, lts, m1, None, ALU.is_ge),
                            wait=[("dve", v_m1)], inc="dve")
                        v_lm = em.op("vector", lambda e, eq=eq, lts=lts: e.scalar_tensor_tensor(
                            eq, eq, -1e30, lts, ALU.mult, ALU.add),
                            wait=[("dve", v_eq)], inc="dve")
                        v_m2 = em.op("vector", lambda e, m2=m2, eq=eq: e.reduce_max(
                            m2, eq, axis=AX.X), wait=[("dve", v_lm)], inc="dve")
                        v_e = em.op("scalar", lambda e, ets=ets, lts=lts, nm1=nm1: e.activation(
                            ets, lts, AF.Exp, bias=nm1, scale=1.0),
                            wait=[("dve", v_nm1)], inc="act")
                        v_ge = em.op("vector", lambda e, eq=eq, lts=lts, m2=m2: e.tensor_scalar(
                            eq, lts, m2, None, ALU.is_ge),
                            wait=[("dve", v_m2)], inc="dve")
                        v_g = em.op("vector", lambda e, ets=ets, eq=eq: e.tensor_mul(
                            ets, ets, eq), wait=[("dve", v_ge), ("act", v_e)], inc="dve")
                        v_z = em.op("vector", lambda e, z=z, ets=ets: e.reduce_sum(
                            z, ets, axis=AX.X), wait=[("dve", v_g)], inc="dve")
                        v_rz = em.op("vector", lambda e, m1=m1, z=z: e.reciprocal(m1, z),
                                     wait=[("dve", v_z)], inc="dve")
                        v_gn = em.op("vector", lambda e, ets=ets, m1=m1: e.tensor_scalar(
                            ets, ets, m1, 4.0, ALU.mult, ALU.mult),
                            wait=[("dve", v_rz)], inc="dve")
                        v_gt = em.op("tensor", lambda e, ets=ets: e.transpose(
                            pb[2][0:4, 0:128], ets, ident32_sb[:, :]),
                            wait=[("dve", v_gn)], inc="pe")
                        v_gte = em.op("vector", lambda e, tau=tau: e.tensor_copy(
                            gt_sb[:, 128 * tau:128 * tau + 128], pb[2][0:4, 0:128]),
                            wait=[("pe", v_gt)], inc="dve")
                        v_gb = em.op("tensor", lambda e, tau=tau: e.matmul(
                            pbm[0:64, 512:640], e4_sb[:, :],
                            gt_sb[:, 128 * tau:128 * tau + 128],
                            start=True, stop=True),
                            wait=[("dve", v_gte)], inc="pe")
                        v_gbe = em.op("vector", lambda e, tau=tau: e.tensor_copy(
                            gb_sb[:, 128 * tau:128 * tau + 128], pbm[0:64, 512:640]),
                            wait=[("pe", v_gb)], inc="dve")
                        v_gb_evs.append((v_gbe, v_gb))
                    v_gb_all = em.val("dve")

                    v_mid_ev = []
                    for hf in range(2):
                        v_mm = None
                        for ic in range(NC):
                            w0 = [("dve", v_gb_all)] if ic == 0 else ()
                            def bld(e, ic=ic, hf=hf):
                                return e.matmul(
                                    pbm[0:64, 512 * hf:512 * hf + 512],
                                    a_sb[:, 64 * ic:64 * ic + 64],
                                    xn2_fm[:, 1024 * ic + 512 * hf:1024 * ic + 512 * hf + 512],
                                    start=(ic == 0), stop=(ic == NC - 1))
                            v_mm = em.op("tensor", bld, wait=w0,
                                         inc=("pe" if ic == NC - 1 else None))
                        def gm(e, hf=hf):
                            return e.tensor_mul(gmid[:, 512 * hf:512 * hf + 512],
                                                pbm[0:64, 512 * hf:512 * hf + 512],
                                                gb_sb[:, 512 * hf:512 * hf + 512])
                        v_gm = em.op("vector", gm, wait=[("pe", v_mm)], inc="dve")
                        v_mid_ev.append(v_gm)

                    v_p5_pe = v_tp_prev
                    pb_free2 = [None] * 4
                    bank_i = 0
                    for oc in range(NC):
                        for hf in range(2):
                            bank = bank_i % 4
                            bank_i += 1
                            mw = [("dve", v_mid_ev[hf])]
                            if pb_free2[bank] is not None:
                                mw.append(("act", pb_free2[bank]))
                            def bld(e, bank=bank, oc=oc, hf=hf):
                                return e.matmul(pb[bank][:, :],
                                                bmat_sb[:, 128 * oc:128 * oc + 128],
                                                gmid[:, 512 * hf:512 * hf + 512],
                                                start=True, stop=True)
                            v_mm = em.op("tensor", bld, wait=mw, inc="pe")
                            def ev(e, bank=bank, oc=oc, hf=hf):
                                return e.activation(
                                    outT[:, 1024 * oc + 512 * hf:1024 * oc + 512 * hf + 512],
                                    pb[bank][:, :], AF.Identity, scale=1.0)
                            v_dev = em.op("scalar", ev, wait=[("pe", v_mm)], inc="act")
                            pb_free2[bank] = v_dev
                    em.drain_and_barrier(block)

            with ExitStack() as _st7:
                x1b = _st7.enter_context(nc.sbuf_tensor("x1b", [128, 2 * 4096], F32, allow_name_mangling=True))
                x2_t = _st7.enter_context(nc.sbuf_tensor("x2_t", [128, 2 * 4096], F32, allow_name_mangling=True))
                pbf_0 = _st7.enter_context(nc.psum_tensor("pbf0", [128, 1024], BF16, allow_name_mangling=True))
                pbf_1 = _st7.enter_context(nc.psum_tensor("pbf1", [128, 1024], BF16, allow_name_mangling=True))
                pbf = [pbf_0, pbf_1]
                with nc.Block("p7") as block:
                    pbf_free = {0: None, 1: None}
                    x1_done = [None, None]
                    out_vals = []
                    for tau in range(NT):
                        half = tau % 2
                        lw = []
                        if x1_done[half] is not None:
                            lw.append(("dve", x1_done[half]))
                        v_ld = em.op("sync", lambda e, tau=tau, half=half: e.dma_start(
                            out=x1b[:, 4096 * half:4096 * half + 4096],
                            in_=x1_dram[128 * tau:128 * tau + 128, :]),
                            wait=lw, inc="s_sync", inc_k=16)
                        v_add = None
                        for f in range(4):
                            bank = pbf[f % 2]
                            tw = [("act", em.val("act"))]
                            if pbf_free[f % 2] is not None:
                                tw.append(("dve", pbf_free[f % 2]))
                            v_tp = None
                            for k in range(8):
                                oc = 8 * f + k
                                def bld(e, bank=bank, k=k, oc=oc, tau=tau):
                                    return e.transpose(
                                        bank[:, 128 * k:128 * k + 128],
                                        outT[:, 1024 * oc + 128 * tau:1024 * oc + 128 * tau + 128],
                                        ident_sb[:, :])
                                v_tp = em.op("tensor", bld, wait=(tw if k == 0 else ()),
                                             inc=("pe" if k == 7 else None))
                            aw = [("pe", v_tp), ("s_sync", v_ld)]
                            if len(out_vals) >= 2 and f == 0:
                                aw.append(("s_gp", out_vals[-2]))
                            def addb(e, bank=bank, f=f, half=half):
                                return e.tensor_add(
                                    x2_t[:, 4096 * half + 1024 * f:4096 * half + 1024 * f + 1024],
                                    x1b[:, 4096 * half + 1024 * f:4096 * half + 1024 * f + 1024],
                                    bank[:, :])
                            v_add = em.op("vector", addb, wait=aw, inc="dve")
                            pbf_free[f % 2] = v_add
                        x1_done[half] = v_add
                        v_st = em.op("gpsimd", lambda e, tau=tau, half=half: e.dma_start(
                            out=y_out[128 * tau:128 * tau + 128, :],
                            in_=x2_t[:, 4096 * half:4096 * half + 4096]),
                            wait=[("dve", v_add)], inc="s_gp", inc_k=16)
                        out_vals.append(v_st)
                    for t, (dst, src) in taps.items():
                        em.op("gpsimd", lambda e, dst=dst, src=src: e.dma_start(
                            out=dst[:, :], in_=src[:, :]), inc="s_gp", inc_k=16)
                    em.wait("gpsimd", [("s_gp", em.val("s_gp"))])
                    em.wait("sync", [("s_gp", em.val("s_gp"))])
                    em.flush(block)

    return nc


def _host_prepare(inputs):
    x = np.asarray(inputs["L0_attn_x"], dtype=np.float32).reshape(8192, D)
    wt = {}
    for nm, key in (("wq", "L0_attn_Wq"), ("wk", "L0_attn_Wk"), ("wv", "L0_attn_Wv")):
        W = np.asarray(inputs[key], dtype=np.float32)
        wt[nm] = W.T.reshape(NC, 128, NC, 128)
    Wo = np.asarray(inputs["L0_attn_Wo"], dtype=np.float32)
    wo4 = np.ascontiguousarray(Wo.T.reshape(NC, 128, NC, 128).transpose(2, 1, 0, 3))
    bq = np.asarray(inputs["L0_attn_bq"], dtype=np.float32).reshape(NC, 128)
    bk = np.asarray(inputs["L0_attn_bk"], dtype=np.float32).reshape(NC, 128)
    bv = np.asarray(inputs["L0_attn_bv"], dtype=np.float32).reshape(NC, 128)
    bo = np.asarray(inputs["L0_attn_bo"], dtype=np.float32)
    bo4 = np.ascontiguousarray(bo.reshape(NC, 128).T)
    A = np.asarray(inputs["L0_ffn_A"], dtype=np.float32)
    B = np.asarray(inputs["L0_ffn_B"], dtype=np.float32)
    Wr = np.asarray(inputs["L0_ffn_Wr"], dtype=np.float32)
    a_allt = A.transpose(2, 0, 1).reshape(D, 64)
    a4 = np.ascontiguousarray(
        a_allt.reshape(NC, 128, 64).transpose(1, 0, 2)).reshape(128, NC * 64)
    bt = np.ascontiguousarray(B.transpose(0, 2, 1).reshape(64, D))
    wr4 = np.ascontiguousarray(
        Wr.T.reshape(NC, 128, 4).transpose(1, 0, 2)).reshape(128, NC * 4)
    shared = dict(a4=a4, bt=bt, wr4=wr4, wo4=wo4, bo4=bo4)
    in_maps = []
    for j in range(8):
        m = dict(shared)
        m["x_own"] = np.ascontiguousarray(x[1024 * j:1024 * (j + 1)])
        hc = slice(4 * j, 4 * j + 4)
        for nm in ("wq", "wk", "wv"):
            m[nm + "4"] = np.ascontiguousarray(
                wt[nm][:, :, hc, :].transpose(2, 1, 0, 3))
        m["bqkv"] = np.ascontiguousarray(
            np.concatenate([bq[hc].T, bk[hc].T, bv[hc].T], axis=1))
        in_maps.append(m)
    return in_maps


def kernel(**inputs):
    in_maps = _host_prepare(inputs)
    nc = build_core()
    res = run_bass_kernel_spmd(nc, in_maps, core_ids=list(range(8)))
    out = np.empty((8192, D), dtype=np.float32)
    for j in range(8):
        out[1024 * j:1024 * (j + 1)] = res.results[j]["y"]
    return out.reshape(2, 4096, D)
